# version 01
# baseline (speedup 1.0000x reference)
import math
import contextlib
import numpy as np
import concourse.bass as bass
import concourse.mybir as mybir
from concourse.bass_utils import run_bass_kernel_spmd

F32 = mybir.dt.float32
BF16 = mybir.dt.bfloat16
AF = mybir.ActivationFunctionType
ALU = mybir.AluOpType
AX = mybir.AxisListType

D = 2048
NH = 16
QR = 1536
KVR = 512
DN = 128
DR = 64
HID = 8192
SCALE = 1.0 / math.sqrt(DN + DR)
MSHIFT = -10.0
EPS_LN = 1e-5
EPS_RMS = 1e-6
NEGBIG = -1.0e30
B_MAX_QT = 10**9
B_MAX_CH = 10**9
B_PREFETCH = True
B_H_MAX = 16
B_FIN = True
C_MAX_T = 10**9
A_MAX_T = 10**9


class Emit:
    ENGINES = ("sync", "tensor", "vector", "scalar", "gpsimd")

    def __init__(self, nc, stack):
        self.nc = nc
        self.stack = stack
        self.ops = {e: [] for e in self.ENGINES}
        self.h = {}
        self.n = {}

    def sem(self, name):
        if name not in self.h:
            self.h[name] = self.stack.enter_context(self.nc.semaphore(name))
            self.n[name] = 0
        return self.h[name]

    def op(self, eng, fn, inc=None, k=1):
        if inc is None:
            self.ops[eng].append(fn)
            return None
        hh = self.sem(inc)
        self.n[inc] += k
        self.ops[eng].append(lambda e, f=fn, hh=hh, k=k: f(e).then_inc(hh, k))
        return self.n[inc]

    def wait(self, eng, fence):
        if fence is None:
            return
        name, val = fence
        if val is None or val <= 0:
            return
        hh = self.sem(name)
        self.ops[eng].append(lambda e, hh=hh, v=val: e.wait_ge(hh, v))

    def cnt(self, name):
        self.sem(name)
        return self.n[name]

    def sbar(self, eng, ctr):
        self.wait(eng, (ctr, self.n.get(ctr, 0)))

    def barrier(self, fences):
        for eng in self.ENGINES:
            for other, f in fences.items():
                if other != eng:
                    self.wait(eng, f)


def build_program(S, dbg=False):
    nc = bass.Bass("TRN2", target_bir_lowering=False)
    NT = S // 512
    skind = "ExternalOutput" if dbg else "Internal"

    x_h = nc.dram_tensor("x", [S, D], F32, kind="ExternalInput").ap()
    wqd_h = nc.dram_tensor("wqd", [D, QR], F32, kind="ExternalInput").ap()
    wqu_h = nc.dram_tensor("wqu", [QR, NH * DN], F32, kind="ExternalInput").ap()
    wqr_h = nc.dram_tensor("wqr", [QR, NH * DR], F32, kind="ExternalInput").ap()
    qnw_h = nc.dram_tensor("qnw", [128, QR // 128], F32, kind="ExternalInput").ap()
    wkd_h = nc.dram_tensor("wkd", [D, KVR], F32, kind="ExternalInput").ap()
    kvw_h = nc.dram_tensor("kvw", [128, KVR // 128], F32, kind="ExternalInput").ap()
    wku_h = nc.dram_tensor("wku", [KVR, NH * 256], F32, kind="ExternalInput").ap()
    wkr_h = nc.dram_tensor("wkr", [D, DR], F32, kind="ExternalInput").ap()
    wo_h = nc.dram_tensor("wo", [D, D], F32, kind="ExternalInput").ap()
    wg_h = nc.dram_tensor("wg", [D, HID], F32, kind="ExternalInput").ap()
    wu_h = nc.dram_tensor("wu", [D, HID], F32, kind="ExternalInput").ap()
    wd_h = nc.dram_tensor("wd", [HID, D], F32, kind="ExternalInput").ap()
    id_h = nc.dram_tensor("ident", [128, 128], F32, kind="ExternalInput").ap()
    tm_h = nc.dram_tensor("trimask", [128, 128], F32, kind="ExternalInput").ap()
    rc_h = nc.dram_tensor("rcos", [128, S], F32, kind="ExternalInput").ap()
    rs_h = nc.dram_tensor("rsin", [128, S], F32, kind="ExternalInput").ap()
    y_h = nc.dram_tensor("y", [S, D], F32, kind="ExternalOutput").ap()
    s_h_h = nc.dram_tensor("s_h", [D, S], BF16, kind=skind).ap()
    s_qn_h = nc.dram_tensor("s_qn", [NH * DN, S], BF16, kind=skind).ap()
    s_qp_h = nc.dram_tensor("s_qp", [NH * DR, S], BF16, kind=skind).ap()
    s_kn_h = nc.dram_tensor("s_kn", [NH * DN, S], BF16, kind=skind).ap()
    s_v_h = nc.dram_tensor("s_v", [S, NH * DN], BF16, kind=skind).ap()
    s_at_h = nc.dram_tensor("s_at", [D, S], BF16, kind=skind).ap()
    if dbg:
        d_wo_h = nc.dram_tensor("d_wo", [S, D], F32, kind="ExternalOutput").ap()
        d_x2_h = nc.dram_tensor("d_x2", [S, D], F32, kind="ExternalOutput").ap()
        d_z_h = nc.dram_tensor("d_z", [S, D], F32, kind="ExternalOutput").ap()
    wo_b_h = nc.dram_tensor("wo_b", [D, D], BF16, kind="Internal").ap()
    wg_b_h = nc.dram_tensor("wg_b", [D, HID], BF16, kind="Internal").ap()
    wu_b_h = nc.dram_tensor("wu_b", [D, HID], BF16, kind="Internal").ap()
    wd_b_h = nc.dram_tensor("wd_b", [HID, D], BF16, kind="Internal").ap()

    r128 = lambda ap_: ap_.rearrange("(c p) t -> p c t", p=128)
    r64 = lambda ap_: ap_.rearrange("(c p) t -> p c t", p=64)

    stack = contextlib.ExitStack()
    with stack:
        E = Emit(nc, stack)
        sb = lambda name, shape, dt: stack.enter_context(nc.sbuf_tensor(name, shape, dt))
        st = {}

        id_sb = sb("id_sb", [128, 128], BF16)
        tm_sb = sb("tm_sb", [128, 128], F32)
        krope = sb("krope", [64, S], BF16)
        epsln = sb("epsln", [128, 1], F32)
        epsrm = sb("epsrm", [128, 1], F32)
        mshift = sb("mshift", [128, 1], F32)
        junk = sb("junk", [128, 2048], BF16)

        E.op("gpsimd", lambda e: e.dma_start(out=id_sb[:, :], in_=id_h[:, :]), inc="LW0", k=16)
        E.op("gpsimd", lambda e: e.dma_start(out=tm_sb[:, :], in_=tm_h[:, :]), inc="LW0", k=16)
        E.op("gpsimd", lambda e: e.memset(epsln[:, :], EPS_LN), inc="GE")
        E.op("gpsimd", lambda e: e.memset(epsrm[:, :], EPS_RMS), inc="GE")
        E.op("gpsimd", lambda e: e.memset(mshift[:, :], MSHIFT), inc="GE")
        P0_LW = ("LW0", E.cnt("LW0"))
        P0_GE = ("GE", E.cnt("GE"))

        def ln_chain(src, out, width, eps_tile, s4, tag, src_f=None, rms=False,
                     ss_ready=None, nss=1):
            sums, ss, nmu, rst = s4
            prev = st.get(tag)
            if src_f is not None:
                E.wait("vector", src_f)
                E.wait("scalar", src_f)
            if prev is not None:
                E.wait("vector", ("AE", prev[1]))
                E.wait("scalar", ("VE", prev[0]))
            if not rms:
                E.op("vector", lambda e: e.reduce_sum(sums, src, axis=AX.X), inc="VE")
            if ss_ready is None:
                E.sbar("scalar", "AE")
                a_sq = E.op("scalar", lambda e: e.activation(junk[:, 0:width], src, AF.Square,
                                                             accum_out=ss[:, 0:1]), inc="AE")
            else:
                a_sq = ss_ready[1]
                E.wait("scalar", ss_ready)
            E.wait("vector", ("AE", a_sq))
            if nss > 1:
                E.op("vector", lambda e: e.reduce_sum(sums if rms else rst, ss[:, 0:nss],
                                                      axis=AX.X), inc="VE")
            if rms:
                srcss = sums if nss > 1 else ss
                E.sbar("vector", "VE")
                E.op("vector", lambda e: e.tensor_scalar_mul(rst, srcss[:, 0:1], 1.0 / width),
                     inc="VE")
            else:
                E.sbar("vector", "VE")
                E.op("vector", lambda e: e.tensor_scalar_mul(nmu, sums, -1.0 / width), inc="VE")
                if nss > 1:
                    E.op("vector", lambda e: e.tensor_scalar_mul(rst, rst, 1.0 / width),
                         inc="VE")
                else:
                    E.op("vector", lambda e: e.tensor_scalar_mul(rst, ss[:, 0:1], 1.0 / width),
                         inc="VE")
                E.sbar("vector", "VE")
                E.op("vector", lambda e: e.tensor_tensor(sums, nmu, nmu, ALU.mult), inc="VE")
                E.sbar("vector", "VE")
                E.op("vector", lambda e: e.tensor_tensor(rst, rst, sums, ALU.subtract), inc="VE")
            v3 = E.cnt("VE")
            E.wait("scalar", ("VE", v3))
            a_rt = E.op("scalar", lambda e: e.activation(rst, rst, AF.Sqrt,
                                                         bias=eps_tile[:, 0:1]), inc="AE")
            E.wait("vector", ("AE", a_rt))
            E.op("vector", lambda e: e.reciprocal(rst, rst), inc="VE")
            E.sbar("vector", "VE")
            if rms:
                vap = E.op("vector", lambda e: e.tensor_scalar_mul(out, src, rst[:, 0:1]),
                           inc="VE")
            else:
                vap = E.op("vector", lambda e: e.tensor_scalar(out, src, nmu[:, 0:1], rst[:, 0:1],
                                                               ALU.add, ALU.mult), inc="VE")
            st[tag] = (vap, a_rt)
            return ("VE", vap)

        with contextlib.ExitStack() as pst:
            psb = lambda name, shape, dt: pst.enter_context(nc.sbuf_tensor(name, shape, dt))
            pps = lambda name, shape, dt: pst.enter_context(nc.psum_tensor(name, shape, dt))
            wqd_sb = psb("wqd_sb", [128, 16, QR], BF16)
            wqu_s = [psb(f"wqu_s{i}", [128, 12, 512], BF16) for i in range(2)]
            wqr_sb = psb("wqr_sb", [128, 12, NH * DR], BF16)
            qnw_sb = psb("qnw_sb", [128, 12], F32)
            x_sb = [psb(f"x_sb{i}", [128, D], F32) for i in range(2)]
            h_sb = [psb(f"h_sb{i}", [128, D], BF16) for i in range(2)]
            hT_sb = psb("hT_sb", [128, 16, 512], BF16)
            qc_sb = psb("qc_sb", [128, QR], F32)
            qcn_sb = psb("qcn_sb", [128, QR], BF16)
            qcT_sb = psb("qcT_sb", [128, 12, 512], BF16)
            qn_st = [psb(f"qn_st{i}", [128, 4, 512], BF16) for i in range(2)]
            qpe_sb = psb("qpe_sb", [128, 8, 512], BF16)
            ro_e = psb("ro_e", [128, 512], BF16)
            ro_o = psb("ro_o", [128, 512], BF16)
            rt1 = psb("rt1", [128, 512], BF16)
            rt2 = psb("rt2", [128, 512], BF16)
            rc_sb = psb("rc_sb", [128, 512], F32)
            rs_sb = psb("rs_sb", [128, 512], F32)
            sums = psb("sums", [128, 1], F32)
            ss = psb("ss", [128, 3], F32)
            nmu = psb("nmu", [128, 1], F32)
            rst = psb("rst", [128, 1], F32)
            s4 = (sums[:, 0:1], ss, nmu[:, 0:1], rst[:, 0:1])
            pA = [pps(f"pA{i}", [128, 512], F32) for i in range(2)]
            pT = [pps(f"pT{i}", [128, 2048], BF16) for i in range(2)]

            E.op("gpsimd", lambda e: e.dma_start(out=wqd_sb[:, :, :], in_=r128(wqd_h)),
                 inc="LW1", k=16)
            E.op("gpsimd", lambda e: e.dma_start(out=wqr_sb[:, :, :], in_=r128(wqr_h)),
                 inc="LW1", k=16)
            E.op("gpsimd", lambda e: e.dma_start(out=qnw_sb[:, :], in_=qnw_h[:, :]),
                 inc="LW1", k=16)
            wA1 = ("LW1", E.cnt("LW1"))
            E.wait("vector", wA1)
            E.wait("tensor", wA1)
            E.wait("tensor", P0_LW)
            E.wait("vector", P0_LW)

            def load_wqu(t, g):
                sl = (t * 4 + g) % 2
                E.wait("gpsimd", st.get(f"wqu{sl}"))
                n = E.op("gpsimd", lambda e, g=g, sl=sl: e.dma_start(
                    out=wqu_s[sl][:, :, :],
                    in_=r128(wqu_h)[:, :, g * 512:g * 512 + 512]), inc=f"WQ{sl}", k=16)
                return sl, (f"WQ{sl}", n)

            for t in range(NT):
                t0 = t * 512
                E.wait("sync", st.get("a1rope"))
                rs_n = None
                E.op("sync", lambda e, t0=t0: e.dma_start(out=rc_sb[:, :],
                     in_=rc_h[:, t0:t0 + 512]), inc="RT1", k=16)
                rs_n = E.op("sync", lambda e, t0=t0: e.dma_start(out=rs_sb[:, :],
                     in_=rs_h[:, t0:t0 + 512]), inc="RT1", k=16)
                for qs in range(4):
                    xi = qs % 2
                    E.wait("sync", st.get(f"a1x{xi}"))
                    xn = E.op("sync", lambda e, t0=t0, qs=qs, xi=xi: e.dma_start(
                        out=x_sb[xi][:, :], in_=x_h[t0 + qs * 128:t0 + qs * 128 + 128, :]),
                        inc=f"XA{xi}", k=16)
                    E.wait("vector", st.get(f"a1h{xi}"))
                    apf = ln_chain(x_sb[xi][:, :], h_sb[xi][:, :], D, epsln, s4, "sA1",
                                   src_f=(f"XA{xi}", xn))
                    st[f"a1x{xi}"] = apf
                    E.wait("tensor", apf)
                    E.wait("tensor", st.get(f"a1pt{qs % 2}"))
                    for c in range(16):
                        E.op("tensor", lambda e, c=c, qs=qs, xi=xi: e.transpose(
                            pT[qs % 2][:, c * 128:(c + 1) * 128],
                            h_sb[xi][:, c * 128:(c + 1) * 128], id_sb[:, :]),
                            inc=("TE" if c == 15 else None))
                    tn = E.cnt("TE")
                    st[f"a1h{xi}"] = ("TE", tn)
                    E.wait("vector", ("TE", tn))
                    E.wait("vector", st.get(f"a1hT{qs}"))
                    E.wait("vector", st.get(f"a1hTs{qs}"))
                    cp = E.op("vector", lambda e, qs=qs: e.tensor_copy(
                        hT_sb[:, :, qs * 128:qs * 128 + 128],
                        pT[qs % 2][:, :]), inc="VE")
                    st[f"a1pt{qs % 2}"] = ("VE", cp)
                    E.wait("sync", ("VE", cp))
                    shn = E.op("sync", lambda e, t0=t0, qs=qs: e.dma_start(
                        out=r128(s_h_h)[:, :, t0 + qs * 128:t0 + qs * 128 + 128],
                        in_=hT_sb[:, :, qs * 128:qs * 128 + 128]), inc=f"SH{qs}", k=16)
                    st[f"a1hTs{qs}"] = (f"SH{qs}", shn)

                    for fo in range(3):
                        pi = (qs * 3 + fo) % 2
                        E.wait("tensor", st.get(f"a1pa{pi}"))
                        E.wait("tensor", ("VE", cp))
                        for kc in range(16):
                            E.op("tensor", lambda e, kc=kc, fo=fo, qs=qs, pi=pi: e.matmul(
                                pA[pi][:, :], hT_sb[:, kc, qs * 128:qs * 128 + 128],
                                wqd_sb[:, kc, fo * 512:fo * 512 + 512],
                                start=(kc == 0), stop=(kc == 15)),
                                inc=("TE" if kc == 15 else None))
                        mmn = E.cnt("TE")
                        E.wait("vector", ("TE", mmn))
                        if fo == 0:
                            E.wait("vector", st.get("a1qcr"))
                        ev = E.op("vector", lambda e, fo=fo, pi=pi: e.tensor_copy(
                            qc_sb[:, fo * 512:fo * 512 + 512], pA[pi][:, :]), inc="VE")
                        st[f"a1pa{pi}"] = ("VE", ev)
                        E.wait("scalar", ("VE", ev))
                        E.wait("scalar", st.get("a1ssr"))
                        E.sbar("scalar", "AE")
                        aq = E.op("scalar", lambda e, fo=fo: e.activation(
                            junk[:, 0:512], qc_sb[:, fo * 512:fo * 512 + 512], AF.Square,
                            accum_out=ss[:, fo:fo + 1]), inc="AE")
                    E.wait("vector", st.get("a1qcn"))
                    apf = ln_chain(qc_sb[:, :], qcn_sb[:, :], QR, epsrm, s4, "sA1",
                                   rms=True, ss_ready=("AE", aq), nss=3)
                    st["a1qcr"] = ("AE", aq)
                    st["a1ssr"] = ("VE", E.cnt("VE"))
                    E.wait("tensor", apf)
                    E.wait("tensor", st.get(f"a1pt{qs % 2}"))
                    for c in range(12):
                        E.op("tensor", lambda e, c=c, qs=qs: e.transpose(
                            pT[qs % 2][:, c * 128:(c + 1) * 128],
                            qcn_sb[:, c * 128:(c + 1) * 128], id_sb[:, :]),
                            inc=("TE" if c == 11 else None))
                    tn = E.cnt("TE")
                    st["a1qcn"] = ("TE", tn)
                    E.wait("vector", ("TE", tn))
                    E.wait("vector", st.get("a1qcT"))
                    for kc in range(12):
                        cp2 = E.op("vector", lambda e, qs=qs, kc=kc: e.tensor_scalar_mul(
                            qcT_sb[:, kc, qs * 128:qs * 128 + 128],
                            pT[qs % 2][:, kc * 128:kc * 128 + 128],
                            qnw_sb[:, kc:kc + 1]), inc="VE")
                    st[f"a1pt{qs % 2}"] = ("VE", cp2)
                    st[f"a1hT{qs}"] = ("TE", E.cnt("TE"))
                qcT_rdy = ("VE", E.cnt("VE"))
                E.wait("tensor", qcT_rdy)
                wq_slots = {}
                wq_slots[0] = load_wqu(t, 0)
                for oc in range(16):
                    g = oc // 4
                    if oc % 4 == 0 and g + 1 < 4:
                        wq_slots[g + 1] = load_wqu(t, g + 1)
                    sl_w, af_w = wq_slots[g]
                    pi = oc % 2
                    E.wait("tensor", st.get(f"a1pa{pi}"))
                    E.wait("tensor", af_w)
                    for kc in range(12):
                        E.op("tensor", lambda e, kc=kc, oc=oc, pi=pi, sl_w=sl_w: e.matmul(
                            pA[pi][:, :], wqu_s[sl_w][:, kc, (oc % 4) * 128:(oc % 4) * 128 + 128],
                            qcT_sb[:, kc, :], start=(kc == 0), stop=(kc == 11)),
                            inc=("TE" if kc == 11 else None))
                    mmn = E.cnt("TE")
                    if oc % 4 == 3:
                        st[f"wqu{sl_w}"] = ("TE", mmn)
                    E.wait("vector", ("TE", mmn))
                    sl, bi = oc % 4, (oc // 4) % 2
                    if sl == 0:
                        E.wait("vector", st.get(f"a1qn{bi}"))
                    ev = E.op("vector", lambda e, sl=sl, bi=bi, pi=pi: e.tensor_copy(
                        qn_st[bi][:, sl, :], pA[pi][:, :]), inc="VE")
                    st[f"a1pa{pi}"] = ("VE", ev)
                    if sl == 3:
                        E.wait("sync", ("VE", ev))
                        stn = E.op("sync", lambda e, t0=t0, oc=oc, bi=bi: e.dma_start(
                            out=r128(s_qn_h)[:, oc - 3:oc + 1, t0:t0 + 512],
                            in_=qn_st[bi][:, :, :]), inc=f"SA{bi}", k=16)
                        st[f"a1qn{bi}"] = (f"SA{bi}", stn)
                E.wait("vector", st.get("a1ropeV"))
                for oc in range(8):
                    pi = oc % 2
                    E.wait("tensor", st.get(f"a1pa{pi}"))
                    for kc in range(12):
                        E.op("tensor", lambda e, kc=kc, oc=oc, pi=pi: e.matmul(
                            pA[pi][:, :], wqr_sb[:, kc, oc * 128:oc * 128 + 128],
                            qcT_sb[:, kc, :], start=(kc == 0), stop=(kc == 11)),
                            inc=("TE" if kc == 11 else None))
                    mmn = E.cnt("TE")
                    E.wait("vector", ("TE", mmn))
                    ev = E.op("vector", lambda e, oc=oc, pi=pi: e.tensor_copy(
                        qpe_sb[:, oc, :], pA[pi][:, :]), inc="VE")
                    st[f"a1pa{pi}"] = ("VE", ev)
                st["a1qcT"] = ("TE", E.cnt("TE"))
                E.wait("vector", ("RT1", rs_n))
                qp_g = s_qp_h.rearrange("(c q r) t -> c q r t", c=4, q=4, r=64)
                for c in range(4):
                    E.wait("vector", st.get("a1roS"))
                    E.op("vector", lambda e, c=c: e.tensor_tensor(rt1[:, :], qpe_sb[:, c, :],
                         rc_sb[:, :], ALU.mult), inc="VE")
                    E.op("vector", lambda e, c=c: e.tensor_tensor(rt2[:, :], qpe_sb[:, c + 4, :],
                         rs_sb[:, :], ALU.mult), inc="VE")
                    E.sbar("vector", "VE")
                    E.op("vector", lambda e: e.tensor_tensor(ro_e[:, :], rt1[:, :], rt2[:, :],
                         ALU.subtract), inc="VE")
                    E.sbar("vector", "VE")
                    E.op("vector", lambda e, c=c: e.tensor_tensor(rt1[:, :], qpe_sb[:, c, :],
                         rs_sb[:, :], ALU.mult), inc="VE")
                    E.op("vector", lambda e, c=c: e.tensor_tensor(rt2[:, :], qpe_sb[:, c + 4, :],
                         rc_sb[:, :], ALU.mult), inc="VE")
                    E.sbar("vector", "VE")
                    rov = E.op("vector", lambda e: e.tensor_tensor(ro_o[:, :], rt1[:, :], rt2[:, :],
                               ALU.add), inc="VE")
                    E.wait("sync", ("VE", rov))
                    E.op("sync", lambda e, c=c, t0=t0: e.dma_start(
                        out=qp_g[c, :, 0:32, t0:t0 + 512], in_=ro_e[:, :]), inc="RO", k=16)
                    ron = E.op("sync", lambda e, c=c, t0=t0: e.dma_start(
                        out=qp_g[c, :, 32:64, t0:t0 + 512], in_=ro_o[:, :]), inc="RO", k=16)
                    st["a1roS"] = ("RO", ron)
                st["a1ropeV"] = ("VE", E.cnt("VE"))
                st["a1rope"] = ("VE", E.cnt("VE"))
            A1_f = {"sync": ("RO", E.cnt("RO")), "tensor": ("TE", E.cnt("TE")),
                    "vector": ("VE", E.cnt("VE")), "scalar": ("AE", E.cnt("AE")),
                    "gpsimd": ("LW1", E.cnt("LW1"))}
            A1_extra = [(f"SH{q}", E.cnt(f"SH{q}")) for q in range(4)] + \
                       [("SA0", E.cnt("SA0")), ("SA1", E.cnt("SA1"))]
        E.barrier(A1_f)
        for f in A1_extra:
            for eng in Emit.ENGINES:
                E.wait(eng, f)

        with contextlib.ExitStack() as pst:
            psb = lambda name, shape, dt: pst.enter_context(nc.sbuf_tensor(name, shape, dt))
            pps = lambda name, shape, dt: pst.enter_context(nc.psum_tensor(name, shape, dt))
            wkd_sb = psb("wkd_sb", [128, 16, KVR], BF16)
            wku_sb = psb("wku_sb", [128, 4, NH * 256], BF16)
            wkr_sb = psb("wkr_sb", [128, 16, DR], BF16)
            kvw_sb = psb("kvw_sb", [128, 4], F32)
            hT2 = [psb(f"hT2_{i}", [128, 16, 512], BF16) for i in range(2)]
            kvc_sb = psb("kvc_sb", [128, KVR], F32)
            kvn_sb = psb("kvn_sb", [128, KVR], BF16)
            kvT_sb = psb("kvT_sb", [128, 4, 512], BF16)
            kn_st = [psb(f"kn_st{i}", [128, 4, 512], BF16) for i in range(2)]
            v_bf = psb("v_bf", [128, 512], BF16)
            v_st = psb("v_st", [128, 4, NH * DN], BF16)
            rt1b = psb("rt1b", [32, 512], F32)
            rt2b = psb("rt2b", [32, 512], F32)
            rc2 = psb("rc2", [128, 512], F32)
            rs2 = psb("rs2", [128, 512], F32)
            sums2 = psb("sums2", [128, 1], F32)
            ss2 = psb("ss2", [128, 1], F32)
            nmu2 = psb("nmu2", [128, 1], F32)
            rst2 = psb("rst2", [128, 1], F32)
            s42 = (sums2[:, 0:1], ss2, nmu2[:, 0:1], rst2[:, 0:1])
            pA2 = [pps(f"pA2_{i}", [128, 512], F32) for i in range(2)]
            pT2 = [pps(f"pT2_{i}", [128, 2048], BF16) for i in range(2)]
            pV2 = pps("pV2", [128, 512], BF16)
            pK2 = pps("pK2", [128, 512], F32)

            E.op("gpsimd", lambda e: e.dma_start(out=wkd_sb[:, :, :], in_=r128(wkd_h)),
                 inc="LW2", k=16)
            E.op("gpsimd", lambda e: e.dma_start(out=wku_sb[:, :, :], in_=r128(wku_h)),
                 inc="LW2", k=16)
            E.op("gpsimd", lambda e: e.dma_start(out=wkr_sb[:, :, :], in_=r128(wkr_h)),
                 inc="LW2", k=16)
            E.op("gpsimd", lambda e: e.dma_start(out=kvw_sb[:, :], in_=kvw_h[:, :]),
                 inc="LW2", k=16)
            wA2 = ("LW2", E.cnt("LW2"))
            E.wait("vector", wA2)
            for kc in range(4):
                E.op("vector", lambda e, kc=kc: e.tensor_scalar_mul(
                    wku_sb[:, kc, :], wku_sb[:, kc, :], kvw_sb[:, kc:kc + 1]), inc="VE")
            foldkv = ("VE", E.cnt("VE"))
            E.wait("tensor", foldkv)

            for t in range(NT):
                t0 = t * 512
                ti = t % 2
                E.wait("sync", st.get("a2rope"))
                rs_n = None
                E.op("sync", lambda e, t0=t0: e.dma_start(out=rc2[:, :],
                     in_=rc_h[:, t0:t0 + 512]), inc="RT2", k=16)
                rs_n = E.op("sync", lambda e, t0=t0: e.dma_start(out=rs2[:, :],
                     in_=rs_h[:, t0:t0 + 512]), inc="RT2", k=16)
                E.wait("sync", st.get(f"a2h{ti}"))
                hn = E.op("sync", lambda e, t0=t0, ti=ti: e.dma_start(
                    out=hT2[ti][:, :, :], in_=r128(s_h_h)[:, :, t0:t0 + 512]),
                    inc=f"XB{ti}", k=16)
                for qs in range(4):
                    pi = qs % 2
                    E.wait("tensor", st.get(f"a2pa{pi}"))
                    E.wait("tensor", (f"XB{ti}", hn))
                    for kc in range(16):
                        E.op("tensor", lambda e, kc=kc, qs=qs, pi=pi, ti=ti: e.matmul(
                            pA2[pi][:, :], hT2[ti][:, kc, qs * 128:qs * 128 + 128],
                            wkd_sb[:, kc, :], start=(kc == 0), stop=(kc == 15)),
                            inc=("TE" if kc == 15 else None))
                    mmn = E.cnt("TE")
                    E.wait("vector", ("TE", mmn))
                    E.wait("vector", st.get("a2kvr"))
                    ev = E.op("vector", lambda e, pi=pi: e.tensor_copy(kvc_sb[:, :],
                              pA2[pi][:, :]), inc="VE")
                    st[f"a2pa{pi}"] = ("VE", ev)
                    E.wait("vector", st.get("a2kvn"))
                    apf = ln_chain(kvc_sb[:, :], kvn_sb[:, :], KVR, epsrm, s42, "sA2",
                                   src_f=("VE", ev), rms=True)
                    st["a2kvr"] = ("AE", E.cnt("AE"))
                    E.wait("tensor", apf)
                    E.wait("tensor", st.get(f"a2pt{pi}"))
                    for c in range(4):
                        E.op("tensor", lambda e, c=c, pi=pi: e.transpose(
                            pT2[pi][:, c * 128:(c + 1) * 128],
                            kvn_sb[:, c * 128:(c + 1) * 128], id_sb[:, :]),
                            inc=("TE" if c == 3 else None))
                    tn = E.cnt("TE")
                    st["a2kvn"] = ("TE", tn)
                    E.wait("vector", ("TE", tn))
                    E.wait("vector", st.get("a2kvT"))
                    cp = E.op("vector", lambda e, qs=qs, pi=pi: e.tensor_copy(
                        kvT_sb[:, :, qs * 128:qs * 128 + 128],
                        pT2[pi][:, 0:512]), inc="VE")
                    st[f"a2pt{pi}"] = ("VE", cp)
                kvT_rdy = ("VE", E.cnt("VE"))
                E.wait("tensor", kvT_rdy)
                for oc in range(32):
                    pi = oc % 2
                    E.wait("tensor", st.get(f"a2pa{pi}"))
                    for kc in range(4):
                        E.op("tensor", lambda e, kc=kc, oc=oc, pi=pi: e.matmul(
                            pA2[pi][:, :], wku_sb[:, kc, oc * 128:oc * 128 + 128],
                            kvT_sb[:, kc, :], start=(kc == 0), stop=(kc == 3)),
                            inc=("TE" if kc == 3 else None))
                    mmn = E.cnt("TE")
                    h_ = oc // 2
                    E.wait("vector", ("TE", mmn))
                    if oc % 2 == 0:
                        sl, bi = h_ % 4, (h_ // 4) % 2
                        if sl == 0:
                            E.wait("vector", st.get(f"a2kn{bi}"))
                        ev = E.op("vector", lambda e, sl=sl, bi=bi, pi=pi: e.tensor_copy(
                            kn_st[bi][:, sl, :], pA2[pi][:, :]), inc="VE")
                        st[f"a2pa{pi}"] = ("VE", ev)
                        if sl == 3:
                            E.wait("sync", ("VE", ev))
                            stn = E.op("sync", lambda e, t0=t0, h_=h_, bi=bi: e.dma_start(
                                out=r128(s_kn_h)[:, h_ - 3:h_ + 1, t0:t0 + 512],
                                in_=kn_st[bi][:, :, :]), inc=f"SK{bi}", k=16)
                            st[f"a2kn{bi}"] = (f"SK{bi}", stn)
                    else:
                        E.wait("vector", st.get("a2vbf"))
                        ev = E.op("vector", lambda e, pi=pi: e.tensor_copy(
                            v_bf[:, :], pA2[pi][:, :]), inc="VE")
                        st[f"a2pa{pi}"] = ("VE", ev)
                        E.wait("tensor", ("VE", ev))
                        E.wait("tensor", st.get("a2pv"))
                        for ts_ in range(4):
                            E.op("tensor", lambda e, ts_=ts_: e.transpose(
                                pV2[:, ts_ * 128:ts_ * 128 + 128],
                                v_bf[:, ts_ * 128:ts_ * 128 + 128], id_sb[:, :]),
                                inc=("TE" if ts_ == 3 else None))
                        tn = E.cnt("TE")
                        st["a2vbf"] = ("TE", tn)
                        E.wait("vector", ("TE", tn))
                        if h_ == 0:
                            E.wait("vector", st.get("a2vst"))
                        cpv = E.op("vector", lambda e, h_=h_: e.tensor_copy(
                            v_st[:, :, h_ * 128:h_ * 128 + 128],
                            pV2[:, :]), inc="VE")
                        st["a2pv"] = ("VE", cpv)
                        if h_ == 15:
                            E.wait("sync", ("VE", cpv))
                            stn = E.op("sync", lambda e, t0=t0: e.dma_start(
                                out=s_v_h.rearrange("(s p) d -> p s d", p=128)[:, t0 // 128:t0 // 128 + 4, :],
                                in_=v_st[:, :, :]), inc="SV", k=16)
                            st["a2vst"] = ("SV", stn)
                st["a2kvT"] = ("TE", E.cnt("TE"))
                E.wait("tensor", st.get("a2krP"))
                for kc in range(16):
                    E.op("tensor", lambda e, kc=kc, ti=ti: e.matmul(
                        pK2[0:64, :], wkr_sb[:, kc, :], hT2[ti][:, kc, :],
                        start=(kc == 0), stop=(kc == 15)), inc=("TE" if kc == 15 else None))
                mmn = E.cnt("TE")
                st[f"a2h{ti}"] = ("TE", mmn)
                E.wait("vector", ("TE", mmn))
                E.wait("vector", ("RT2", rs_n))
                E.op("vector", lambda e: e.tensor_tensor(rt1b[:, :], pK2[0:32, :], rc2[0:32, :],
                     ALU.mult), inc="VE")
                E.op("vector", lambda e: e.tensor_tensor(rt2b[:, :], pK2[32:64, :], rs2[0:32, :],
                     ALU.mult), inc="VE")
                E.sbar("vector", "VE")
                E.op("vector", lambda e, t0=t0: e.tensor_tensor(krope[0:32, t0:t0 + 512],
                     rt1b[:, :], rt2b[:, :], ALU.subtract), inc="VE")
                E.sbar("vector", "VE")
                E.op("vector", lambda e: e.tensor_tensor(rt1b[:, :], pK2[0:32, :], rs2[0:32, :],
                     ALU.mult), inc="VE")
                E.op("vector", lambda e: e.tensor_tensor(rt2b[:, :], pK2[32:64, :], rc2[0:32, :],
                     ALU.mult), inc="VE")
                E.sbar("vector", "VE")
                krn = E.op("vector", lambda e, t0=t0: e.tensor_tensor(krope[32:64, t0:t0 + 512],
                           rt1b[:, :], rt2b[:, :], ALU.add), inc="VE")
                st["a2krP"] = ("VE", krn)
                st["a2rope"] = ("VE", krn)
            A2_f = {"sync": ("SV", E.cnt("SV")), "tensor": ("TE", E.cnt("TE")),
                    "vector": ("VE", E.cnt("VE")), "scalar": ("AE", E.cnt("AE")),
                    "gpsimd": ("LW2", E.cnt("LW2"))}
            A2_extra = [("SK0", E.cnt("SK0")), ("SK1", E.cnt("SK1"))]
        E.barrier(A2_f)
        for f in A2_extra:
            for eng in Emit.ENGINES:
                E.wait(eng, f)

        with contextlib.ExitStack() as pst:
            psb = lambda name, shape, dt: pst.enter_context(nc.sbuf_tensor(name, shape, dt))
            cvb = [psb(f"cvb{i}", [128, 4096], BF16) for i in range(2)]
            pairs = []
            for (src, dst, rows, cols) in ((wo_h, wo_b_h, D, D), (wg_h, wg_b_h, D, HID),
                                           (wu_h, wu_b_h, D, HID), (wd_h, wd_b_h, HID, D)):
                src_r = src.rearrange("a b -> (a b)").rearrange("(c p n) -> p c n",
                                                                p=128, n=4096)
                dst_r = dst.rearrange("a b -> (a b)").rearrange("(c p n) -> p c n",
                                                                p=128, n=4096)
                for c in range(rows * cols // (128 * 4096)):
                    pairs.append((src_r, dst_r, c))
            for i, (src_r, dst_r, c) in enumerate(pairs):
                bi = i % 2
                E.wait("gpsimd", st.get(f"cv{bi}"))
                ldn = E.op("gpsimd", lambda e, src_r=src_r, c=c, bi=bi: e.dma_start(
                    out=cvb[bi][:, :], in_=src_r[:, c, :]),
                    inc=f"CL{bi}", k=16)
                E.wait("gpsimd", (f"CL{bi}", ldn))
                stn = E.op("gpsimd", lambda e, dst_r=dst_r, c=c, bi=bi: e.dma_start(
                    out=dst_r[:, c, :], in_=cvb[bi][:, :]),
                    inc=f"CS{bi}", k=16)
                st[f"cv{bi}"] = (f"CS{bi}", stn)
            P1_f = {"gpsimd": ("CS0", E.cnt("CS0"))}
            P1_extra = [("CS1", E.cnt("CS1"))]
        E.barrier(P1_f)
        for f in P1_extra:
            for eng in Emit.ENGINES:
                E.wait(eng, f)

        with contextlib.ExitStack() as pst:
            psb = lambda name, shape, dt: pst.enter_context(nc.sbuf_tensor(name, shape, dt))
            pps = lambda name, shape, dt: pst.enter_context(nc.psum_tensor(name, shape, dt))
            qn_sb = psb("qn_sb", [128, 16, 512], BF16)
            qp_sb = psb("qp_sb", [64, 16, 512], BF16)
            kn_g = [psb(f"kn_g{i}", [128, 16, 512], BF16) for i in range(3)]
            v_g = [psb(f"v_g{i}", [128, 4, NH * DN], BF16) for i in range(3)]
            probs = [psb(f"probs{i}", [128, 512], BF16) for i in range(2)]
            probsT = [psb(f"probsT{i}", [128, 512], BF16) for i in range(2)]
            Oacc = psb("Oacc", [128, 64 * 128], F32)
            l_sb = psb("l_sb", [128, 64], F32)
            rl_sb = psb("rl_sb", [128, 64], F32)
            cs_sb = psb("cs_sb", [128, 2], F32)
            obf = psb("obf", [128, 128], BF16)
            att_st = psb("att_st", [128, 16, 512], BF16)
            pS = [pps(f"pS{i}", [128, 512], F32) for i in range(2)]
            pP = [pps(f"pP{i}", [128, 512], BF16) for i in range(2)]
            pO = [pps(f"pO{i}", [128, 512], F32) for i in range(2)]

            inst = 0
            pending = [None]
            E.wait("scalar", P0_GE)

            def stage_b(p):
                if p is None:
                    return
                (si, W, nsub, gr0, gr1, h, qs, col, first, ex, rels) = p
                E.wait("tensor", ("AE", ex))
                E.wait("tensor", st.get(f"bpp{si}"))
                for j in range(nsub):
                    E.op("tensor", lambda e, si=si, j=j: e.transpose(
                        pP[si][:, j * 128:j * 128 + 128],
                        probs[si][:, j * 128:j * 128 + 128], id_sb[:, :]),
                        inc=("TE" if j == nsub - 1 else None))
                ptn = E.cnt("TE")
                st[f"bprb{si}"] = ("TE", ptn)
                if si == 0:
                    E.wait("vector", ("TE", ptn))
                    E.wait("vector", st.get(f"bptT{si}"))
                    pc = E.op("vector", lambda e, si=si, W=W: e.tensor_copy(
                        probsT[si][:, 0:W], pP[si][:, 0:W]), inc="VE")
                    pcf = ("VE", pc)
                else:
                    E.wait("scalar", ("TE", ptn))
                    E.wait("scalar", st.get(f"bptT{si}"))
                    pc = E.op("scalar", lambda e, si=si, W=W: e.copy(
                        probsT[si][:, 0:W], pP[si][:, 0:W]), inc="AE")
                    pcf = ("AE", pc)
                st[f"bpp{si}"] = pcf
                E.wait("tensor", pcf)
                E.wait("tensor", st.get(f"bpo{si}"))
                for j in range(nsub):
                    g = gr0 if j < 4 else gr1
                    gi = g % 3
                    subl = j % 4
                    E.op("tensor", lambda e, si=si, j=j, gi=gi, subl=subl, h=h, nsub=nsub: e.matmul(
                        pO[si][:, 0:128], probsT[si][:, j * 128:j * 128 + 128],
                        v_g[gi][:, subl, h * 128:h * 128 + 128],
                        start=(j == 0), stop=(j == nsub - 1)),
                        inc=("TE" if j == nsub - 1 else None))
                on = E.cnt("TE")
                st[f"bptT{si}"] = ("TE", on)
                for key in rels:
                    st[key] = ("TE", on)
                E.wait("vector", ("TE", on))
                if first:
                    ov = E.op("vector", lambda e, col=col, si=si: e.tensor_copy(
                        Oacc[:, col * 128:col * 128 + 128], pO[si][:, 0:128]),
                        inc="VE")
                else:
                    ov = E.op("vector", lambda e, col=col, si=si: e.tensor_tensor(
                        Oacc[:, col * 128:col * 128 + 128],
                        Oacc[:, col * 128:col * 128 + 128], pO[si][:, 0:128],
                        ALU.add), inc="VE")
                st[f"bpo{si}"] = ("VE", ov)

            for qt in range(min(NT, B_MAX_QT)):
                t0 = qt * 512
                E.wait("sync", st.get("bq_rel"))
                E.op("sync", lambda e, t0=t0: e.dma_start(out=qn_sb[:, :, :],
                     in_=r128(s_qn_h)[:, :, t0:t0 + 512]), inc="BQ", k=16)
                qld = E.op("sync", lambda e, t0=t0: e.dma_start(out=qp_sb[:, :, :],
                     in_=r64(s_qp_h)[:, :, t0:t0 + 512]), inc="BQ", k=16)
                ngr = qt + 1

                def load_gran(g):
                    gi = g % 3
                    E.wait("sync", st.get(f"bg{gi}"))
                    E.op("sync", lambda e, g=g, gi=gi: e.dma_start(out=kn_g[gi][:, :, :],
                         in_=r128(s_kn_h)[:, :, g * 512:g * 512 + 512]), inc=f"BK{gi}", k=16)
                    n = E.op("sync", lambda e, g=g, gi=gi: e.dma_start(out=v_g[gi][:, :, :],
                         in_=s_v_h.rearrange("(s p) d -> p s d", p=128)[:, g * 4:g * 4 + 4, :]),
                         inc=f"BK{gi}", k=16)
                    st[f"bgr{gi}"] = (f"BK{gi}", n)

                load_gran(0)
                for c in range(ngr):
                    stage_b(pending[0])
                    pending[0] = None
                    if c + 1 < ngr:
                        load_gran(c + 1)
                    k0 = c * 512
                    gci = c % 3
                    rdy0 = st[f"bgr{gci}"]
                    for h in range(16):
                        for qs in range(4):
                            limit = t0 + qs * 128 + 128
                            if k0 >= limit:
                                continue
                            W = min(512, limit - k0)
                            si = inst % 2
                            first = (c == 0)
                            E.wait("tensor", ("BQ", qld))
                            E.wait("tensor", rdy0)
                            E.wait("tensor", st.get(f"bexp{si}"))
                            E.op("tensor", lambda e, h=h, qs=qs, si=si, gci=gci: e.matmul(
                                pS[si][:, :], qn_sb[:, h, qs * 128:qs * 128 + 128],
                                kn_g[gci][:, h, :], start=True, stop=False), inc=None)
                            E.op("tensor", lambda e, h=h, qs=qs, si=si, c=c: e.matmul(
                                pS[si][:, :], qp_sb[:, h, qs * 128:qs * 128 + 128],
                                krope[:, c * 512:c * 512 + 512], start=False, stop=True),
                                inc="TE")
                            smm = E.cnt("TE")
                            diag = (k0 + W == limit)
                            E.wait("vector", ("TE", smm))
                            if diag:
                                moff = W - 128
                                E.op("vector", lambda e, si=si, moff=moff: e.tensor_tensor(
                                    pS[si][:, moff:moff + 128], pS[si][:, moff:moff + 128],
                                    tm_sb[:, :], ALU.add), inc="VE")
                            msk = ("VE", E.cnt("VE"))
                            E.wait("scalar", ("TE", smm))
                            if diag:
                                E.wait("scalar", msk)
                            E.wait("scalar", st.get(f"bcs{si}"))
                            E.wait("scalar", st.get(f"bprb{si}"))
                            ex = E.op("scalar", lambda e, si=si, W=W: e.activation(
                                probs[si][:, 0:W], pS[si][:, 0:W], AF.Exp,
                                bias=mshift[:, 0:1], scale=SCALE,
                                accum_out=cs_sb[:, si:si + 1]), inc="AE")
                            st[f"bexp{si}"] = ("AE", ex)
                            col = h * 4 + qs
                            E.wait("vector", ("AE", ex))
                            if first:
                                lv = E.op("vector", lambda e, col=col, si=si: e.tensor_copy(
                                    l_sb[:, col:col + 1], cs_sb[:, si:si + 1]), inc="VE")
                            else:
                                lv = E.op("vector", lambda e, col=col, si=si: e.tensor_tensor(
                                    l_sb[:, col:col + 1], l_sb[:, col:col + 1],
                                    cs_sb[:, si:si + 1], ALU.add), inc="VE")
                            st[f"bcs{si}"] = ("VE", lv)
                            nsub = (W + 127) // 128
                            rels = [f"bg{gci}"]
                            if c == ngr - 1:
                                rels.append("bq_rel")
                            stage_b(pending[0])
                            pending[0] = (si, W, nsub, c, c, h, qs, col, first, ex, rels)
                            inst += 1
                stage_b(pending[0])
                pending[0] = None
                if not B_FIN:
                    continue
                E.sbar("vector", "VE")
                E.op("vector", lambda e: e.reciprocal(rl_sb[:, :], l_sb[:, :]), inc="VE")
                E.sbar("vector", "VE")
                E.wait("vector", st.get("bat_rel"))
                for h in range(16):
                    for qs in range(4):
                        col = h * 4 + qs
                        E.wait("vector", st.get("bobf"))
                        ob = E.op("vector", lambda e, col=col: e.tensor_scalar_mul(
                            obf[:, :], Oacc[:, col * 128:col * 128 + 128],
                            rl_sb[:, col:col + 1]), inc="VE")
                        E.wait("tensor", ("VE", ob))
                        E.wait("tensor", st.get("bfinc"))
                        E.wait("tensor", st.get("bpp0"))
                        E.op("tensor", lambda e, qs=qs: e.transpose(
                            pP[0][:, qs * 128:qs * 128 + 128], obf[:, :], id_sb[:, :]),
                            inc="TE")
                        tn = E.cnt("TE")
                        st["bobf"] = ("TE", tn)
                        E.wait("vector", ("TE", tn))
                        cpa = E.op("vector", lambda e, h=h, qs=qs: e.tensor_copy(
                            att_st[:, h, qs * 128:qs * 128 + 128],
                            pP[0][:, qs * 128:qs * 128 + 128]), inc="VE")
                        st["bfinc"] = ("VE", cpa)
                        st["bpp0"] = ("VE", cpa)
                fin = ("VE", E.cnt("VE"))
                E.wait("sync", fin)
                atn = E.op("sync", lambda e, t0=t0: e.dma_start(
                    out=r128(s_at_h)[:, :, t0:t0 + 512], in_=att_st[:, :, :]), inc="SAT", k=16)
                st["bat_rel"] = ("SAT", atn)
            B_f = {"sync": ("SAT", E.cnt("SAT")), "tensor": ("TE", E.cnt("TE")),
                   "vector": ("VE", E.cnt("VE")), "scalar": ("AE", E.cnt("AE"))}
        E.barrier(B_f)

        with contextlib.ExitStack() as pst:
            psb = lambda name, shape, dt: pst.enter_context(nc.sbuf_tensor(name, shape, dt))
            pps = lambda name, shape, dt: pst.enter_context(nc.psum_tensor(name, shape, dt))
            arena = [psb(f"arena{i}", [128, 16, 512], BF16) for i in range(3)]
            xc_sb = psb("xc_sbC", [128, 4, D], F32)
            yz_sb = psb("yz_sb", [128, 4, D], F32)
            gT_sb = psb("gT_sb", [128, 32, 512], BF16)
            h2_bf = psb("h2_bf", [128, D], BF16)
            h2T_sb = psb("h2T_sb", [128, 16, 512], BF16)
            sg_sb = [psb(f"sg_sb{i}", [128, 512], BF16) for i in range(2)]
            t_scr = psb("t_scr", [128, D], F32)
            out_sb = psb("out_sb", [128, D], F32)
            sums3 = psb("sums3", [128, 1], F32)
            ss3 = psb("ss3", [128, 1], F32)
            nmu3 = psb("nmu3", [128, 1], F32)
            rst3 = psb("rst3", [128, 1], F32)
            s43 = (sums3[:, 0:1], ss3, nmu3[:, 0:1], rst3[:, 0:1])
            pG = [pps(f"pG{i}", [128, 512], F32) for i in range(2)]
            pU = [pps(f"pU{i}", [128, 512], F32) for i in range(2)]
            pD = [pps(f"pD{i}", [128, 512], F32) for i in range(2)]
            pH = pps("pH", [128, 2048], BF16)

            arena_i = 0

            def arena_load(src_ap, pin=False):
                nonlocal arena_i
                if pin:
                    sl = 2
                else:
                    sl = arena_i % 2
                    arena_i += 1
                E.wait("sync", st.get(f"car{sl}"))
                n = E.op("sync", lambda e, sl=sl, src_ap=src_ap: e.dma_start(
                    out=arena[sl][:, :, :], in_=src_ap), inc=f"CA{sl}", k=16)
                return sl, (f"CA{sl}", n)

            for t in range(min(NT, C_MAX_T)):
                t0 = t * 512
                ti = t % 2
                sl_a, af_a = arena_load(r128(s_at_h)[:, :, t0:t0 + 512], pin=True)
                E.wait("sync", st.get("cx_rel"))
                for qs in range(4):
                    xn = E.op("sync", lambda e, t0=t0, qs=qs: e.dma_start(
                        out=xc_sb[:, qs, :], in_=x_h[t0 + qs * 128:t0 + qs * 128 + 128, :]),
                        inc=f"CX{ti}", k=16)
                cx_rdy = (f"CX{ti}", xn)
                for fo in range(4):
                    sl, af = arena_load(r128(wo_b_h)[:, :, fo * 512:fo * 512 + 512])
                    E.wait("tensor", af)
                    E.wait("tensor", af_a)
                    for qs in range(4):
                        pi = (fo * 4 + qs) % 2
                        E.wait("tensor", st.get(f"cpg{pi}"))
                        for kc in range(16):
                            E.op("tensor", lambda e, kc=kc, qs=qs, sl=sl, pi=pi, sl_a=sl_a: e.matmul(
                                pG[pi][:, :], arena[sl_a][:, kc, qs * 128:qs * 128 + 128],
                                arena[sl][:, kc, :], start=(kc == 0), stop=(kc == 15)),
                                inc=("TE" if kc == 15 else None))
                        mmn = E.cnt("TE")
                        if qs == 3:
                            st[f"car{sl}"] = ("TE", mmn)
                            if fo == 3:
                                st[f"car{sl_a}"] = ("TE", mmn)
                        E.wait("vector", ("TE", mmn))
                        E.wait("vector", st.get("cyz_w"))
                        ev = E.op("vector", lambda e, qs=qs, fo=fo, pi=pi: e.tensor_copy(
                            yz_sb[:, qs, fo * 512:fo * 512 + 512], pG[pi][:, :]), inc="VE")
                        st[f"cpg{pi}"] = ("VE", ev)
                yrdy = ("VE", E.cnt("VE"))
                if dbg:
                    E.wait("sync", yrdy)
                    E.op("sync", lambda e, t0=t0: e.dma_start(
                        out=d_wo_h.rearrange("(a p) d -> p a d", p=128)[:, t0 // 128:t0 // 128 + 4, :],
                        in_=yz_sb[:, :, :]), inc="DBG", k=16)
                for qs in range(4):
                    E.wait("vector", st.get("ct_rel"))
                    apf = ln_chain(yz_sb[:, qs, :], t_scr[:, :], D, epsln, s43, "lnC",
                                   src_f=(yrdy if qs == 0 else None))
                    E.wait("gpsimd", apf)
                    E.wait("gpsimd", cx_rdy)
                    gadd = E.op("gpsimd", lambda e, qs=qs: e.tensor_tensor(
                        xc_sb[:, qs, :], xc_sb[:, qs, :], t_scr[:, :], ALU.add), inc="GE")
                    st["ct_rel"] = ("GE", gadd)
                    E.wait("vector", ("GE", gadd))
                    E.wait("scalar", ("GE", gadd))
                    E.wait("vector", st.get("ch2"))
                    apf2 = ln_chain(xc_sb[:, qs, :], h2_bf[:, :],
                                    D, epsln, s43, "lnC", src_f=None)
                    E.wait("tensor", apf2)
                    E.wait("tensor", st.get("cpH"))
                    for cc in range(16):
                        E.op("tensor", lambda e, cc=cc: e.transpose(
                            pH[:, cc * 128:(cc + 1) * 128], h2_bf[:, cc * 128:(cc + 1) * 128],
                            id_sb[:, :]), inc=("TE" if cc == 15 else None))
                    tn = E.cnt("TE")
                    st["ch2"] = ("TE", tn)
                    E.wait("vector", ("TE", tn))
                    E.wait("vector", st.get("ch2T"))
                    cph = E.op("vector", lambda e, qs=qs: e.tensor_copy(
                        h2T_sb[:, :, qs * 128:qs * 128 + 128],
                        pH[:, :]), inc="VE")
                    st["cpH"] = ("VE", cph)
                st["cyz_w2"] = ("VE", E.cnt("VE"))
                if dbg:
                    E.wait("sync", st.get("ct_rel"))
                    E.op("sync", lambda e, t0=t0: e.dma_start(
                        out=d_x2_h.rearrange("(a p) d -> p a d", p=128)[:, t0 // 128:t0 // 128 + 4, :],
                        in_=xc_sb[:, :, :]), inc="DBG", k=16)
                h2T_rdy = ("VE", E.cnt("VE"))
                E.wait("tensor", h2T_rdy)
                for half in range(2):
                    hb = half * 32
                    for hcg in range(8):
                        slg, afg = arena_load(
                            r128(wg_b_h)[:, :, (hb + hcg * 4) * 128:(hb + hcg * 4) * 128 + 512])
                        slu, afu = arena_load(
                            r128(wu_b_h)[:, :, (hb + hcg * 4) * 128:(hb + hcg * 4) * 128 + 512])
                        for hl in range(4):
                            hc = hcg * 4 + hl
                            pi = hc % 2
                            E.wait("tensor", afg)
                            E.wait("tensor", st.get(f"cpg{pi}"))
                            for kc in range(16):
                                E.op("tensor", lambda e, kc=kc, hl=hl, slg=slg, pi=pi: e.matmul(
                                    pG[pi][:, :], arena[slg][:, kc, hl * 128:hl * 128 + 128],
                                    h2T_sb[:, kc, :], start=(kc == 0), stop=(kc == 15)),
                                    inc=("TE" if kc == 15 else None))
                            gmm = E.cnt("TE")
                            E.wait("tensor", afu)
                            E.wait("tensor", st.get(f"cpu{pi}"))
                            for kc in range(16):
                                E.op("tensor", lambda e, kc=kc, hl=hl, slu=slu, pi=pi: e.matmul(
                                    pU[pi][:, :], arena[slu][:, kc, hl * 128:hl * 128 + 128],
                                    h2T_sb[:, kc, :], start=(kc == 0), stop=(kc == 15)),
                                    inc=("TE" if kc == 15 else None))
                            umm = E.cnt("TE")
                            if hl == 3:
                                st[f"car{slg}"] = ("TE", gmm)
                                st[f"car{slu}"] = ("TE", umm)
                            E.wait("scalar", ("TE", gmm))
                            E.wait("scalar", st.get(f"csg{pi}"))
                            sg_n = E.op("scalar", lambda e, pi=pi: e.activation(
                                sg_sb[pi][:, :], pG[pi][:, :], AF.Silu), inc="AE")
                            st[f"cpg{pi}"] = ("AE", sg_n)
                            E.wait("vector", ("AE", sg_n))
                            E.wait("vector", ("TE", umm))
                            E.wait("vector", st.get("cgT_rel"))
                            gv = E.op("vector", lambda e, hc=hc, pi=pi: e.tensor_tensor(
                                gT_sb[:, hc % 32, :], sg_sb[pi][:, :], pU[pi][:, :], ALU.mult),
                                inc="VE")
                            st[f"csg{pi}"] = ("VE", gv)
                            st[f"cpu{pi}"] = ("VE", gv)
                    gT_rdy = ("VE", E.cnt("VE"))
                    E.wait("tensor", gT_rdy)
                    for fo in range(4):
                        wdsl = []
                        for j in range(2):
                            slw, afw = arena_load(
                                r128(wd_b_h)[:, (hb + j * 16):(hb + j * 16) + 16,
                                             fo * 512:fo * 512 + 512])
                            wdsl.append((slw, afw))
                        for qs in range(4):
                            pi = qs % 2
                            E.wait("tensor", st.get(f"cpd{pi}"))
                            for j in range(2):
                                slw, afw = wdsl[j]
                                E.wait("tensor", afw)
                                for hl in range(16):
                                    hc = j * 16 + hl
                                    E.op("tensor", lambda e, hc=hc, hl=hl, qs=qs, slw=slw, pi=pi, j=j: e.matmul(
                                        pD[pi][:, :], gT_sb[:, hc, qs * 128:qs * 128 + 128],
                                        arena[slw][:, hl, :],
                                        start=(j == 0 and hl == 0), stop=(j == 1 and hl == 15)),
                                        inc=("TE" if (j == 1 and hl == 15) else None))
                            dmm = E.cnt("TE")
                            if qs == 3:
                                for (slw, afw) in wdsl:
                                    st[f"car{slw}"] = ("TE", dmm)
                                if fo == 3:
                                    st["cgT_rel"] = ("TE", dmm)
                            E.wait("vector", ("TE", dmm))
                            E.wait("vector", st.get("cyz_w2"))
                            if half == 0:
                                ev = E.op("vector", lambda e, qs=qs, fo=fo, pi=pi: e.tensor_copy(
                                    yz_sb[:, qs, fo * 512:fo * 512 + 512], pD[pi][:, :]),
                                    inc="VE")
                            else:
                                ev = E.op("vector", lambda e, qs=qs, fo=fo, pi=pi: e.tensor_tensor(
                                    yz_sb[:, qs, fo * 512:fo * 512 + 512],
                                    yz_sb[:, qs, fo * 512:fo * 512 + 512], pD[pi][:, :],
                                    ALU.add), inc="VE")
                            st[f"cpd{pi}"] = ("VE", ev)
                zrdy = ("VE", E.cnt("VE"))
                if dbg:
                    E.wait("sync", zrdy)
                    E.op("sync", lambda e, t0=t0: e.dma_start(
                        out=d_z_h.rearrange("(a p) d -> p a d", p=128)[:, t0 // 128:t0 // 128 + 4, :],
                        in_=yz_sb[:, :, :]), inc="DBG", k=16)
                for qs in range(4):
                    E.wait("vector", st.get("ct_rel"))
                    apf = ln_chain(yz_sb[:, qs, :], t_scr[:, :], D, epsln, s43, "lnC",
                                   src_f=(zrdy if qs == 0 else None))
                    E.wait("gpsimd", apf)
                    E.wait("gpsimd", st.get("cout"))
                    E.sbar("gpsimd", "GE")
                    gadd = E.op("gpsimd", lambda e, qs=qs: e.tensor_tensor(
                        out_sb[:, :], xc_sb[:, qs, :], t_scr[:, :], ALU.add), inc="GE")
                    st["ct_rel"] = ("GE", gadd)
                    if qs == 3:
                        st["cx_rel"] = ("GE", gadd)
                    E.wait("sync", ("GE", gadd))
                    yo = E.op("sync", lambda e, t0=t0, qs=qs: e.dma_start(
                        out=y_h[t0 + qs * 128:t0 + qs * 128 + 128, :], in_=out_sb[:, :]),
                        inc="YO", k=16)
                    st["cout"] = ("YO", yo)
                st["cyz_w"] = ("VE", E.cnt("VE"))
            E.wait("sync", ("YO", E.cnt("YO")))
            if dbg:
                E.wait("sync", ("DBG", E.cnt("DBG")))

        with nc.Block() as block:
            @block.sync
            def _(e):
                for f in E.ops["sync"]:
                    f(e)

            @block.tensor
            def _(e):
                for f in E.ops["tensor"]:
                    f(e)

            @block.vector
            def _(e):
                for f in E.ops["vector"]:
                    f(e)

            @block.scalar
            def _(e):
                for f in E.ops["scalar"]:
                    f(e)

            @block.gpsimd
            def _(e):
                for f in E.ops["gpsimd"]:
                    f(e)

    return nc


def _prep_inputs(inputs, S):
    T = lambda a: np.ascontiguousarray(np.asarray(a).T.astype(np.float32))
    wqr = T(inputs["L0_attn_wq_rope"])
    tq = np.empty(NH * DR, dtype=np.int64)
    for d in range(NH * DR):
        H, r = d // DR, d % DR
        tq[d] = (512 if r % 2 else 0) + H * 32 + r // 2
    wqr_p = np.empty_like(wqr)
    wqr_p[:, tq] = wqr
    wkr = T(inputs["L0_attn_wk_rope"])
    tk = np.empty(DR, dtype=np.int64)
    for r in range(DR):
        tk[r] = (32 if r % 2 else 0) + r // 2
    wkr_p = np.empty_like(wkr)
    wkr_p[:, tk] = wkr
    ident = np.eye(128, dtype=np.float32)
    ii = np.arange(128)
    trimask = np.where(ii[None, :] <= ii[:, None], 0.0, NEGBIG).astype(np.float32)
    inv = 10000.0 ** (-(np.arange(32, dtype=np.float64) * 2.0 / DR))
    fr = np.outer(inv, np.arange(S, dtype=np.float64))
    rcos = np.tile(np.cos(fr), (4, 1)).astype(np.float32)
    rsin = np.tile(np.sin(fr), (4, 1)).astype(np.float32)
    return dict(
        wqd=T(inputs["L0_attn_wq_down"]), wqu=T(inputs["L0_attn_wq_up"]), wqr=wqr_p,
        qnw=np.ascontiguousarray(np.asarray(inputs["L0_attn_q_norm_w"], dtype=np.float32).reshape(QR // 128, 128).T),
        wkd=T(inputs["L0_attn_wkv_down"]),
        kvw=np.ascontiguousarray(np.asarray(inputs["L0_attn_kv_norm_w"], dtype=np.float32).reshape(KVR // 128, 128).T),
        wku=T(inputs["L0_attn_wkv_up"]), wkr=wkr_p, wo=T(inputs["L0_attn_wo"]),
        wg=T(inputs["L0_ffn_w_gate"]), wu=T(inputs["L0_ffn_w_up"]),
        wd=T(inputs["L0_ffn_w_down"]), ident=ident, trimask=trimask,
        rcos=rcos, rsin=rsin)


def kernel(**inputs):
    S = 4096
    x = np.asarray(inputs["L0_attn_x"])
    B = x.shape[0]
    common = _prep_inputs(inputs, S)
    in_maps = [dict(common, x=np.ascontiguousarray(x[c], dtype=np.float32)) for c in range(B)]
    nc = build_program(S)
    res = run_bass_kernel_spmd(nc, in_maps, core_ids=list(range(B)))
    out = np.stack([np.asarray(res.results[c]["y"]) for c in range(B)], axis=0)
    return out.astype(np.float32)


# version 02
# speedup vs baseline: 1.0087x; 1.0087x over previous
import math
import contextlib
import numpy as np
import concourse.bass as bass
import concourse.mybir as mybir
from concourse.bass_utils import run_bass_kernel_spmd

F32 = mybir.dt.float32
BF16 = mybir.dt.bfloat16
AF = mybir.ActivationFunctionType
ALU = mybir.AluOpType
AX = mybir.AxisListType

D = 2048
NH = 16
QR = 1536
KVR = 512
DN = 128
DR = 64
HID = 8192
SCALE = 1.0 / math.sqrt(DN + DR)
MSHIFT = -10.0
EPS_LN = 1e-5
EPS_RMS = 1e-6
NEGBIG = -1.0e30
B_MAX_QT = 10**9
B_MAX_CH = 10**9
B_PREFETCH = True
B_H_MAX = 16
B_FIN = True
C_MAX_T = 10**9
A_MAX_T = 10**9


class Emit:
    ENGINES = ("sync", "tensor", "vector", "scalar", "gpsimd")

    def __init__(self, nc, stack):
        self.nc = nc
        self.stack = stack
        self.ops = {e: [] for e in self.ENGINES}
        self.h = {}
        self.n = {}

    def sem(self, name):
        if name not in self.h:
            self.h[name] = self.stack.enter_context(self.nc.semaphore(name))
            self.n[name] = 0
        return self.h[name]

    def op(self, eng, fn, inc=None, k=1):
        if inc is None:
            self.ops[eng].append(fn)
            return None
        hh = self.sem(inc)
        self.n[inc] += k
        self.ops[eng].append(lambda e, f=fn, hh=hh, k=k: f(e).then_inc(hh, k))
        return self.n[inc]

    def wait(self, eng, fence):
        if fence is None:
            return
        name, val = fence
        if val is None or val <= 0:
            return
        hh = self.sem(name)
        self.ops[eng].append(lambda e, hh=hh, v=val: e.wait_ge(hh, v))

    def cnt(self, name):
        self.sem(name)
        return self.n[name]

    def sbar(self, eng, ctr):
        self.wait(eng, (ctr, self.n.get(ctr, 0)))

    def barrier(self, fences):
        for eng in self.ENGINES:
            for other, f in fences.items():
                if other != eng:
                    self.wait(eng, f)


def build_program(S, dbg=False):
    nc = bass.Bass("TRN2", target_bir_lowering=False)
    NT = S // 512
    skind = "ExternalOutput" if dbg else "Internal"

    x_h = nc.dram_tensor("x", [S, D], F32, kind="ExternalInput").ap()
    wqd_h = nc.dram_tensor("wqd", [D, QR], F32, kind="ExternalInput").ap()
    wqu_h = nc.dram_tensor("wqu", [QR, NH * DN], F32, kind="ExternalInput").ap()
    wqr_h = nc.dram_tensor("wqr", [QR, NH * DR], F32, kind="ExternalInput").ap()
    qnw_h = nc.dram_tensor("qnw", [128, QR // 128], F32, kind="ExternalInput").ap()
    wkd_h = nc.dram_tensor("wkd", [D, KVR], F32, kind="ExternalInput").ap()
    kvw_h = nc.dram_tensor("kvw", [128, KVR // 128], F32, kind="ExternalInput").ap()
    wku_h = nc.dram_tensor("wku", [KVR, NH * 256], F32, kind="ExternalInput").ap()
    wkr_h = nc.dram_tensor("wkr", [D, DR], F32, kind="ExternalInput").ap()
    wo_h = nc.dram_tensor("wo", [D, D], F32, kind="ExternalInput").ap()
    wg_h = nc.dram_tensor("wg", [D, HID], F32, kind="ExternalInput").ap()
    wu_h = nc.dram_tensor("wu", [D, HID], F32, kind="ExternalInput").ap()
    wd_h = nc.dram_tensor("wd", [HID, D], F32, kind="ExternalInput").ap()
    id_h = nc.dram_tensor("ident", [128, 128], F32, kind="ExternalInput").ap()
    tm_h = nc.dram_tensor("trimask", [128, 128], F32, kind="ExternalInput").ap()
    rc_h = nc.dram_tensor("rcos", [128, S], F32, kind="ExternalInput").ap()
    rs_h = nc.dram_tensor("rsin", [128, S], F32, kind="ExternalInput").ap()
    y_h = nc.dram_tensor("y", [S, D], F32, kind="ExternalOutput").ap()
    s_h_h = nc.dram_tensor("s_h", [D, S], BF16, kind=skind).ap()
    s_qn_h = nc.dram_tensor("s_qn", [NH * DN, S], BF16, kind=skind).ap()
    s_qp_h = nc.dram_tensor("s_qp", [NH * DR, S], BF16, kind=skind).ap()
    s_kn_h = nc.dram_tensor("s_kn", [NH * DN, S], BF16, kind=skind).ap()
    s_v_h = nc.dram_tensor("s_v", [S, NH * DN], BF16, kind=skind).ap()
    s_at_h = nc.dram_tensor("s_at", [D, S], BF16, kind=skind).ap()
    if dbg:
        d_wo_h = nc.dram_tensor("d_wo", [S, D], F32, kind="ExternalOutput").ap()
        d_x2_h = nc.dram_tensor("d_x2", [S, D], F32, kind="ExternalOutput").ap()
        d_z_h = nc.dram_tensor("d_z", [S, D], F32, kind="ExternalOutput").ap()
    wo_b_h = nc.dram_tensor("wo_b", [D, D], BF16, kind="Internal").ap()
    wg_b_h = nc.dram_tensor("wg_b", [D, HID], BF16, kind="Internal").ap()
    wu_b_h = nc.dram_tensor("wu_b", [D, HID], BF16, kind="Internal").ap()
    wd_b_h = nc.dram_tensor("wd_b", [HID, D], BF16, kind="Internal").ap()

    r128 = lambda ap_: ap_.rearrange("(c p) t -> p c t", p=128)
    r64 = lambda ap_: ap_.rearrange("(c p) t -> p c t", p=64)

    stack = contextlib.ExitStack()
    with stack:
        E = Emit(nc, stack)
        sb = lambda name, shape, dt: stack.enter_context(nc.sbuf_tensor(name, shape, dt))
        st = {}

        id_sb = sb("id_sb", [128, 128], BF16)
        tm_sb = sb("tm_sb", [128, 128], F32)
        krope = sb("krope", [64, S], BF16)
        epsln = sb("epsln", [128, 1], F32)
        epsrm = sb("epsrm", [128, 1], F32)
        mshift = sb("mshift", [128, 1], F32)
        junk = sb("junk", [128, 2048], BF16)

        E.op("gpsimd", lambda e: e.dma_start(out=id_sb[:, :], in_=id_h[:, :]), inc="LW0", k=16)
        E.op("gpsimd", lambda e: e.dma_start(out=tm_sb[:, :], in_=tm_h[:, :]), inc="LW0", k=16)
        E.op("gpsimd", lambda e: e.memset(epsln[:, :], EPS_LN), inc="GE")
        E.op("gpsimd", lambda e: e.memset(epsrm[:, :], EPS_RMS), inc="GE")
        E.op("gpsimd", lambda e: e.memset(mshift[:, :], MSHIFT), inc="GE")
        P0_LW = ("LW0", E.cnt("LW0"))
        P0_GE = ("GE", E.cnt("GE"))

        def ln_chain(src, out, width, eps_tile, s4, tag, src_f=None, rms=False,
                     ss_ready=None, nss=1):
            sums, ss, nmu, rst = s4
            prev = st.get(tag)
            if src_f is not None:
                E.wait("vector", src_f)
                E.wait("scalar", src_f)
            if prev is not None:
                E.wait("vector", ("AE", prev[1]))
                E.wait("scalar", ("VE", prev[0]))
            if not rms:
                E.op("vector", lambda e: e.reduce_sum(sums, src, axis=AX.X), inc="VE")
            if ss_ready is None:
                E.sbar("scalar", "AE")
                a_sq = E.op("scalar", lambda e: e.activation(junk[:, 0:width], src, AF.Square,
                                                             accum_out=ss[:, 0:1]), inc="AE")
            else:
                a_sq = ss_ready[1]
                E.wait("scalar", ss_ready)
            E.wait("vector", ("AE", a_sq))
            if nss > 1:
                E.op("vector", lambda e: e.reduce_sum(sums if rms else rst, ss[:, 0:nss],
                                                      axis=AX.X), inc="VE")
            if rms:
                srcss = sums if nss > 1 else ss
                E.sbar("vector", "VE")
                E.op("vector", lambda e: e.tensor_scalar_mul(rst, srcss[:, 0:1], 1.0 / width),
                     inc="VE")
            else:
                E.sbar("vector", "VE")
                E.op("vector", lambda e: e.tensor_scalar_mul(nmu, sums, -1.0 / width), inc="VE")
                if nss > 1:
                    E.op("vector", lambda e: e.tensor_scalar_mul(rst, rst, 1.0 / width),
                         inc="VE")
                else:
                    E.op("vector", lambda e: e.tensor_scalar_mul(rst, ss[:, 0:1], 1.0 / width),
                         inc="VE")
                E.sbar("vector", "VE")
                E.op("vector", lambda e: e.tensor_tensor(sums, nmu, nmu, ALU.mult), inc="VE")
                E.sbar("vector", "VE")
                E.op("vector", lambda e: e.tensor_tensor(rst, rst, sums, ALU.subtract), inc="VE")
            v3 = E.cnt("VE")
            E.wait("scalar", ("VE", v3))
            a_rt = E.op("scalar", lambda e: e.activation(rst, rst, AF.Sqrt,
                                                         bias=eps_tile[:, 0:1]), inc="AE")
            E.wait("vector", ("AE", a_rt))
            E.op("vector", lambda e: e.reciprocal(rst, rst), inc="VE")
            E.sbar("vector", "VE")
            if rms:
                vap = E.op("vector", lambda e: e.tensor_scalar_mul(out, src, rst[:, 0:1]),
                           inc="VE")
            else:
                vap = E.op("vector", lambda e: e.tensor_scalar(out, src, nmu[:, 0:1], rst[:, 0:1],
                                                               ALU.add, ALU.mult), inc="VE")
            st[tag] = (vap, a_rt)
            return ("VE", vap)

        conv_jobs = []
        for (src, dst, rows, rstep) in ((wo_h, wo_b_h, D, 1024), (wg_h, wg_b_h, D, 256),
                                        (wu_h, wu_b_h, D, 256), (wd_h, wd_b_h, HID, 1024)):
            for r0 in range(0, rows, rstep):
                conv_jobs.append((src, dst, r0, r0 + rstep))
        conv_it = iter(conv_jobs)

        def drip_conv(n):
            for _ in range(n):
                job = next(conv_it, None)
                if job is None:
                    return
                src, dst, r0, r1 = job
                E.op("gpsimd", lambda e, src=src, dst=dst, r0=r0, r1=r1: e.dma_start(
                    out=dst[r0:r1, :], in_=src[r0:r1, :]), inc="CVT", k=16)

        with contextlib.ExitStack() as pst:
            psb = lambda name, shape, dt: pst.enter_context(nc.sbuf_tensor(name, shape, dt))
            pps = lambda name, shape, dt: pst.enter_context(nc.psum_tensor(name, shape, dt))
            wqd_sb = psb("wqd_sb", [128, 16, QR], BF16)
            wqu_s = [psb(f"wqu_s{i}", [128, 12, 512], BF16) for i in range(2)]
            wqr_sb = psb("wqr_sb", [128, 12, NH * DR], BF16)
            qnw_sb = psb("qnw_sb", [128, 12], F32)
            x_sb = [psb(f"x_sb{i}", [128, D], F32) for i in range(2)]
            h_sb = [psb(f"h_sb{i}", [128, D], BF16) for i in range(2)]
            hT_sb = psb("hT_sb", [128, 16, 512], BF16)
            qc_sb = psb("qc_sb", [128, QR], F32)
            qcn_sb = psb("qcn_sb", [128, QR], BF16)
            qcT_sb = psb("qcT_sb", [128, 12, 512], BF16)
            qn_st = [psb(f"qn_st{i}", [128, 4, 512], BF16) for i in range(2)]
            qpe_sb = psb("qpe_sb", [128, 8, 512], BF16)
            ro_e = psb("ro_e", [128, 512], BF16)
            ro_o = psb("ro_o", [128, 512], BF16)
            rt1 = psb("rt1", [128, 512], BF16)
            rt2 = psb("rt2", [128, 512], BF16)
            rc_sb = psb("rc_sb", [128, 512], F32)
            rs_sb = psb("rs_sb", [128, 512], F32)
            sums = psb("sums", [128, 1], F32)
            ss = psb("ss", [128, 3], F32)
            nmu = psb("nmu", [128, 1], F32)
            rst = psb("rst", [128, 1], F32)
            s4 = (sums[:, 0:1], ss, nmu[:, 0:1], rst[:, 0:1])
            pA = [pps(f"pA{i}", [128, 512], F32) for i in range(2)]
            pT = [pps(f"pT{i}", [128, 2048], BF16) for i in range(2)]

            E.op("gpsimd", lambda e: e.dma_start(out=wqd_sb[:, :, :], in_=r128(wqd_h)),
                 inc="LW1", k=16)
            E.op("gpsimd", lambda e: e.dma_start(out=wqr_sb[:, :, :], in_=r128(wqr_h)),
                 inc="LW1", k=16)
            E.op("gpsimd", lambda e: e.dma_start(out=qnw_sb[:, :], in_=qnw_h[:, :]),
                 inc="LW1", k=16)
            wA1 = ("LW1", E.cnt("LW1"))
            E.wait("vector", wA1)
            E.wait("tensor", wA1)
            E.wait("tensor", P0_LW)
            E.wait("vector", P0_LW)

            def load_wqu(t, g):
                sl = (t * 4 + g) % 2
                E.wait("gpsimd", st.get(f"wqu{sl}"))
                n = E.op("gpsimd", lambda e, g=g, sl=sl: e.dma_start(
                    out=wqu_s[sl][:, :, :],
                    in_=r128(wqu_h)[:, :, g * 512:g * 512 + 512]), inc=f"WQ{sl}", k=16)
                drip_conv(1)
                return sl, (f"WQ{sl}", n)

            for t in range(NT):
                t0 = t * 512
                E.wait("sync", st.get("a1rope"))
                rs_n = None
                E.op("sync", lambda e, t0=t0: e.dma_start(out=rc_sb[:, :],
                     in_=rc_h[:, t0:t0 + 512]), inc="RT1", k=16)
                rs_n = E.op("sync", lambda e, t0=t0: e.dma_start(out=rs_sb[:, :],
                     in_=rs_h[:, t0:t0 + 512]), inc="RT1", k=16)
                for qs in range(4):
                    xi = qs % 2
                    E.wait("sync", st.get(f"a1x{xi}"))
                    xn = E.op("sync", lambda e, t0=t0, qs=qs, xi=xi: e.dma_start(
                        out=x_sb[xi][:, :], in_=x_h[t0 + qs * 128:t0 + qs * 128 + 128, :]),
                        inc=f"XA{xi}", k=16)
                    E.wait("vector", st.get(f"a1h{xi}"))
                    apf = ln_chain(x_sb[xi][:, :], h_sb[xi][:, :], D, epsln, s4, "sA1",
                                   src_f=(f"XA{xi}", xn))
                    st[f"a1x{xi}"] = apf
                    E.wait("tensor", apf)
                    E.wait("tensor", st.get(f"a1pt{qs % 2}"))
                    for c in range(16):
                        E.op("tensor", lambda e, c=c, qs=qs, xi=xi: e.transpose(
                            pT[qs % 2][:, c * 128:(c + 1) * 128],
                            h_sb[xi][:, c * 128:(c + 1) * 128], id_sb[:, :]),
                            inc=("TE" if c == 15 else None))
                    tn = E.cnt("TE")
                    st[f"a1h{xi}"] = ("TE", tn)
                    E.wait("vector", ("TE", tn))
                    E.wait("vector", st.get(f"a1hT{qs}"))
                    E.wait("vector", st.get(f"a1hTs{qs}"))
                    cp = E.op("vector", lambda e, qs=qs: e.tensor_copy(
                        hT_sb[:, :, qs * 128:qs * 128 + 128],
                        pT[qs % 2][:, :]), inc="VE")
                    st[f"a1pt{qs % 2}"] = ("VE", cp)
                    E.wait("sync", ("VE", cp))
                    shn = E.op("sync", lambda e, t0=t0, qs=qs: e.dma_start(
                        out=r128(s_h_h)[:, :, t0 + qs * 128:t0 + qs * 128 + 128],
                        in_=hT_sb[:, :, qs * 128:qs * 128 + 128]), inc=f"SH{qs}", k=16)
                    st[f"a1hTs{qs}"] = (f"SH{qs}", shn)

                    for fo in range(3):
                        pi = (qs * 3 + fo) % 2
                        E.wait("tensor", st.get(f"a1pa{pi}"))
                        E.wait("tensor", ("VE", cp))
                        for kc in range(16):
                            E.op("tensor", lambda e, kc=kc, fo=fo, qs=qs, pi=pi: e.matmul(
                                pA[pi][:, :], hT_sb[:, kc, qs * 128:qs * 128 + 128],
                                wqd_sb[:, kc, fo * 512:fo * 512 + 512],
                                start=(kc == 0), stop=(kc == 15)),
                                inc=("TE" if kc == 15 else None))
                        mmn = E.cnt("TE")
                        E.wait("vector", ("TE", mmn))
                        if fo == 0:
                            E.wait("vector", st.get("a1qcr"))
                        ev = E.op("vector", lambda e, fo=fo, pi=pi: e.tensor_copy(
                            qc_sb[:, fo * 512:fo * 512 + 512], pA[pi][:, :]), inc="VE")
                        st[f"a1pa{pi}"] = ("VE", ev)
                        E.wait("scalar", ("VE", ev))
                        E.wait("scalar", st.get("a1ssr"))
                        E.sbar("scalar", "AE")
                        aq = E.op("scalar", lambda e, fo=fo: e.activation(
                            junk[:, 0:512], qc_sb[:, fo * 512:fo * 512 + 512], AF.Square,
                            accum_out=ss[:, fo:fo + 1]), inc="AE")
                    E.wait("vector", st.get("a1qcn"))
                    apf = ln_chain(qc_sb[:, :], qcn_sb[:, :], QR, epsrm, s4, "sA1",
                                   rms=True, ss_ready=("AE", aq), nss=3)
                    st["a1qcr"] = ("AE", aq)
                    st["a1ssr"] = ("VE", E.cnt("VE"))
                    E.wait("tensor", apf)
                    E.wait("tensor", st.get(f"a1pt{qs % 2}"))
                    for c in range(12):
                        E.op("tensor", lambda e, c=c, qs=qs: e.transpose(
                            pT[qs % 2][:, c * 128:(c + 1) * 128],
                            qcn_sb[:, c * 128:(c + 1) * 128], id_sb[:, :]),
                            inc=("TE" if c == 11 else None))
                    tn = E.cnt("TE")
                    st["a1qcn"] = ("TE", tn)
                    E.wait("vector", ("TE", tn))
                    E.wait("vector", st.get("a1qcT"))
                    for kc in range(12):
                        cp2 = E.op("vector", lambda e, qs=qs, kc=kc: e.tensor_scalar_mul(
                            qcT_sb[:, kc, qs * 128:qs * 128 + 128],
                            pT[qs % 2][:, kc * 128:kc * 128 + 128],
                            qnw_sb[:, kc:kc + 1]), inc="VE")
                    st[f"a1pt{qs % 2}"] = ("VE", cp2)
                    st[f"a1hT{qs}"] = ("TE", E.cnt("TE"))
                qcT_rdy = ("VE", E.cnt("VE"))
                E.wait("tensor", qcT_rdy)
                wq_slots = {}
                wq_slots[0] = load_wqu(t, 0)
                for oc in range(16):
                    g = oc // 4
                    if oc % 4 == 0 and g + 1 < 4:
                        wq_slots[g + 1] = load_wqu(t, g + 1)
                    sl_w, af_w = wq_slots[g]
                    pi = oc % 2
                    E.wait("tensor", st.get(f"a1pa{pi}"))
                    E.wait("tensor", af_w)
                    for kc in range(12):
                        E.op("tensor", lambda e, kc=kc, oc=oc, pi=pi, sl_w=sl_w: e.matmul(
                            pA[pi][:, :], wqu_s[sl_w][:, kc, (oc % 4) * 128:(oc % 4) * 128 + 128],
                            qcT_sb[:, kc, :], start=(kc == 0), stop=(kc == 11)),
                            inc=("TE" if kc == 11 else None))
                    mmn = E.cnt("TE")
                    if oc % 4 == 3:
                        st[f"wqu{sl_w}"] = ("TE", mmn)
                    E.wait("vector", ("TE", mmn))
                    sl, bi = oc % 4, (oc // 4) % 2
                    if sl == 0:
                        E.wait("vector", st.get(f"a1qn{bi}"))
                    ev = E.op("vector", lambda e, sl=sl, bi=bi, pi=pi: e.tensor_copy(
                        qn_st[bi][:, sl, :], pA[pi][:, :]), inc="VE")
                    st[f"a1pa{pi}"] = ("VE", ev)
                    if sl == 3:
                        E.wait("sync", ("VE", ev))
                        stn = E.op("sync", lambda e, t0=t0, oc=oc, bi=bi: e.dma_start(
                            out=r128(s_qn_h)[:, oc - 3:oc + 1, t0:t0 + 512],
                            in_=qn_st[bi][:, :, :]), inc=f"SA{bi}", k=16)
                        st[f"a1qn{bi}"] = (f"SA{bi}", stn)
                E.wait("vector", st.get("a1ropeV"))
                for oc in range(8):
                    pi = oc % 2
                    E.wait("tensor", st.get(f"a1pa{pi}"))
                    for kc in range(12):
                        E.op("tensor", lambda e, kc=kc, oc=oc, pi=pi: e.matmul(
                            pA[pi][:, :], wqr_sb[:, kc, oc * 128:oc * 128 + 128],
                            qcT_sb[:, kc, :], start=(kc == 0), stop=(kc == 11)),
                            inc=("TE" if kc == 11 else None))
                    mmn = E.cnt("TE")
                    E.wait("vector", ("TE", mmn))
                    ev = E.op("vector", lambda e, oc=oc, pi=pi: e.tensor_copy(
                        qpe_sb[:, oc, :], pA[pi][:, :]), inc="VE")
                    st[f"a1pa{pi}"] = ("VE", ev)
                st["a1qcT"] = ("TE", E.cnt("TE"))
                E.wait("vector", ("RT1", rs_n))
                qp_g = s_qp_h.rearrange("(c q r) t -> c q r t", c=4, q=4, r=64)
                for c in range(4):
                    E.wait("vector", st.get("a1roS"))
                    E.op("vector", lambda e, c=c: e.tensor_tensor(rt1[:, :], qpe_sb[:, c, :],
                         rc_sb[:, :], ALU.mult), inc="VE")
                    E.op("vector", lambda e, c=c: e.tensor_tensor(rt2[:, :], qpe_sb[:, c + 4, :],
                         rs_sb[:, :], ALU.mult), inc="VE")
                    E.sbar("vector", "VE")
                    E.op("vector", lambda e: e.tensor_tensor(ro_e[:, :], rt1[:, :], rt2[:, :],
                         ALU.subtract), inc="VE")
                    E.sbar("vector", "VE")
                    E.op("vector", lambda e, c=c: e.tensor_tensor(rt1[:, :], qpe_sb[:, c, :],
                         rs_sb[:, :], ALU.mult), inc="VE")
                    E.op("vector", lambda e, c=c: e.tensor_tensor(rt2[:, :], qpe_sb[:, c + 4, :],
                         rc_sb[:, :], ALU.mult), inc="VE")
                    E.sbar("vector", "VE")
                    rov = E.op("vector", lambda e: e.tensor_tensor(ro_o[:, :], rt1[:, :], rt2[:, :],
                               ALU.add), inc="VE")
                    E.wait("sync", ("VE", rov))
                    E.op("sync", lambda e, c=c, t0=t0: e.dma_start(
                        out=qp_g[c, :, 0:32, t0:t0 + 512], in_=ro_e[:, :]), inc="RO", k=16)
                    ron = E.op("sync", lambda e, c=c, t0=t0: e.dma_start(
                        out=qp_g[c, :, 32:64, t0:t0 + 512], in_=ro_o[:, :]), inc="RO", k=16)
                    st["a1roS"] = ("RO", ron)
                st["a1ropeV"] = ("VE", E.cnt("VE"))
                st["a1rope"] = ("VE", E.cnt("VE"))
            drip_conv(100)
            A1_f = {"sync": ("RO", E.cnt("RO")), "tensor": ("TE", E.cnt("TE")),
                    "vector": ("VE", E.cnt("VE")), "scalar": ("AE", E.cnt("AE")),
                    "gpsimd": ("LW1", E.cnt("LW1"))}
            A1_extra = [(f"SH{q}", E.cnt(f"SH{q}")) for q in range(4)] + \
                       [("SA0", E.cnt("SA0")), ("SA1", E.cnt("SA1"))]
        E.barrier(A1_f)
        for f in A1_extra:
            for eng in Emit.ENGINES:
                E.wait(eng, f)

        with contextlib.ExitStack() as pst:
            psb = lambda name, shape, dt: pst.enter_context(nc.sbuf_tensor(name, shape, dt))
            pps = lambda name, shape, dt: pst.enter_context(nc.psum_tensor(name, shape, dt))
            wkd_sb = psb("wkd_sb", [128, 16, KVR], BF16)
            wku_sb = psb("wku_sb", [128, 4, NH * 256], BF16)
            wkr_sb = psb("wkr_sb", [128, 16, DR], BF16)
            kvw_sb = psb("kvw_sb", [128, 4], F32)
            hT2 = [psb(f"hT2_{i}", [128, 16, 512], BF16) for i in range(2)]
            kvc_sb = psb("kvc_sb", [128, KVR], F32)
            kvn_sb = psb("kvn_sb", [128, KVR], BF16)
            kvT_sb = psb("kvT_sb", [128, 4, 512], BF16)
            kn_st = [psb(f"kn_st{i}", [128, 4, 512], BF16) for i in range(2)]
            v_bf = psb("v_bf", [128, 512], BF16)
            v_st = psb("v_st", [128, 4, NH * DN], BF16)
            rt1b = psb("rt1b", [32, 512], F32)
            rt2b = psb("rt2b", [32, 512], F32)
            rc2 = psb("rc2", [128, 512], F32)
            rs2 = psb("rs2", [128, 512], F32)
            sums2 = psb("sums2", [128, 1], F32)
            ss2 = psb("ss2", [128, 1], F32)
            nmu2 = psb("nmu2", [128, 1], F32)
            rst2 = psb("rst2", [128, 1], F32)
            s42 = (sums2[:, 0:1], ss2, nmu2[:, 0:1], rst2[:, 0:1])
            pA2 = [pps(f"pA2_{i}", [128, 512], F32) for i in range(2)]
            pT2 = [pps(f"pT2_{i}", [128, 2048], BF16) for i in range(2)]
            pV2 = pps("pV2", [128, 512], BF16)
            pK2 = pps("pK2", [128, 512], F32)

            E.op("gpsimd", lambda e: e.dma_start(out=wkd_sb[:, :, :], in_=r128(wkd_h)),
                 inc="LW2", k=16)
            E.op("gpsimd", lambda e: e.dma_start(out=wku_sb[:, :, :], in_=r128(wku_h)),
                 inc="LW2", k=16)
            E.op("gpsimd", lambda e: e.dma_start(out=wkr_sb[:, :, :], in_=r128(wkr_h)),
                 inc="LW2", k=16)
            E.op("gpsimd", lambda e: e.dma_start(out=kvw_sb[:, :], in_=kvw_h[:, :]),
                 inc="LW2", k=16)
            wA2 = ("LW2", E.cnt("LW2"))
            E.wait("vector", wA2)
            for kc in range(4):
                E.op("vector", lambda e, kc=kc: e.tensor_scalar_mul(
                    wku_sb[:, kc, :], wku_sb[:, kc, :], kvw_sb[:, kc:kc + 1]), inc="VE")
            foldkv = ("VE", E.cnt("VE"))
            E.wait("tensor", foldkv)

            for t in range(NT):
                t0 = t * 512
                ti = t % 2
                E.wait("sync", st.get("a2rope"))
                rs_n = None
                E.op("sync", lambda e, t0=t0: e.dma_start(out=rc2[:, :],
                     in_=rc_h[:, t0:t0 + 512]), inc="RT2", k=16)
                rs_n = E.op("sync", lambda e, t0=t0: e.dma_start(out=rs2[:, :],
                     in_=rs_h[:, t0:t0 + 512]), inc="RT2", k=16)
                E.wait("sync", st.get(f"a2h{ti}"))
                hn = E.op("sync", lambda e, t0=t0, ti=ti: e.dma_start(
                    out=hT2[ti][:, :, :], in_=r128(s_h_h)[:, :, t0:t0 + 512]),
                    inc=f"XB{ti}", k=16)
                for qs in range(4):
                    pi = qs % 2
                    E.wait("tensor", st.get(f"a2pa{pi}"))
                    E.wait("tensor", (f"XB{ti}", hn))
                    for kc in range(16):
                        E.op("tensor", lambda e, kc=kc, qs=qs, pi=pi, ti=ti: e.matmul(
                            pA2[pi][:, :], hT2[ti][:, kc, qs * 128:qs * 128 + 128],
                            wkd_sb[:, kc, :], start=(kc == 0), stop=(kc == 15)),
                            inc=("TE" if kc == 15 else None))
                    mmn = E.cnt("TE")
                    E.wait("vector", ("TE", mmn))
                    E.wait("vector", st.get("a2kvr"))
                    ev = E.op("vector", lambda e, pi=pi: e.tensor_copy(kvc_sb[:, :],
                              pA2[pi][:, :]), inc="VE")
                    st[f"a2pa{pi}"] = ("VE", ev)
                    E.wait("vector", st.get("a2kvn"))
                    apf = ln_chain(kvc_sb[:, :], kvn_sb[:, :], KVR, epsrm, s42, "sA2",
                                   src_f=("VE", ev), rms=True)
                    st["a2kvr"] = ("AE", E.cnt("AE"))
                    E.wait("tensor", apf)
                    E.wait("tensor", st.get(f"a2pt{pi}"))
                    for c in range(4):
                        E.op("tensor", lambda e, c=c, pi=pi: e.transpose(
                            pT2[pi][:, c * 128:(c + 1) * 128],
                            kvn_sb[:, c * 128:(c + 1) * 128], id_sb[:, :]),
                            inc=("TE" if c == 3 else None))
                    tn = E.cnt("TE")
                    st["a2kvn"] = ("TE", tn)
                    E.wait("vector", ("TE", tn))
                    E.wait("vector", st.get("a2kvT"))
                    cp = E.op("vector", lambda e, qs=qs, pi=pi: e.tensor_copy(
                        kvT_sb[:, :, qs * 128:qs * 128 + 128],
                        pT2[pi][:, 0:512]), inc="VE")
                    st[f"a2pt{pi}"] = ("VE", cp)
                kvT_rdy = ("VE", E.cnt("VE"))
                E.wait("tensor", kvT_rdy)
                for oc in range(32):
                    pi = oc % 2
                    E.wait("tensor", st.get(f"a2pa{pi}"))
                    for kc in range(4):
                        E.op("tensor", lambda e, kc=kc, oc=oc, pi=pi: e.matmul(
                            pA2[pi][:, :], wku_sb[:, kc, oc * 128:oc * 128 + 128],
                            kvT_sb[:, kc, :], start=(kc == 0), stop=(kc == 3)),
                            inc=("TE" if kc == 3 else None))
                    mmn = E.cnt("TE")
                    h_ = oc // 2
                    E.wait("vector", ("TE", mmn))
                    if oc % 2 == 0:
                        sl, bi = h_ % 4, (h_ // 4) % 2
                        if sl == 0:
                            E.wait("vector", st.get(f"a2kn{bi}"))
                        ev = E.op("vector", lambda e, sl=sl, bi=bi, pi=pi: e.tensor_copy(
                            kn_st[bi][:, sl, :], pA2[pi][:, :]), inc="VE")
                        st[f"a2pa{pi}"] = ("VE", ev)
                        if sl == 3:
                            E.wait("sync", ("VE", ev))
                            stn = E.op("sync", lambda e, t0=t0, h_=h_, bi=bi: e.dma_start(
                                out=r128(s_kn_h)[:, h_ - 3:h_ + 1, t0:t0 + 512],
                                in_=kn_st[bi][:, :, :]), inc=f"SK{bi}", k=16)
                            st[f"a2kn{bi}"] = (f"SK{bi}", stn)
                    else:
                        E.wait("vector", st.get("a2vbf"))
                        ev = E.op("vector", lambda e, pi=pi: e.tensor_copy(
                            v_bf[:, :], pA2[pi][:, :]), inc="VE")
                        st[f"a2pa{pi}"] = ("VE", ev)
                        E.wait("tensor", ("VE", ev))
                        E.wait("tensor", st.get("a2pv"))
                        for ts_ in range(4):
                            E.op("tensor", lambda e, ts_=ts_: e.transpose(
                                pV2[:, ts_ * 128:ts_ * 128 + 128],
                                v_bf[:, ts_ * 128:ts_ * 128 + 128], id_sb[:, :]),
                                inc=("TE" if ts_ == 3 else None))
                        tn = E.cnt("TE")
                        st["a2vbf"] = ("TE", tn)
                        E.wait("vector", ("TE", tn))
                        if h_ == 0:
                            E.wait("vector", st.get("a2vst"))
                        cpv = E.op("vector", lambda e, h_=h_: e.tensor_copy(
                            v_st[:, :, h_ * 128:h_ * 128 + 128],
                            pV2[:, :]), inc="VE")
                        st["a2pv"] = ("VE", cpv)
                        if h_ == 15:
                            E.wait("sync", ("VE", cpv))
                            stn = E.op("sync", lambda e, t0=t0: e.dma_start(
                                out=s_v_h.rearrange("(s p) d -> p s d", p=128)[:, t0 // 128:t0 // 128 + 4, :],
                                in_=v_st[:, :, :]), inc="SV", k=16)
                            st["a2vst"] = ("SV", stn)
                st["a2kvT"] = ("TE", E.cnt("TE"))
                E.wait("tensor", st.get("a2krP"))
                for kc in range(16):
                    E.op("tensor", lambda e, kc=kc, ti=ti: e.matmul(
                        pK2[0:64, :], wkr_sb[:, kc, :], hT2[ti][:, kc, :],
                        start=(kc == 0), stop=(kc == 15)), inc=("TE" if kc == 15 else None))
                mmn = E.cnt("TE")
                st[f"a2h{ti}"] = ("TE", mmn)
                E.wait("vector", ("TE", mmn))
                E.wait("vector", ("RT2", rs_n))
                E.op("vector", lambda e: e.tensor_tensor(rt1b[:, :], pK2[0:32, :], rc2[0:32, :],
                     ALU.mult), inc="VE")
                E.op("vector", lambda e: e.tensor_tensor(rt2b[:, :], pK2[32:64, :], rs2[0:32, :],
                     ALU.mult), inc="VE")
                E.sbar("vector", "VE")
                E.op("vector", lambda e, t0=t0: e.tensor_tensor(krope[0:32, t0:t0 + 512],
                     rt1b[:, :], rt2b[:, :], ALU.subtract), inc="VE")
                E.sbar("vector", "VE")
                E.op("vector", lambda e: e.tensor_tensor(rt1b[:, :], pK2[0:32, :], rs2[0:32, :],
                     ALU.mult), inc="VE")
                E.op("vector", lambda e: e.tensor_tensor(rt2b[:, :], pK2[32:64, :], rc2[0:32, :],
                     ALU.mult), inc="VE")
                E.sbar("vector", "VE")
                krn = E.op("vector", lambda e, t0=t0: e.tensor_tensor(krope[32:64, t0:t0 + 512],
                           rt1b[:, :], rt2b[:, :], ALU.add), inc="VE")
                st["a2krP"] = ("VE", krn)
                st["a2rope"] = ("VE", krn)
            A2_f = {"sync": ("SV", E.cnt("SV")), "tensor": ("TE", E.cnt("TE")),
                    "vector": ("VE", E.cnt("VE")), "scalar": ("AE", E.cnt("AE")),
                    "gpsimd": ("LW2", E.cnt("LW2"))}
            A2_extra = [("SK0", E.cnt("SK0")), ("SK1", E.cnt("SK1"))]
        E.barrier(A2_f)
        for f in A2_extra:
            for eng in Emit.ENGINES:
                E.wait(eng, f)

        with contextlib.ExitStack() as pst:
            psb = lambda name, shape, dt: pst.enter_context(nc.sbuf_tensor(name, shape, dt))
            pps = lambda name, shape, dt: pst.enter_context(nc.psum_tensor(name, shape, dt))
            qn_sb = psb("qn_sb", [128, 16, 512], BF16)
            qp_sb = psb("qp_sb", [64, 16, 512], BF16)
            kn_g = [psb(f"kn_g{i}", [128, 16, 512], BF16) for i in range(3)]
            v_g = [psb(f"v_g{i}", [128, 4, NH * DN], BF16) for i in range(3)]
            probs = [psb(f"probs{i}", [128, 512], BF16) for i in range(2)]
            probsT = [psb(f"probsT{i}", [128, 512], BF16) for i in range(2)]
            Oacc = psb("Oacc", [128, 64 * 128], F32)
            l_sb = psb("l_sb", [128, 64], F32)
            rl_sb = psb("rl_sb", [128, 64], F32)
            cs_sb = psb("cs_sb", [128, 2], F32)
            obf = psb("obf", [128, 128], BF16)
            att_st = psb("att_st", [128, 16, 512], BF16)
            pS = [pps(f"pS{i}", [128, 512], F32) for i in range(2)]
            pP = [pps(f"pP{i}", [128, 512], BF16) for i in range(2)]
            pO = [pps(f"pO{i}", [128, 512], F32) for i in range(2)]

            inst = 0
            pending = [None]
            E.wait("scalar", P0_GE)

            def stage_b(p):
                if p is None:
                    return
                (si, W, nsub, gr0, gr1, h, qs, col, first, ex, rels) = p
                E.wait("tensor", ("AE", ex))
                E.wait("tensor", st.get(f"bpp{si}"))
                for j in range(nsub):
                    E.op("tensor", lambda e, si=si, j=j: e.transpose(
                        pP[si][:, j * 128:j * 128 + 128],
                        probs[si][:, j * 128:j * 128 + 128], id_sb[:, :]),
                        inc=("TE" if j == nsub - 1 else None))
                ptn = E.cnt("TE")
                st[f"bprb{si}"] = ("TE", ptn)
                if si == 0:
                    E.wait("vector", ("TE", ptn))
                    E.wait("vector", st.get(f"bptT{si}"))
                    pc = E.op("vector", lambda e, si=si, W=W: e.tensor_copy(
                        probsT[si][:, 0:W], pP[si][:, 0:W]), inc="VE")
                    pcf = ("VE", pc)
                else:
                    E.wait("scalar", ("TE", ptn))
                    E.wait("scalar", st.get(f"bptT{si}"))
                    pc = E.op("scalar", lambda e, si=si, W=W: e.copy(
                        probsT[si][:, 0:W], pP[si][:, 0:W]), inc="AE")
                    pcf = ("AE", pc)
                st[f"bpp{si}"] = pcf
                E.wait("tensor", pcf)
                E.wait("tensor", st.get(f"bpo{si}"))
                for j in range(nsub):
                    g = gr0 if j < 4 else gr1
                    gi = g % 3
                    subl = j % 4
                    E.op("tensor", lambda e, si=si, j=j, gi=gi, subl=subl, h=h, nsub=nsub: e.matmul(
                        pO[si][:, 0:128], probsT[si][:, j * 128:j * 128 + 128],
                        v_g[gi][:, subl, h * 128:h * 128 + 128],
                        start=(j == 0), stop=(j == nsub - 1)),
                        inc=("TE" if j == nsub - 1 else None))
                on = E.cnt("TE")
                st[f"bptT{si}"] = ("TE", on)
                for key in rels:
                    st[key] = ("TE", on)
                E.wait("vector", ("TE", on))
                if first:
                    ov = E.op("vector", lambda e, col=col, si=si: e.tensor_copy(
                        Oacc[:, col * 128:col * 128 + 128], pO[si][:, 0:128]),
                        inc="VE")
                else:
                    ov = E.op("vector", lambda e, col=col, si=si: e.tensor_tensor(
                        Oacc[:, col * 128:col * 128 + 128],
                        Oacc[:, col * 128:col * 128 + 128], pO[si][:, 0:128],
                        ALU.add), inc="VE")
                st[f"bpo{si}"] = ("VE", ov)

            for qt in range(min(NT, B_MAX_QT)):
                t0 = qt * 512
                E.wait("sync", st.get("bq_rel"))
                E.op("sync", lambda e, t0=t0: e.dma_start(out=qn_sb[:, :, :],
                     in_=r128(s_qn_h)[:, :, t0:t0 + 512]), inc="BQ", k=16)
                qld = E.op("sync", lambda e, t0=t0: e.dma_start(out=qp_sb[:, :, :],
                     in_=r64(s_qp_h)[:, :, t0:t0 + 512]), inc="BQ", k=16)
                ngr = qt + 1

                def load_gran(g):
                    gi = g % 3
                    E.wait("sync", st.get(f"bg{gi}"))
                    E.op("sync", lambda e, g=g, gi=gi: e.dma_start(out=kn_g[gi][:, :, :],
                         in_=r128(s_kn_h)[:, :, g * 512:g * 512 + 512]), inc=f"BK{gi}", k=16)
                    n = E.op("sync", lambda e, g=g, gi=gi: e.dma_start(out=v_g[gi][:, :, :],
                         in_=s_v_h.rearrange("(s p) d -> p s d", p=128)[:, g * 4:g * 4 + 4, :]),
                         inc=f"BK{gi}", k=16)
                    st[f"bgr{gi}"] = (f"BK{gi}", n)

                load_gran(0)
                for c in range(ngr):
                    stage_b(pending[0])
                    pending[0] = None
                    if c + 1 < ngr:
                        load_gran(c + 1)
                    k0 = c * 512
                    gci = c % 3
                    rdy0 = st[f"bgr{gci}"]
                    for h in range(16):
                        for qs in range(4):
                            limit = t0 + qs * 128 + 128
                            if k0 >= limit:
                                continue
                            W = min(512, limit - k0)
                            si = inst % 2
                            first = (c == 0)
                            E.wait("tensor", ("BQ", qld))
                            E.wait("tensor", rdy0)
                            E.wait("tensor", st.get(f"bexp{si}"))
                            E.op("tensor", lambda e, h=h, qs=qs, si=si, gci=gci: e.matmul(
                                pS[si][:, :], qn_sb[:, h, qs * 128:qs * 128 + 128],
                                kn_g[gci][:, h, :], start=True, stop=False), inc=None)
                            E.op("tensor", lambda e, h=h, qs=qs, si=si, c=c: e.matmul(
                                pS[si][:, :], qp_sb[:, h, qs * 128:qs * 128 + 128],
                                krope[:, c * 512:c * 512 + 512], start=False, stop=True),
                                inc="TE")
                            smm = E.cnt("TE")
                            diag = (k0 + W == limit)
                            E.wait("vector", ("TE", smm))
                            if diag:
                                moff = W - 128
                                E.op("vector", lambda e, si=si, moff=moff: e.tensor_tensor(
                                    pS[si][:, moff:moff + 128], pS[si][:, moff:moff + 128],
                                    tm_sb[:, :], ALU.add), inc="VE")
                            msk = ("VE", E.cnt("VE"))
                            E.wait("scalar", ("TE", smm))
                            if diag:
                                E.wait("scalar", msk)
                            E.wait("scalar", st.get(f"bcs{si}"))
                            E.wait("scalar", st.get(f"bprb{si}"))
                            ex = E.op("scalar", lambda e, si=si, W=W: e.activation(
                                probs[si][:, 0:W], pS[si][:, 0:W], AF.Exp,
                                bias=mshift[:, 0:1], scale=SCALE,
                                accum_out=cs_sb[:, si:si + 1]), inc="AE")
                            st[f"bexp{si}"] = ("AE", ex)
                            col = h * 4 + qs
                            E.wait("vector", ("AE", ex))
                            if first:
                                lv = E.op("vector", lambda e, col=col, si=si: e.tensor_copy(
                                    l_sb[:, col:col + 1], cs_sb[:, si:si + 1]), inc="VE")
                            else:
                                lv = E.op("vector", lambda e, col=col, si=si: e.tensor_tensor(
                                    l_sb[:, col:col + 1], l_sb[:, col:col + 1],
                                    cs_sb[:, si:si + 1], ALU.add), inc="VE")
                            st[f"bcs{si}"] = ("VE", lv)
                            nsub = (W + 127) // 128
                            rels = [f"bg{gci}"]
                            if c == ngr - 1:
                                rels.append("bq_rel")
                            stage_b(pending[0])
                            pending[0] = (si, W, nsub, c, c, h, qs, col, first, ex, rels)
                            inst += 1
                stage_b(pending[0])
                pending[0] = None
                if not B_FIN:
                    continue
                E.sbar("vector", "VE")
                E.op("vector", lambda e: e.reciprocal(rl_sb[:, :], l_sb[:, :]), inc="VE")
                E.sbar("vector", "VE")
                E.wait("vector", st.get("bat_rel"))
                for h in range(16):
                    for qs in range(4):
                        col = h * 4 + qs
                        E.wait("vector", st.get("bobf"))
                        ob = E.op("vector", lambda e, col=col: e.tensor_scalar_mul(
                            obf[:, :], Oacc[:, col * 128:col * 128 + 128],
                            rl_sb[:, col:col + 1]), inc="VE")
                        E.wait("tensor", ("VE", ob))
                        E.wait("tensor", st.get("bfinc"))
                        E.wait("tensor", st.get("bpp0"))
                        E.op("tensor", lambda e, qs=qs: e.transpose(
                            pP[0][:, qs * 128:qs * 128 + 128], obf[:, :], id_sb[:, :]),
                            inc="TE")
                        tn = E.cnt("TE")
                        st["bobf"] = ("TE", tn)
                        E.wait("vector", ("TE", tn))
                        cpa = E.op("vector", lambda e, h=h, qs=qs: e.tensor_copy(
                            att_st[:, h, qs * 128:qs * 128 + 128],
                            pP[0][:, qs * 128:qs * 128 + 128]), inc="VE")
                        st["bfinc"] = ("VE", cpa)
                        st["bpp0"] = ("VE", cpa)
                fin = ("VE", E.cnt("VE"))
                E.wait("sync", fin)
                atn = E.op("sync", lambda e, t0=t0: e.dma_start(
                    out=r128(s_at_h)[:, :, t0:t0 + 512], in_=att_st[:, :, :]), inc="SAT", k=16)
                st["bat_rel"] = ("SAT", atn)
            B_f = {"sync": ("SAT", E.cnt("SAT")), "tensor": ("TE", E.cnt("TE")),
                   "vector": ("VE", E.cnt("VE")), "scalar": ("AE", E.cnt("AE"))}
        E.barrier(B_f)

        with contextlib.ExitStack() as pst:
            psb = lambda name, shape, dt: pst.enter_context(nc.sbuf_tensor(name, shape, dt))
            pps = lambda name, shape, dt: pst.enter_context(nc.psum_tensor(name, shape, dt))
            arena = [psb(f"arena{i}", [128, 16, 512], BF16) for i in range(3)]
            xc_sb = psb("xc_sbC", [128, 4, D], F32)
            yz_sb = psb("yz_sb", [128, 4, D], F32)
            gT_sb = psb("gT_sb", [128, 32, 512], BF16)
            h2_bf = psb("h2_bf", [128, D], BF16)
            h2T_sb = psb("h2T_sb", [128, 16, 512], BF16)
            sg_sb = [psb(f"sg_sb{i}", [128, 512], BF16) for i in range(2)]
            t_scr = psb("t_scr", [128, D], F32)
            out_sb = psb("out_sb", [128, D], F32)
            sums3 = psb("sums3", [128, 1], F32)
            ss3 = psb("ss3", [128, 1], F32)
            nmu3 = psb("nmu3", [128, 1], F32)
            rst3 = psb("rst3", [128, 1], F32)
            s43 = (sums3[:, 0:1], ss3, nmu3[:, 0:1], rst3[:, 0:1])
            pG = [pps(f"pG{i}", [128, 512], F32) for i in range(2)]
            pU = [pps(f"pU{i}", [128, 512], F32) for i in range(2)]
            pD = [pps(f"pD{i}", [128, 512], F32) for i in range(2)]
            pH = pps("pH", [128, 2048], BF16)

            E.wait("sync", ("CVT", E.cnt("CVT")))
            arena_i = 0

            def arena_load(src_ap, pin=False):
                nonlocal arena_i
                if pin:
                    sl = 2
                else:
                    sl = arena_i % 2
                    arena_i += 1
                E.wait("sync", st.get(f"car{sl}"))
                n = E.op("sync", lambda e, sl=sl, src_ap=src_ap: e.dma_start(
                    out=arena[sl][:, :, :], in_=src_ap), inc=f"CA{sl}", k=16)
                return sl, (f"CA{sl}", n)

            for t in range(min(NT, C_MAX_T)):
                t0 = t * 512
                ti = t % 2
                sl_a, af_a = arena_load(r128(s_at_h)[:, :, t0:t0 + 512], pin=True)
                E.wait("sync", st.get("cx_rel"))
                for qs in range(4):
                    xn = E.op("sync", lambda e, t0=t0, qs=qs: e.dma_start(
                        out=xc_sb[:, qs, :], in_=x_h[t0 + qs * 128:t0 + qs * 128 + 128, :]),
                        inc=f"CX{ti}", k=16)
                cx_rdy = (f"CX{ti}", xn)
                for fo in range(4):
                    sl, af = arena_load(r128(wo_b_h)[:, :, fo * 512:fo * 512 + 512])
                    E.wait("tensor", af)
                    E.wait("tensor", af_a)
                    for qs in range(4):
                        pi = (fo * 4 + qs) % 2
                        E.wait("tensor", st.get(f"cpg{pi}"))
                        for kc in range(16):
                            E.op("tensor", lambda e, kc=kc, qs=qs, sl=sl, pi=pi, sl_a=sl_a: e.matmul(
                                pG[pi][:, :], arena[sl_a][:, kc, qs * 128:qs * 128 + 128],
                                arena[sl][:, kc, :], start=(kc == 0), stop=(kc == 15)),
                                inc=("TE" if kc == 15 else None))
                        mmn = E.cnt("TE")
                        if qs == 3:
                            st[f"car{sl}"] = ("TE", mmn)
                            if fo == 3:
                                st[f"car{sl_a}"] = ("TE", mmn)
                        E.wait("vector", ("TE", mmn))
                        E.wait("vector", st.get("cyz_w"))
                        ev = E.op("vector", lambda e, qs=qs, fo=fo, pi=pi: e.tensor_copy(
                            yz_sb[:, qs, fo * 512:fo * 512 + 512], pG[pi][:, :]), inc="VE")
                        st[f"cpg{pi}"] = ("VE", ev)
                yrdy = ("VE", E.cnt("VE"))
                if dbg:
                    E.wait("sync", yrdy)
                    E.op("sync", lambda e, t0=t0: e.dma_start(
                        out=d_wo_h.rearrange("(a p) d -> p a d", p=128)[:, t0 // 128:t0 // 128 + 4, :],
                        in_=yz_sb[:, :, :]), inc="DBG", k=16)
                for qs in range(4):
                    E.wait("vector", st.get("ct_rel"))
                    apf = ln_chain(yz_sb[:, qs, :], t_scr[:, :], D, epsln, s43, "lnC",
                                   src_f=(yrdy if qs == 0 else None))
                    E.wait("gpsimd", apf)
                    E.wait("gpsimd", cx_rdy)
                    gadd = E.op("gpsimd", lambda e, qs=qs: e.tensor_tensor(
                        xc_sb[:, qs, :], xc_sb[:, qs, :], t_scr[:, :], ALU.add), inc="GE")
                    st["ct_rel"] = ("GE", gadd)
                    E.wait("vector", ("GE", gadd))
                    E.wait("scalar", ("GE", gadd))
                    E.wait("vector", st.get("ch2"))
                    apf2 = ln_chain(xc_sb[:, qs, :], h2_bf[:, :],
                                    D, epsln, s43, "lnC", src_f=None)
                    E.wait("tensor", apf2)
                    E.wait("tensor", st.get("cpH"))
                    for cc in range(16):
                        E.op("tensor", lambda e, cc=cc: e.transpose(
                            pH[:, cc * 128:(cc + 1) * 128], h2_bf[:, cc * 128:(cc + 1) * 128],
                            id_sb[:, :]), inc=("TE" if cc == 15 else None))
                    tn = E.cnt("TE")
                    st["ch2"] = ("TE", tn)
                    E.wait("vector", ("TE", tn))
                    E.wait("vector", st.get("ch2T"))
                    cph = E.op("vector", lambda e, qs=qs: e.tensor_copy(
                        h2T_sb[:, :, qs * 128:qs * 128 + 128],
                        pH[:, :]), inc="VE")
                    st["cpH"] = ("VE", cph)
                st["cyz_w2"] = ("VE", E.cnt("VE"))
                if dbg:
                    E.wait("sync", st.get("ct_rel"))
                    E.op("sync", lambda e, t0=t0: e.dma_start(
                        out=d_x2_h.rearrange("(a p) d -> p a d", p=128)[:, t0 // 128:t0 // 128 + 4, :],
                        in_=xc_sb[:, :, :]), inc="DBG", k=16)
                h2T_rdy = ("VE", E.cnt("VE"))
                E.wait("tensor", h2T_rdy)
                for half in range(2):
                    hb = half * 32
                    for hcg in range(8):
                        slg, afg = arena_load(
                            r128(wg_b_h)[:, :, (hb + hcg * 4) * 128:(hb + hcg * 4) * 128 + 512])
                        slu, afu = arena_load(
                            r128(wu_b_h)[:, :, (hb + hcg * 4) * 128:(hb + hcg * 4) * 128 + 512])
                        for hl in range(4):
                            hc = hcg * 4 + hl
                            pi = hc % 2
                            E.wait("tensor", afg)
                            E.wait("tensor", st.get(f"cpg{pi}"))
                            for kc in range(16):
                                E.op("tensor", lambda e, kc=kc, hl=hl, slg=slg, pi=pi: e.matmul(
                                    pG[pi][:, :], arena[slg][:, kc, hl * 128:hl * 128 + 128],
                                    h2T_sb[:, kc, :], start=(kc == 0), stop=(kc == 15)),
                                    inc=("TE" if kc == 15 else None))
                            gmm = E.cnt("TE")
                            E.wait("tensor", afu)
                            E.wait("tensor", st.get(f"cpu{pi}"))
                            for kc in range(16):
                                E.op("tensor", lambda e, kc=kc, hl=hl, slu=slu, pi=pi: e.matmul(
                                    pU[pi][:, :], arena[slu][:, kc, hl * 128:hl * 128 + 128],
                                    h2T_sb[:, kc, :], start=(kc == 0), stop=(kc == 15)),
                                    inc=("TE" if kc == 15 else None))
                            umm = E.cnt("TE")
                            if hl == 3:
                                st[f"car{slg}"] = ("TE", gmm)
                                st[f"car{slu}"] = ("TE", umm)
                            E.wait("scalar", ("TE", gmm))
                            E.wait("scalar", st.get(f"csg{pi}"))
                            sg_n = E.op("scalar", lambda e, pi=pi: e.activation(
                                sg_sb[pi][:, :], pG[pi][:, :], AF.Silu), inc="AE")
                            st[f"cpg{pi}"] = ("AE", sg_n)
                            E.wait("vector", ("AE", sg_n))
                            E.wait("vector", ("TE", umm))
                            E.wait("vector", st.get("cgT_rel"))
                            gv = E.op("vector", lambda e, hc=hc, pi=pi: e.tensor_tensor(
                                gT_sb[:, hc % 32, :], sg_sb[pi][:, :], pU[pi][:, :], ALU.mult),
                                inc="VE")
                            st[f"csg{pi}"] = ("VE", gv)
                            st[f"cpu{pi}"] = ("VE", gv)
                    gT_rdy = ("VE", E.cnt("VE"))
                    E.wait("tensor", gT_rdy)
                    for fo in range(4):
                        wdsl = []
                        for j in range(2):
                            slw, afw = arena_load(
                                r128(wd_b_h)[:, (hb + j * 16):(hb + j * 16) + 16,
                                             fo * 512:fo * 512 + 512])
                            wdsl.append((slw, afw))
                        for qs in range(4):
                            pi = qs % 2
                            E.wait("tensor", st.get(f"cpd{pi}"))
                            for j in range(2):
                                slw, afw = wdsl[j]
                                E.wait("tensor", afw)
                                for hl in range(16):
                                    hc = j * 16 + hl
                                    E.op("tensor", lambda e, hc=hc, hl=hl, qs=qs, slw=slw, pi=pi, j=j: e.matmul(
                                        pD[pi][:, :], gT_sb[:, hc, qs * 128:qs * 128 + 128],
                                        arena[slw][:, hl, :],
                                        start=(j == 0 and hl == 0), stop=(j == 1 and hl == 15)),
                                        inc=("TE" if (j == 1 and hl == 15) else None))
                            dmm = E.cnt("TE")
                            if qs == 3:
                                for (slw, afw) in wdsl:
                                    st[f"car{slw}"] = ("TE", dmm)
                                if fo == 3:
                                    st["cgT_rel"] = ("TE", dmm)
                            E.wait("vector", ("TE", dmm))
                            E.wait("vector", st.get("cyz_w2"))
                            if half == 0:
                                ev = E.op("vector", lambda e, qs=qs, fo=fo, pi=pi: e.tensor_copy(
                                    yz_sb[:, qs, fo * 512:fo * 512 + 512], pD[pi][:, :]),
                                    inc="VE")
                            else:
                                ev = E.op("vector", lambda e, qs=qs, fo=fo, pi=pi: e.tensor_tensor(
                                    yz_sb[:, qs, fo * 512:fo * 512 + 512],
                                    yz_sb[:, qs, fo * 512:fo * 512 + 512], pD[pi][:, :],
                                    ALU.add), inc="VE")
                            st[f"cpd{pi}"] = ("VE", ev)
                zrdy = ("VE", E.cnt("VE"))
                if dbg:
                    E.wait("sync", zrdy)
                    E.op("sync", lambda e, t0=t0: e.dma_start(
                        out=d_z_h.rearrange("(a p) d -> p a d", p=128)[:, t0 // 128:t0 // 128 + 4, :],
                        in_=yz_sb[:, :, :]), inc="DBG", k=16)
                for qs in range(4):
                    E.wait("vector", st.get("ct_rel"))
                    apf = ln_chain(yz_sb[:, qs, :], t_scr[:, :], D, epsln, s43, "lnC",
                                   src_f=(zrdy if qs == 0 else None))
                    E.wait("gpsimd", apf)
                    E.wait("gpsimd", st.get("cout"))
                    E.sbar("gpsimd", "GE")
                    gadd = E.op("gpsimd", lambda e, qs=qs: e.tensor_tensor(
                        out_sb[:, :], xc_sb[:, qs, :], t_scr[:, :], ALU.add), inc="GE")
                    st["ct_rel"] = ("GE", gadd)
                    if qs == 3:
                        st["cx_rel"] = ("GE", gadd)
                    E.wait("sync", ("GE", gadd))
                    yo = E.op("sync", lambda e, t0=t0, qs=qs: e.dma_start(
                        out=y_h[t0 + qs * 128:t0 + qs * 128 + 128, :], in_=out_sb[:, :]),
                        inc="YO", k=16)
                    st["cout"] = ("YO", yo)
                st["cyz_w"] = ("VE", E.cnt("VE"))
            E.wait("sync", ("YO", E.cnt("YO")))
            if dbg:
                E.wait("sync", ("DBG", E.cnt("DBG")))

        with nc.Block() as block:
            @block.sync
            def _(e):
                for f in E.ops["sync"]:
                    f(e)

            @block.tensor
            def _(e):
                for f in E.ops["tensor"]:
                    f(e)

            @block.vector
            def _(e):
                for f in E.ops["vector"]:
                    f(e)

            @block.scalar
            def _(e):
                for f in E.ops["scalar"]:
                    f(e)

            @block.gpsimd
            def _(e):
                for f in E.ops["gpsimd"]:
                    f(e)

    return nc


def _prep_inputs(inputs, S):
    T = lambda a: np.ascontiguousarray(np.asarray(a).T.astype(np.float32))
    wqr = T(inputs["L0_attn_wq_rope"])
    tq = np.empty(NH * DR, dtype=np.int64)
    for d in range(NH * DR):
        H, r = d // DR, d % DR
        tq[d] = (512 if r % 2 else 0) + H * 32 + r // 2
    wqr_p = np.empty_like(wqr)
    wqr_p[:, tq] = wqr
    wkr = T(inputs["L0_attn_wk_rope"])
    tk = np.empty(DR, dtype=np.int64)
    for r in range(DR):
        tk[r] = (32 if r % 2 else 0) + r // 2
    wkr_p = np.empty_like(wkr)
    wkr_p[:, tk] = wkr
    ident = np.eye(128, dtype=np.float32)
    ii = np.arange(128)
    trimask = np.where(ii[None, :] <= ii[:, None], 0.0, NEGBIG).astype(np.float32)
    inv = 10000.0 ** (-(np.arange(32, dtype=np.float64) * 2.0 / DR))
    fr = np.outer(inv, np.arange(S, dtype=np.float64))
    rcos = np.tile(np.cos(fr), (4, 1)).astype(np.float32)
    rsin = np.tile(np.sin(fr), (4, 1)).astype(np.float32)
    return dict(
        wqd=T(inputs["L0_attn_wq_down"]), wqu=T(inputs["L0_attn_wq_up"]), wqr=wqr_p,
        qnw=np.ascontiguousarray(np.asarray(inputs["L0_attn_q_norm_w"], dtype=np.float32).reshape(QR // 128, 128).T),
        wkd=T(inputs["L0_attn_wkv_down"]),
        kvw=np.ascontiguousarray(np.asarray(inputs["L0_attn_kv_norm_w"], dtype=np.float32).reshape(KVR // 128, 128).T),
        wku=T(inputs["L0_attn_wkv_up"]), wkr=wkr_p, wo=T(inputs["L0_attn_wo"]),
        wg=T(inputs["L0_ffn_w_gate"]), wu=T(inputs["L0_ffn_w_up"]),
        wd=T(inputs["L0_ffn_w_down"]), ident=ident, trimask=trimask,
        rcos=rcos, rsin=rsin)


def kernel(**inputs):
    S = 4096
    x = np.asarray(inputs["L0_attn_x"])
    B = x.shape[0]
    common = _prep_inputs(inputs, S)
    in_maps = [dict(common, x=np.ascontiguousarray(x[c], dtype=np.float32)) for c in range(B)]
    nc = build_program(S)
    res = run_bass_kernel_spmd(nc, in_maps, core_ids=list(range(B)))
    out = np.stack([np.asarray(res.results[c]["y"]) for c in range(B)], axis=0)
    return out.astype(np.float32)


# version 03
# speedup vs baseline: 1.3373x; 1.3258x over previous
import math
import contextlib
import numpy as np
import concourse.bass as bass
import concourse.mybir as mybir
from concourse.bass_utils import run_bass_kernel_spmd

F32 = mybir.dt.float32
BF16 = mybir.dt.bfloat16
AF = mybir.ActivationFunctionType
ALU = mybir.AluOpType
AX = mybir.AxisListType

D = 2048
NH = 16
QR = 1536
KVR = 512
DN = 128
DR = 64
HID = 8192
SCALE = 1.0 / math.sqrt(DN + DR)
MSHIFT = -10.0
EPS_LN = 1e-5
EPS_RMS = 1e-6
NEGBIG = -1.0e30
B_MAX_QT = 10**9
B_MAX_CH = 10**9
B_PREFETCH = True
B_H_MAX = 16
B_FIN = True
C_MAX_T = 10**9
A_MAX_T = 10**9


class Emit:
    ENGINES = ("sync", "tensor", "vector", "scalar", "gpsimd")

    def __init__(self, nc, stack):
        self.nc = nc
        self.stack = stack
        self.ops = {e: [] for e in self.ENGINES}
        self.h = {}
        self.n = {}

    def sem(self, name):
        if name not in self.h:
            self.h[name] = self.stack.enter_context(self.nc.semaphore(name))
            self.n[name] = 0
        return self.h[name]

    def op(self, eng, fn, inc=None, k=1):
        if inc is None:
            self.ops[eng].append(fn)
            return None
        hh = self.sem(inc)
        self.n[inc] += k
        self.ops[eng].append(lambda e, f=fn, hh=hh, k=k: f(e).then_inc(hh, k))
        return self.n[inc]

    def wait(self, eng, fence):
        if fence is None:
            return
        name, val = fence
        if val is None or val <= 0:
            return
        hh = self.sem(name)
        self.ops[eng].append(lambda e, hh=hh, v=val: e.wait_ge(hh, v))

    def cnt(self, name):
        self.sem(name)
        return self.n[name]

    def sbar(self, eng, ctr):
        self.wait(eng, (ctr, self.n.get(ctr, 0)))

    def barrier(self, fences):
        for eng in self.ENGINES:
            for other, f in fences.items():
                if other != eng:
                    self.wait(eng, f)


def build_program(S, dbg=False):
    nc = bass.Bass("TRN2", target_bir_lowering=False)
    NT = S // 512
    skind = "ExternalOutput" if dbg else "Internal"

    x_h = nc.dram_tensor("x", [S, D], F32, kind="ExternalInput").ap()
    wqd_h = nc.dram_tensor("wqd", [D, QR], F32, kind="ExternalInput").ap()
    wqu_h = nc.dram_tensor("wqu", [QR, NH * DN], F32, kind="ExternalInput").ap()
    wqr_h = nc.dram_tensor("wqr", [QR, NH * DR], F32, kind="ExternalInput").ap()
    qnw_h = nc.dram_tensor("qnw", [128, QR // 128], F32, kind="ExternalInput").ap()
    wkd_h = nc.dram_tensor("wkd", [D, KVR], F32, kind="ExternalInput").ap()
    kvw_h = nc.dram_tensor("kvw", [128, KVR // 128], F32, kind="ExternalInput").ap()
    wku_h = nc.dram_tensor("wku", [KVR, NH * 256], F32, kind="ExternalInput").ap()
    wkr_h = nc.dram_tensor("wkr", [D, DR], F32, kind="ExternalInput").ap()
    wo_h = nc.dram_tensor("wo", [D, D], F32, kind="ExternalInput").ap()
    wg_h = nc.dram_tensor("wg", [D, HID], F32, kind="ExternalInput").ap()
    wu_h = nc.dram_tensor("wu", [D, HID], F32, kind="ExternalInput").ap()
    wd_h = nc.dram_tensor("wd", [HID, D], F32, kind="ExternalInput").ap()
    id_h = nc.dram_tensor("ident", [128, 128], F32, kind="ExternalInput").ap()
    tm_h = nc.dram_tensor("trimask", [128, 128], F32, kind="ExternalInput").ap()
    rc_h = nc.dram_tensor("rcos", [128, S], F32, kind="ExternalInput").ap()
    rs_h = nc.dram_tensor("rsin", [128, S], F32, kind="ExternalInput").ap()
    y_h = nc.dram_tensor("y", [S, D], F32, kind="ExternalOutput").ap()
    s_h_h = nc.dram_tensor("s_h", [D, S], BF16, kind=skind).ap()
    s_qn_h = nc.dram_tensor("s_qn", [NH * DN, S], BF16, kind=skind).ap()
    s_qp_h = nc.dram_tensor("s_qp", [NH * DR, S], BF16, kind=skind).ap()
    s_kn_h = nc.dram_tensor("s_kn", [NH * DN, S], BF16, kind=skind).ap()
    s_v_h = nc.dram_tensor("s_v", [S, NH * DN], BF16, kind=skind).ap()
    s_at_h = nc.dram_tensor("s_at", [D, S], BF16, kind=skind).ap()
    if dbg:
        d_wo_h = nc.dram_tensor("d_wo", [S, D], F32, kind="ExternalOutput").ap()
        d_x2_h = nc.dram_tensor("d_x2", [S, D], F32, kind="ExternalOutput").ap()
        d_z_h = nc.dram_tensor("d_z", [S, D], F32, kind="ExternalOutput").ap()
    wo_b_h = nc.dram_tensor("wo_b", [D, D], BF16, kind="Internal").ap()
    wg_b_h = nc.dram_tensor("wg_b", [D, HID], BF16, kind="Internal").ap()
    wu_b_h = nc.dram_tensor("wu_b", [D, HID], BF16, kind="Internal").ap()
    wd_b_h = nc.dram_tensor("wd_b", [HID, D], BF16, kind="Internal").ap()

    r128 = lambda ap_: ap_.rearrange("(c p) t -> p c t", p=128)
    r64 = lambda ap_: ap_.rearrange("(c p) t -> p c t", p=64)

    stack = contextlib.ExitStack()
    with stack:
        E = Emit(nc, stack)
        sb = lambda name, shape, dt: stack.enter_context(nc.sbuf_tensor(name, shape, dt))
        st = {}

        id_sb = sb("id_sb", [128, 128], BF16)
        tm_sb = sb("tm_sb", [128, 128], F32)
        krope = sb("krope", [64, S], BF16)
        epsln = sb("epsln", [128, 1], F32)
        epsrm = sb("epsrm", [128, 1], F32)
        mshift = sb("mshift", [128, 1], F32)
        junk = sb("junk", [128, 2048], BF16)

        E.op("gpsimd", lambda e: e.dma_start(out=id_sb[:, :], in_=id_h[:, :]), inc="LW0", k=16)
        E.op("gpsimd", lambda e: e.dma_start(out=tm_sb[:, :], in_=tm_h[:, :]), inc="LW0", k=16)
        E.op("gpsimd", lambda e: e.memset(epsln[:, :], EPS_LN), inc="GE")
        E.op("gpsimd", lambda e: e.memset(epsrm[:, :], EPS_RMS), inc="GE")
        E.op("gpsimd", lambda e: e.memset(mshift[:, :], MSHIFT), inc="GE")
        P0_LW = ("LW0", E.cnt("LW0"))
        P0_GE = ("GE", E.cnt("GE"))

        def ln_chain(src, out, width, eps_tile, s4, tag, src_f=None, rms=False,
                     ss_ready=None, nss=1):
            sums, ss, nmu, rst = s4
            prev = st.get(tag)
            if src_f is not None:
                E.wait("vector", src_f)
                E.wait("scalar", src_f)
            if prev is not None:
                E.wait("vector", ("AE", prev[1]))
                E.wait("scalar", ("VE", prev[0]))
            if not rms:
                E.op("vector", lambda e: e.reduce_sum(sums, src, axis=AX.X), inc="VE")
            if ss_ready is None:
                E.sbar("scalar", "AE")
                a_sq = E.op("scalar", lambda e: e.activation(junk[:, 0:width], src, AF.Square,
                                                             accum_out=ss[:, 0:1]), inc="AE")
            else:
                a_sq = ss_ready[1]
                E.wait("scalar", ss_ready)
            E.wait("vector", ("AE", a_sq))
            if nss > 1:
                E.op("vector", lambda e: e.reduce_sum(sums if rms else rst, ss[:, 0:nss],
                                                      axis=AX.X), inc="VE")
            if rms:
                srcss = sums if nss > 1 else ss
                E.sbar("vector", "VE")
                E.op("vector", lambda e: e.tensor_scalar_mul(rst, srcss[:, 0:1], 1.0 / width),
                     inc="VE")
            else:
                E.sbar("vector", "VE")
                E.op("vector", lambda e: e.tensor_scalar_mul(nmu, sums, -1.0 / width), inc="VE")
                if nss > 1:
                    E.op("vector", lambda e: e.tensor_scalar_mul(rst, rst, 1.0 / width),
                         inc="VE")
                else:
                    E.op("vector", lambda e: e.tensor_scalar_mul(rst, ss[:, 0:1], 1.0 / width),
                         inc="VE")
                E.sbar("vector", "VE")
                E.op("vector", lambda e: e.tensor_tensor(sums, nmu, nmu, ALU.mult), inc="VE")
                E.sbar("vector", "VE")
                E.op("vector", lambda e: e.tensor_tensor(rst, rst, sums, ALU.subtract), inc="VE")
            v3 = E.cnt("VE")
            E.wait("scalar", ("VE", v3))
            a_rt = E.op("scalar", lambda e: e.activation(rst, rst, AF.Sqrt,
                                                         bias=eps_tile[:, 0:1]), inc="AE")
            E.wait("vector", ("AE", a_rt))
            E.op("vector", lambda e: e.reciprocal(rst, rst), inc="VE")
            E.sbar("vector", "VE")
            if rms:
                vap = E.op("vector", lambda e: e.tensor_scalar_mul(out, src, rst[:, 0:1]),
                           inc="VE")
            else:
                vap = E.op("vector", lambda e: e.tensor_scalar(out, src, nmu[:, 0:1], rst[:, 0:1],
                                                               ALU.add, ALU.mult), inc="VE")
            st[tag] = (vap, a_rt)
            return ("VE", vap)

        conv_jobs = []
        for (src, dst, rows, rstep) in ((wo_h, wo_b_h, D, 1024), (wg_h, wg_b_h, D, 256),
                                        (wu_h, wu_b_h, D, 256), (wd_h, wd_b_h, HID, 1024)):
            for r0 in range(0, rows, rstep):
                conv_jobs.append((src, dst, r0, r0 + rstep))
        conv_it = iter(conv_jobs)

        def drip_conv(n):
            for _ in range(n):
                job = next(conv_it, None)
                if job is None:
                    return
                src, dst, r0, r1 = job
                E.op("gpsimd", lambda e, src=src, dst=dst, r0=r0, r1=r1: e.dma_start(
                    out=dst[r0:r1, :], in_=src[r0:r1, :]), inc="CVT", k=16)

        with contextlib.ExitStack() as pst:
            psb = lambda name, shape, dt: pst.enter_context(nc.sbuf_tensor(name, shape, dt))
            pps = lambda name, shape, dt: pst.enter_context(nc.psum_tensor(name, shape, dt))
            wqd_sb = psb("wqd_sb", [128, 16, QR], BF16)
            wqu_s = [psb(f"wqu_s{i}", [128, 12, 512], BF16) for i in range(2)]
            wqr_sb = psb("wqr_sb", [128, 12, NH * DR], BF16)
            qnw_sb = psb("qnw_sb", [128, 12], F32)
            x_sb = [psb(f"x_sb{i}", [128, D], F32) for i in range(2)]
            h_sb = [psb(f"h_sb{i}", [128, D], BF16) for i in range(2)]
            hT_sb = psb("hT_sb", [128, 16, 512], BF16)
            qc_sb = psb("qc_sb", [128, QR], F32)
            qcn_sb = psb("qcn_sb", [128, QR], BF16)
            qcT_sb = psb("qcT_sb", [128, 12, 512], BF16)
            qn_st = [psb(f"qn_st{i}", [128, 4, 512], BF16) for i in range(2)]
            qpe_sb = psb("qpe_sb", [128, 8, 512], BF16)
            ro_e = psb("ro_e", [128, 512], BF16)
            ro_o = psb("ro_o", [128, 512], BF16)
            rt1 = psb("rt1", [128, 512], BF16)
            rt2 = psb("rt2", [128, 512], BF16)
            rc_sb = psb("rc_sb", [128, 512], F32)
            rs_sb = psb("rs_sb", [128, 512], F32)
            sums = psb("sums", [128, 1], F32)
            ss = psb("ss", [128, 3], F32)
            nmu = psb("nmu", [128, 1], F32)
            rst = psb("rst", [128, 1], F32)
            s4 = (sums[:, 0:1], ss, nmu[:, 0:1], rst[:, 0:1])
            pA = [pps(f"pA{i}", [128, 512], F32) for i in range(2)]
            pT = [pps(f"pT{i}", [128, 2048], BF16) for i in range(2)]

            E.op("gpsimd", lambda e: e.dma_start(out=wqd_sb[:, :, :], in_=r128(wqd_h)),
                 inc="LW1", k=16)
            E.op("gpsimd", lambda e: e.dma_start(out=wqr_sb[:, :, :], in_=r128(wqr_h)),
                 inc="LW1", k=16)
            E.op("gpsimd", lambda e: e.dma_start(out=qnw_sb[:, :], in_=qnw_h[:, :]),
                 inc="LW1", k=16)
            wA1 = ("LW1", E.cnt("LW1"))
            E.wait("vector", wA1)
            E.wait("tensor", wA1)
            E.wait("tensor", P0_LW)
            E.wait("vector", P0_LW)

            def load_wqu(t, g):
                sl = (t * 4 + g) % 2
                E.wait("gpsimd", st.get(f"wqu{sl}"))
                n = E.op("gpsimd", lambda e, g=g, sl=sl: e.dma_start(
                    out=wqu_s[sl][:, :, :],
                    in_=r128(wqu_h)[:, :, g * 512:g * 512 + 512]), inc=f"WQ{sl}", k=16)
                drip_conv(1)
                return sl, (f"WQ{sl}", n)

            for t in range(NT):
                t0 = t * 512
                E.wait("sync", st.get("a1rope"))
                rs_n = None
                E.op("sync", lambda e, t0=t0: e.dma_start(out=rc_sb[:, :],
                     in_=rc_h[:, t0:t0 + 512]), inc="RT1", k=16)
                rs_n = E.op("sync", lambda e, t0=t0: e.dma_start(out=rs_sb[:, :],
                     in_=rs_h[:, t0:t0 + 512]), inc="RT1", k=16)
                for qs in range(4):
                    xi = qs % 2
                    E.wait("sync", st.get(f"a1x{xi}"))
                    xn = E.op("sync", lambda e, t0=t0, qs=qs, xi=xi: e.dma_start(
                        out=x_sb[xi][:, :], in_=x_h[t0 + qs * 128:t0 + qs * 128 + 128, :]),
                        inc=f"XA{xi}", k=16)
                    E.wait("vector", st.get(f"a1h{xi}"))
                    apf = ln_chain(x_sb[xi][:, :], h_sb[xi][:, :], D, epsln, s4, "sA1",
                                   src_f=(f"XA{xi}", xn))
                    st[f"a1x{xi}"] = apf
                    E.wait("tensor", apf)
                    E.wait("tensor", st.get(f"a1pt{qs % 2}"))
                    for c in range(16):
                        E.op("tensor", lambda e, c=c, qs=qs, xi=xi: e.transpose(
                            pT[qs % 2][:, c * 128:(c + 1) * 128],
                            h_sb[xi][:, c * 128:(c + 1) * 128], id_sb[:, :]),
                            inc=("TE" if c == 15 else None))
                    tn = E.cnt("TE")
                    st[f"a1h{xi}"] = ("TE", tn)
                    E.wait("vector", ("TE", tn))
                    E.wait("vector", st.get(f"a1hT{qs}"))
                    E.wait("vector", st.get(f"a1hTs{qs}"))
                    cp = E.op("vector", lambda e, qs=qs: e.tensor_copy(
                        hT_sb[:, :, qs * 128:qs * 128 + 128],
                        pT[qs % 2][:, :]), inc="VE")
                    st[f"a1pt{qs % 2}"] = ("VE", cp)
                    E.wait("sync", ("VE", cp))
                    shn = E.op("sync", lambda e, t0=t0, qs=qs: e.dma_start(
                        out=r128(s_h_h)[:, :, t0 + qs * 128:t0 + qs * 128 + 128],
                        in_=hT_sb[:, :, qs * 128:qs * 128 + 128]), inc=f"SH{qs}", k=16)
                    st[f"a1hTs{qs}"] = (f"SH{qs}", shn)

                    for fo in range(3):
                        pi = (qs * 3 + fo) % 2
                        E.wait("tensor", st.get(f"a1pa{pi}"))
                        E.wait("tensor", ("VE", cp))
                        for kc in range(16):
                            E.op("tensor", lambda e, kc=kc, fo=fo, qs=qs, pi=pi: e.matmul(
                                pA[pi][:, :], hT_sb[:, kc, qs * 128:qs * 128 + 128],
                                wqd_sb[:, kc, fo * 512:fo * 512 + 512],
                                start=(kc == 0), stop=(kc == 15)),
                                inc=("TE" if kc == 15 else None))
                        mmn = E.cnt("TE")
                        E.wait("vector", ("TE", mmn))
                        if fo == 0:
                            E.wait("vector", st.get("a1qcr"))
                        ev = E.op("vector", lambda e, fo=fo, pi=pi: e.tensor_copy(
                            qc_sb[:, fo * 512:fo * 512 + 512], pA[pi][:, :]), inc="VE")
                        st[f"a1pa{pi}"] = ("VE", ev)
                        E.wait("scalar", ("VE", ev))
                        E.wait("scalar", st.get("a1ssr"))
                        E.sbar("scalar", "AE")
                        aq = E.op("scalar", lambda e, fo=fo: e.activation(
                            junk[:, 0:512], qc_sb[:, fo * 512:fo * 512 + 512], AF.Square,
                            accum_out=ss[:, fo:fo + 1]), inc="AE")
                    E.wait("vector", st.get("a1qcn"))
                    apf = ln_chain(qc_sb[:, :], qcn_sb[:, :], QR, epsrm, s4, "sA1",
                                   rms=True, ss_ready=("AE", aq), nss=3)
                    st["a1qcr"] = ("AE", aq)
                    st["a1ssr"] = ("VE", E.cnt("VE"))
                    E.wait("tensor", apf)
                    E.wait("tensor", st.get(f"a1pt{qs % 2}"))
                    for c in range(12):
                        E.op("tensor", lambda e, c=c, qs=qs: e.transpose(
                            pT[qs % 2][:, c * 128:(c + 1) * 128],
                            qcn_sb[:, c * 128:(c + 1) * 128], id_sb[:, :]),
                            inc=("TE" if c == 11 else None))
                    tn = E.cnt("TE")
                    st["a1qcn"] = ("TE", tn)
                    E.wait("vector", ("TE", tn))
                    E.wait("vector", st.get("a1qcT"))
                    for kc in range(12):
                        cp2 = E.op("vector", lambda e, qs=qs, kc=kc: e.tensor_scalar_mul(
                            qcT_sb[:, kc, qs * 128:qs * 128 + 128],
                            pT[qs % 2][:, kc * 128:kc * 128 + 128],
                            qnw_sb[:, kc:kc + 1]), inc="VE")
                    st[f"a1pt{qs % 2}"] = ("VE", cp2)
                    st[f"a1hT{qs}"] = ("TE", E.cnt("TE"))
                qcT_rdy = ("VE", E.cnt("VE"))
                E.wait("tensor", qcT_rdy)
                wq_slots = {}
                wq_slots[0] = load_wqu(t, 0)
                for oc in range(16):
                    g = oc // 4
                    if oc % 4 == 0 and g + 1 < 4:
                        wq_slots[g + 1] = load_wqu(t, g + 1)
                    sl_w, af_w = wq_slots[g]
                    pi = oc % 2
                    E.wait("tensor", st.get(f"a1pa{pi}"))
                    E.wait("tensor", af_w)
                    for kc in range(12):
                        E.op("tensor", lambda e, kc=kc, oc=oc, pi=pi, sl_w=sl_w: e.matmul(
                            pA[pi][:, :], wqu_s[sl_w][:, kc, (oc % 4) * 128:(oc % 4) * 128 + 128],
                            qcT_sb[:, kc, :], start=(kc == 0), stop=(kc == 11)),
                            inc=("TE" if kc == 11 else None))
                    mmn = E.cnt("TE")
                    if oc % 4 == 3:
                        st[f"wqu{sl_w}"] = ("TE", mmn)
                    E.wait("vector", ("TE", mmn))
                    sl, bi = oc % 4, (oc // 4) % 2
                    if sl == 0:
                        E.wait("vector", st.get(f"a1qn{bi}"))
                    ev = E.op("vector", lambda e, sl=sl, bi=bi, pi=pi: e.tensor_copy(
                        qn_st[bi][:, sl, :], pA[pi][:, :]), inc="VE")
                    st[f"a1pa{pi}"] = ("VE", ev)
                    if sl == 3:
                        E.wait("sync", ("VE", ev))
                        stn = E.op("sync", lambda e, t0=t0, oc=oc, bi=bi: e.dma_start(
                            out=r128(s_qn_h)[:, oc - 3:oc + 1, t0:t0 + 512],
                            in_=qn_st[bi][:, :, :]), inc=f"SA{bi}", k=16)
                        st[f"a1qn{bi}"] = (f"SA{bi}", stn)
                E.wait("vector", st.get("a1ropeV"))
                for oc in range(8):
                    pi = oc % 2
                    E.wait("tensor", st.get(f"a1pa{pi}"))
                    for kc in range(12):
                        E.op("tensor", lambda e, kc=kc, oc=oc, pi=pi: e.matmul(
                            pA[pi][:, :], wqr_sb[:, kc, oc * 128:oc * 128 + 128],
                            qcT_sb[:, kc, :], start=(kc == 0), stop=(kc == 11)),
                            inc=("TE" if kc == 11 else None))
                    mmn = E.cnt("TE")
                    E.wait("vector", ("TE", mmn))
                    ev = E.op("vector", lambda e, oc=oc, pi=pi: e.tensor_copy(
                        qpe_sb[:, oc, :], pA[pi][:, :]), inc="VE")
                    st[f"a1pa{pi}"] = ("VE", ev)
                st["a1qcT"] = ("TE", E.cnt("TE"))
                E.wait("vector", ("RT1", rs_n))
                qp_g = s_qp_h.rearrange("(c q r) t -> c q r t", c=4, q=4, r=64)
                for c in range(4):
                    E.wait("vector", st.get("a1roS"))
                    E.op("vector", lambda e, c=c: e.tensor_tensor(rt1[:, :], qpe_sb[:, c, :],
                         rc_sb[:, :], ALU.mult), inc="VE")
                    E.op("vector", lambda e, c=c: e.tensor_tensor(rt2[:, :], qpe_sb[:, c + 4, :],
                         rs_sb[:, :], ALU.mult), inc="VE")
                    E.sbar("vector", "VE")
                    E.op("vector", lambda e: e.tensor_tensor(ro_e[:, :], rt1[:, :], rt2[:, :],
                         ALU.subtract), inc="VE")
                    E.sbar("vector", "VE")
                    E.op("vector", lambda e, c=c: e.tensor_tensor(rt1[:, :], qpe_sb[:, c, :],
                         rs_sb[:, :], ALU.mult), inc="VE")
                    E.op("vector", lambda e, c=c: e.tensor_tensor(rt2[:, :], qpe_sb[:, c + 4, :],
                         rc_sb[:, :], ALU.mult), inc="VE")
                    E.sbar("vector", "VE")
                    rov = E.op("vector", lambda e: e.tensor_tensor(ro_o[:, :], rt1[:, :], rt2[:, :],
                               ALU.add), inc="VE")
                    E.wait("sync", ("VE", rov))
                    E.op("sync", lambda e, c=c, t0=t0: e.dma_start(
                        out=qp_g[c, :, 0:32, t0:t0 + 512], in_=ro_e[:, :]), inc="RO", k=16)
                    ron = E.op("sync", lambda e, c=c, t0=t0: e.dma_start(
                        out=qp_g[c, :, 32:64, t0:t0 + 512], in_=ro_o[:, :]), inc="RO", k=16)
                    st["a1roS"] = ("RO", ron)
                st["a1ropeV"] = ("VE", E.cnt("VE"))
                st["a1rope"] = ("VE", E.cnt("VE"))
            drip_conv(100)
            A1_f = {"sync": ("RO", E.cnt("RO")), "tensor": ("TE", E.cnt("TE")),
                    "vector": ("VE", E.cnt("VE")), "scalar": ("AE", E.cnt("AE")),
                    "gpsimd": ("LW1", E.cnt("LW1"))}
            A1_extra = [(f"SH{q}", E.cnt(f"SH{q}")) for q in range(4)] + \
                       [("SA0", E.cnt("SA0")), ("SA1", E.cnt("SA1"))]
        E.barrier(A1_f)
        for f in A1_extra:
            for eng in Emit.ENGINES:
                E.wait(eng, f)

        with contextlib.ExitStack() as pst:
            psb = lambda name, shape, dt: pst.enter_context(nc.sbuf_tensor(name, shape, dt))
            pps = lambda name, shape, dt: pst.enter_context(nc.psum_tensor(name, shape, dt))
            wkd_sb = psb("wkd_sb", [128, 16, KVR], BF16)
            wku_sb = psb("wku_sb", [128, 4, NH * 256], BF16)
            wkr_sb = psb("wkr_sb", [128, 16, DR], BF16)
            kvw_sb = psb("kvw_sb", [128, 4], F32)
            hT2 = [psb(f"hT2_{i}", [128, 16, 512], BF16) for i in range(2)]
            kvc_sb = psb("kvc_sb", [128, KVR], F32)
            kvn_sb = psb("kvn_sb", [128, KVR], BF16)
            kvT_sb = psb("kvT_sb", [128, 4, 512], BF16)
            kn_st = [psb(f"kn_st{i}", [128, 4, 512], BF16) for i in range(2)]
            v_bf = psb("v_bf", [128, 512], BF16)
            v_st = psb("v_st", [128, 4, NH * DN], BF16)
            rt1b = psb("rt1b", [32, 512], F32)
            rt2b = psb("rt2b", [32, 512], F32)
            rc2 = psb("rc2", [128, 512], F32)
            rs2 = psb("rs2", [128, 512], F32)
            sums2 = psb("sums2", [128, 1], F32)
            ss2 = psb("ss2", [128, 1], F32)
            nmu2 = psb("nmu2", [128, 1], F32)
            rst2 = psb("rst2", [128, 1], F32)
            s42 = (sums2[:, 0:1], ss2, nmu2[:, 0:1], rst2[:, 0:1])
            pA2 = [pps(f"pA2_{i}", [128, 512], F32) for i in range(2)]
            pT2 = [pps(f"pT2_{i}", [128, 2048], BF16) for i in range(2)]
            pV2 = pps("pV2", [128, 512], BF16)
            pK2 = pps("pK2", [128, 512], F32)

            E.op("gpsimd", lambda e: e.dma_start(out=wkd_sb[:, :, :], in_=r128(wkd_h)),
                 inc="LW2", k=16)
            E.op("gpsimd", lambda e: e.dma_start(out=wku_sb[:, :, :], in_=r128(wku_h)),
                 inc="LW2", k=16)
            E.op("gpsimd", lambda e: e.dma_start(out=wkr_sb[:, :, :], in_=r128(wkr_h)),
                 inc="LW2", k=16)
            E.op("gpsimd", lambda e: e.dma_start(out=kvw_sb[:, :], in_=kvw_h[:, :]),
                 inc="LW2", k=16)
            wA2 = ("LW2", E.cnt("LW2"))
            E.wait("vector", wA2)
            for kc in range(4):
                E.op("vector", lambda e, kc=kc: e.tensor_scalar_mul(
                    wku_sb[:, kc, :], wku_sb[:, kc, :], kvw_sb[:, kc:kc + 1]), inc="VE")
            foldkv = ("VE", E.cnt("VE"))
            E.wait("tensor", foldkv)

            for t in range(NT):
                t0 = t * 512
                ti = t % 2
                E.wait("sync", st.get("a2rope"))
                rs_n = None
                E.op("sync", lambda e, t0=t0: e.dma_start(out=rc2[:, :],
                     in_=rc_h[:, t0:t0 + 512]), inc="RT2", k=16)
                rs_n = E.op("sync", lambda e, t0=t0: e.dma_start(out=rs2[:, :],
                     in_=rs_h[:, t0:t0 + 512]), inc="RT2", k=16)
                E.wait("sync", st.get(f"a2h{ti}"))
                hn = E.op("sync", lambda e, t0=t0, ti=ti: e.dma_start(
                    out=hT2[ti][:, :, :], in_=r128(s_h_h)[:, :, t0:t0 + 512]),
                    inc=f"XB{ti}", k=16)
                for qs in range(4):
                    pi = qs % 2
                    E.wait("tensor", st.get(f"a2pa{pi}"))
                    E.wait("tensor", (f"XB{ti}", hn))
                    for kc in range(16):
                        E.op("tensor", lambda e, kc=kc, qs=qs, pi=pi, ti=ti: e.matmul(
                            pA2[pi][:, :], hT2[ti][:, kc, qs * 128:qs * 128 + 128],
                            wkd_sb[:, kc, :], start=(kc == 0), stop=(kc == 15)),
                            inc=("TE" if kc == 15 else None))
                    mmn = E.cnt("TE")
                    E.wait("vector", ("TE", mmn))
                    E.wait("vector", st.get("a2kvr"))
                    ev = E.op("vector", lambda e, pi=pi: e.tensor_copy(kvc_sb[:, :],
                              pA2[pi][:, :]), inc="VE")
                    st[f"a2pa{pi}"] = ("VE", ev)
                    E.wait("vector", st.get("a2kvn"))
                    apf = ln_chain(kvc_sb[:, :], kvn_sb[:, :], KVR, epsrm, s42, "sA2",
                                   src_f=("VE", ev), rms=True)
                    st["a2kvr"] = ("AE", E.cnt("AE"))
                    E.wait("tensor", apf)
                    E.wait("tensor", st.get(f"a2pt{pi}"))
                    for c in range(4):
                        E.op("tensor", lambda e, c=c, pi=pi: e.transpose(
                            pT2[pi][:, c * 128:(c + 1) * 128],
                            kvn_sb[:, c * 128:(c + 1) * 128], id_sb[:, :]),
                            inc=("TE" if c == 3 else None))
                    tn = E.cnt("TE")
                    st["a2kvn"] = ("TE", tn)
                    E.wait("vector", ("TE", tn))
                    E.wait("vector", st.get("a2kvT"))
                    cp = E.op("vector", lambda e, qs=qs, pi=pi: e.tensor_copy(
                        kvT_sb[:, :, qs * 128:qs * 128 + 128],
                        pT2[pi][:, 0:512]), inc="VE")
                    st[f"a2pt{pi}"] = ("VE", cp)
                kvT_rdy = ("VE", E.cnt("VE"))
                E.wait("tensor", kvT_rdy)
                for oc in range(32):
                    pi = oc % 2
                    E.wait("tensor", st.get(f"a2pa{pi}"))
                    for kc in range(4):
                        E.op("tensor", lambda e, kc=kc, oc=oc, pi=pi: e.matmul(
                            pA2[pi][:, :], wku_sb[:, kc, oc * 128:oc * 128 + 128],
                            kvT_sb[:, kc, :], start=(kc == 0), stop=(kc == 3)),
                            inc=("TE" if kc == 3 else None))
                    mmn = E.cnt("TE")
                    h_ = oc // 2
                    E.wait("vector", ("TE", mmn))
                    if oc % 2 == 0:
                        sl, bi = h_ % 4, (h_ // 4) % 2
                        if sl == 0:
                            E.wait("vector", st.get(f"a2kn{bi}"))
                        ev = E.op("vector", lambda e, sl=sl, bi=bi, pi=pi: e.tensor_copy(
                            kn_st[bi][:, sl, :], pA2[pi][:, :]), inc="VE")
                        st[f"a2pa{pi}"] = ("VE", ev)
                        if sl == 3:
                            E.wait("sync", ("VE", ev))
                            stn = E.op("sync", lambda e, t0=t0, h_=h_, bi=bi: e.dma_start(
                                out=r128(s_kn_h)[:, h_ - 3:h_ + 1, t0:t0 + 512],
                                in_=kn_st[bi][:, :, :]), inc=f"SK{bi}", k=16)
                            st[f"a2kn{bi}"] = (f"SK{bi}", stn)
                    else:
                        E.wait("vector", st.get("a2vbf"))
                        ev = E.op("vector", lambda e, pi=pi: e.tensor_copy(
                            v_bf[:, :], pA2[pi][:, :]), inc="VE")
                        st[f"a2pa{pi}"] = ("VE", ev)
                        E.wait("tensor", ("VE", ev))
                        E.wait("tensor", st.get("a2pv"))
                        for ts_ in range(4):
                            E.op("tensor", lambda e, ts_=ts_: e.transpose(
                                pV2[:, ts_ * 128:ts_ * 128 + 128],
                                v_bf[:, ts_ * 128:ts_ * 128 + 128], id_sb[:, :]),
                                inc=("TE" if ts_ == 3 else None))
                        tn = E.cnt("TE")
                        st["a2vbf"] = ("TE", tn)
                        E.wait("vector", ("TE", tn))
                        if h_ == 0:
                            E.wait("vector", st.get("a2vst"))
                        cpv = E.op("vector", lambda e, h_=h_: e.tensor_copy(
                            v_st[:, :, h_ * 128:h_ * 128 + 128],
                            pV2[:, :]), inc="VE")
                        st["a2pv"] = ("VE", cpv)
                        if h_ == 15:
                            E.wait("sync", ("VE", cpv))
                            stn = E.op("sync", lambda e, t0=t0: e.dma_start(
                                out=s_v_h.rearrange("(s p) d -> p s d", p=128)[:, t0 // 128:t0 // 128 + 4, :],
                                in_=v_st[:, :, :]), inc="SV", k=16)
                            st["a2vst"] = ("SV", stn)
                st["a2kvT"] = ("TE", E.cnt("TE"))
                E.wait("tensor", st.get("a2krP"))
                for kc in range(16):
                    E.op("tensor", lambda e, kc=kc, ti=ti: e.matmul(
                        pK2[0:64, :], wkr_sb[:, kc, :], hT2[ti][:, kc, :],
                        start=(kc == 0), stop=(kc == 15)), inc=("TE" if kc == 15 else None))
                mmn = E.cnt("TE")
                st[f"a2h{ti}"] = ("TE", mmn)
                E.wait("vector", ("TE", mmn))
                E.wait("vector", ("RT2", rs_n))
                E.op("vector", lambda e: e.tensor_tensor(rt1b[:, :], pK2[0:32, :], rc2[0:32, :],
                     ALU.mult), inc="VE")
                E.op("vector", lambda e: e.tensor_tensor(rt2b[:, :], pK2[32:64, :], rs2[0:32, :],
                     ALU.mult), inc="VE")
                E.sbar("vector", "VE")
                E.op("vector", lambda e, t0=t0: e.tensor_tensor(krope[0:32, t0:t0 + 512],
                     rt1b[:, :], rt2b[:, :], ALU.subtract), inc="VE")
                E.sbar("vector", "VE")
                E.op("vector", lambda e: e.tensor_tensor(rt1b[:, :], pK2[0:32, :], rs2[0:32, :],
                     ALU.mult), inc="VE")
                E.op("vector", lambda e: e.tensor_tensor(rt2b[:, :], pK2[32:64, :], rc2[0:32, :],
                     ALU.mult), inc="VE")
                E.sbar("vector", "VE")
                krn = E.op("vector", lambda e, t0=t0: e.tensor_tensor(krope[32:64, t0:t0 + 512],
                           rt1b[:, :], rt2b[:, :], ALU.add), inc="VE")
                st["a2krP"] = ("VE", krn)
                st["a2rope"] = ("VE", krn)
            A2_f = {"sync": ("SV", E.cnt("SV")), "tensor": ("TE", E.cnt("TE")),
                    "vector": ("VE", E.cnt("VE")), "scalar": ("AE", E.cnt("AE")),
                    "gpsimd": ("LW2", E.cnt("LW2"))}
            A2_extra = [("SK0", E.cnt("SK0")), ("SK1", E.cnt("SK1"))]
        E.barrier(A2_f)
        for f in A2_extra:
            for eng in Emit.ENGINES:
                E.wait(eng, f)

        with contextlib.ExitStack() as pst:
            psb = lambda name, shape, dt: pst.enter_context(nc.sbuf_tensor(name, shape, dt))
            pps = lambda name, shape, dt: pst.enter_context(nc.psum_tensor(name, shape, dt))
            qn_sb = psb("qn_sb", [128, 16, 512], BF16)
            qp_sb = psb("qp_sb", [64, 16, 512], BF16)
            kn_g = [psb(f"kn_g{i}", [128, 16, 512], BF16) for i in range(3)]
            v_g = [psb(f"v_g{i}", [128, 4, NH * DN], BF16) for i in range(3)]
            probs = [psb(f"probs{i}", [128, 512], BF16) for i in range(3)]
            probsT = [psb(f"probsT{i}", [128, 512], BF16) for i in range(3)]
            Oacc = psb("Oacc", [128, 64 * 128], F32)
            l_sb = psb("l_sb", [128, 64], F32)
            rl_sb = psb("rl_sb", [128, 64], F32)
            cs_sb = psb("cs_sb", [128, 3], F32)
            obf = psb("obf", [128, 128], BF16)
            att_st = psb("att_st", [128, 16, 512], BF16)
            pS = [pps(f"pS{i}", [128, 512], F32) for i in range(3)]
            pP = [pps(f"pP{i}", [128, 512], BF16) for i in range(3)]
            pO = [pps(f"pO{i}", [128, 512], F32) for i in range(2)]

            inst = 0
            pend_b = []
            pend_c = []
            E.wait("scalar", P0_GE)

            def stage_sb(p):
                (si, W, nsub, c_, h, qs, col, first, ex, rels) = p
                E.wait("tensor", ("AE", ex))
                E.wait("tensor", st.get(f"bpp{si}"))
                for j in range(nsub):
                    E.op("tensor", lambda e, si=si, j=j: e.transpose(
                        pP[si][:, j * 128:j * 128 + 128],
                        probs[si][:, j * 128:j * 128 + 128], id_sb[:, :]),
                        inc=("TE" if j == nsub - 1 else None))
                ptn = E.cnt("TE")
                st[f"bprb{si}"] = ("TE", ptn)
                if si != 1:
                    E.wait("vector", ("TE", ptn))
                    E.wait("vector", st.get(f"bptT{si}"))
                    pc = E.op("vector", lambda e, si=si, W=W: e.tensor_copy(
                        probsT[si][:, 0:W], pP[si][:, 0:W]), inc="VE")
                    pcf = ("VE", pc)
                else:
                    E.wait("scalar", ("TE", ptn))
                    E.wait("scalar", st.get(f"bptT{si}"))
                    pc = E.op("scalar", lambda e, si=si, W=W: e.copy(
                        probsT[si][:, 0:W], pP[si][:, 0:W]), inc="AE")
                    pcf = ("AE", pc)
                st[f"bpp{si}"] = pcf
                p2 = p + (pcf,)
                return p2

            def stage_sc(p):
                (si, W, nsub, c_, h, qs, col, first, ex, rels, pcf) = p
                po = c_ % 2 if False else (sc_ctr[0] % 2)
                sc_ctr[0] += 1
                E.wait("tensor", pcf)
                E.wait("tensor", st.get(f"bpo{po}"))
                for j in range(nsub):
                    gi = c_ % 3
                    E.op("tensor", lambda e, si=si, j=j, gi=gi, po=po, h=h, nsub=nsub: e.matmul(
                        pO[po][:, 0:128], probsT[si][:, j * 128:j * 128 + 128],
                        v_g[gi][:, j, h * 128:h * 128 + 128],
                        start=(j == 0), stop=(j == nsub - 1)),
                        inc=("TE" if j == nsub - 1 else None))
                on = E.cnt("TE")
                st[f"bptT{si}"] = ("TE", on)
                for key in rels:
                    st[key] = ("TE", on)
                E.wait("vector", ("TE", on))
                if first:
                    ov = E.op("vector", lambda e, col=col, po=po: e.tensor_copy(
                        Oacc[:, col * 128:col * 128 + 128], pO[po][:, 0:128]),
                        inc="VE")
                else:
                    ov = E.op("vector", lambda e, col=col, po=po: e.tensor_tensor(
                        Oacc[:, col * 128:col * 128 + 128],
                        Oacc[:, col * 128:col * 128 + 128], pO[po][:, 0:128],
                        ALU.add), inc="VE")
                st[f"bpo{po}"] = ("VE", ov)

            sc_ctr = [0]

            def push_inst(p):
                pend_b.append(p)
                if len(pend_b) > 2:
                    pend_c.append(stage_sb(pend_b.pop(0)))
                if len(pend_c) > 2:
                    stage_sc(pend_c.pop(0))

            def flush_insts():
                while pend_b:
                    pend_c.append(stage_sb(pend_b.pop(0)))
                while pend_c:
                    stage_sc(pend_c.pop(0))

            for qt in range(min(NT, B_MAX_QT)):
                t0 = qt * 512
                E.wait("sync", st.get("bq_rel"))
                E.op("sync", lambda e, t0=t0: e.dma_start(out=qn_sb[:, :, :],
                     in_=r128(s_qn_h)[:, :, t0:t0 + 512]), inc="BQ", k=16)
                qld = E.op("sync", lambda e, t0=t0: e.dma_start(out=qp_sb[:, :, :],
                     in_=r64(s_qp_h)[:, :, t0:t0 + 512]), inc="BQ", k=16)
                ngr = qt + 1

                def load_gran(g):
                    gi = g % 3
                    E.wait("sync", st.get(f"bg{gi}"))
                    E.op("sync", lambda e, g=g, gi=gi: e.dma_start(out=kn_g[gi][:, :, :],
                         in_=r128(s_kn_h)[:, :, g * 512:g * 512 + 512]), inc=f"BK{gi}", k=16)
                    n = E.op("sync", lambda e, g=g, gi=gi: e.dma_start(out=v_g[gi][:, :, :],
                         in_=s_v_h.rearrange("(s p) d -> p s d", p=128)[:, g * 4:g * 4 + 4, :]),
                         inc=f"BK{gi}", k=16)
                    st[f"bgr{gi}"] = (f"BK{gi}", n)

                load_gran(0)
                for c in range(ngr):
                    if c + 1 < ngr:
                        load_gran(c + 1)
                    k0 = c * 512
                    gci = c % 3
                    rdy0 = st[f"bgr{gci}"]
                    for h in range(16):
                        for qs in range(4):
                            limit = t0 + qs * 128 + 128
                            if k0 >= limit:
                                continue
                            W = min(512, limit - k0)
                            si = inst % 3
                            first = (c == 0)
                            E.wait("tensor", ("BQ", qld))
                            E.wait("tensor", rdy0)
                            E.wait("tensor", st.get(f"bexp{si}"))
                            E.op("tensor", lambda e, h=h, qs=qs, si=si, gci=gci: e.matmul(
                                pS[si][:, :], qn_sb[:, h, qs * 128:qs * 128 + 128],
                                kn_g[gci][:, h, :], start=True, stop=False), inc=None)
                            E.op("tensor", lambda e, h=h, qs=qs, si=si, c=c: e.matmul(
                                pS[si][:, :], qp_sb[:, h, qs * 128:qs * 128 + 128],
                                krope[:, c * 512:c * 512 + 512], start=False, stop=True),
                                inc="TE")
                            smm = E.cnt("TE")
                            diag = (k0 + W == limit)
                            E.wait("vector", ("TE", smm))
                            if diag:
                                moff = W - 128
                                E.op("vector", lambda e, si=si, moff=moff: e.tensor_tensor(
                                    pS[si][:, moff:moff + 128], pS[si][:, moff:moff + 128],
                                    tm_sb[:, :], ALU.add), inc="VE")
                            msk = ("VE", E.cnt("VE"))
                            E.wait("scalar", ("TE", smm))
                            if diag:
                                E.wait("scalar", msk)
                            E.wait("scalar", st.get(f"bcs{si}"))
                            E.wait("scalar", st.get(f"bprb{si}"))
                            ex = E.op("scalar", lambda e, si=si, W=W: e.activation(
                                probs[si][:, 0:W], pS[si][:, 0:W], AF.Exp,
                                bias=mshift[:, 0:1], scale=SCALE,
                                accum_out=cs_sb[:, si:si + 1]), inc="AE")
                            st[f"bexp{si}"] = ("AE", ex)
                            col = h * 4 + qs
                            E.wait("vector", ("AE", ex))
                            if first:
                                lv = E.op("vector", lambda e, col=col, si=si: e.tensor_copy(
                                    l_sb[:, col:col + 1], cs_sb[:, si:si + 1]), inc="VE")
                            else:
                                lv = E.op("vector", lambda e, col=col, si=si: e.tensor_tensor(
                                    l_sb[:, col:col + 1], l_sb[:, col:col + 1],
                                    cs_sb[:, si:si + 1], ALU.add), inc="VE")
                            st[f"bcs{si}"] = ("VE", lv)
                            nsub = (W + 127) // 128
                            rels = [f"bg{gci}"]
                            if c == ngr - 1:
                                rels.append("bq_rel")
                            push_inst((si, W, nsub, c, h, qs, col, first, ex, rels))
                            inst += 1
                flush_insts()
                if not B_FIN:
                    continue
                E.sbar("vector", "VE")
                E.op("vector", lambda e: e.reciprocal(rl_sb[:, :], l_sb[:, :]), inc="VE")
                E.sbar("vector", "VE")
                E.wait("vector", st.get("bat_rel"))
                for h in range(16):
                    for qs in range(4):
                        col = h * 4 + qs
                        E.wait("vector", st.get("bobf"))
                        ob = E.op("vector", lambda e, col=col: e.tensor_scalar_mul(
                            obf[:, :], Oacc[:, col * 128:col * 128 + 128],
                            rl_sb[:, col:col + 1]), inc="VE")
                        E.wait("tensor", ("VE", ob))
                        E.wait("tensor", st.get("bfinc"))
                        E.wait("tensor", st.get("bpp0"))
                        E.op("tensor", lambda e, qs=qs: e.transpose(
                            pP[0][:, qs * 128:qs * 128 + 128], obf[:, :], id_sb[:, :]),
                            inc="TE")
                        tn = E.cnt("TE")
                        st["bobf"] = ("TE", tn)
                        E.wait("vector", ("TE", tn))
                        cpa = E.op("vector", lambda e, h=h, qs=qs: e.tensor_copy(
                            att_st[:, h, qs * 128:qs * 128 + 128],
                            pP[0][:, qs * 128:qs * 128 + 128]), inc="VE")
                        st["bfinc"] = ("VE", cpa)
                        st["bpp0"] = ("VE", cpa)
                fin = ("VE", E.cnt("VE"))
                E.wait("sync", fin)
                atn = E.op("sync", lambda e, t0=t0: e.dma_start(
                    out=r128(s_at_h)[:, :, t0:t0 + 512], in_=att_st[:, :, :]), inc="SAT", k=16)
                st["bat_rel"] = ("SAT", atn)
            B_f = {"sync": ("SAT", E.cnt("SAT")), "tensor": ("TE", E.cnt("TE")),
                   "vector": ("VE", E.cnt("VE")), "scalar": ("AE", E.cnt("AE"))}
        E.barrier(B_f)

        with contextlib.ExitStack() as pst:
            psb = lambda name, shape, dt: pst.enter_context(nc.sbuf_tensor(name, shape, dt))
            pps = lambda name, shape, dt: pst.enter_context(nc.psum_tensor(name, shape, dt))
            arena = [psb(f"arena{i}", [128, 16, 512], BF16) for i in range(3)]
            xc_sb = psb("xc_sbC", [128, 4, D], F32)
            yz_sb = psb("yz_sb", [128, 4, D], F32)
            gT_sb = psb("gT_sb", [128, 32, 512], BF16)
            h2_bf = psb("h2_bf", [128, D], BF16)
            h2T_sb = psb("h2T_sb", [128, 16, 512], BF16)
            sg_sb = [psb(f"sg_sb{i}", [128, 512], BF16) for i in range(2)]
            t_scr = psb("t_scr", [128, D], F32)
            out_sb = psb("out_sb", [128, D], F32)
            sums3 = psb("sums3", [128, 1], F32)
            ss3 = psb("ss3", [128, 1], F32)
            nmu3 = psb("nmu3", [128, 1], F32)
            rst3 = psb("rst3", [128, 1], F32)
            s43 = (sums3[:, 0:1], ss3, nmu3[:, 0:1], rst3[:, 0:1])
            pG = [pps(f"pG{i}", [128, 512], F32) for i in range(2)]
            pU = [pps(f"pU{i}", [128, 512], F32) for i in range(2)]
            pD = [pps(f"pD{i}", [128, 512], F32) for i in range(2)]
            pH = pps("pH", [128, 2048], BF16)

            E.wait("sync", ("CVT", E.cnt("CVT")))
            arena_i = 0

            def arena_load(src_ap, pin=False):
                nonlocal arena_i
                if pin:
                    sl = 2
                else:
                    sl = arena_i % 2
                    arena_i += 1
                E.wait("sync", st.get(f"car{sl}"))
                n = E.op("sync", lambda e, sl=sl, src_ap=src_ap: e.dma_start(
                    out=arena[sl][:, :, :], in_=src_ap), inc=f"CA{sl}", k=16)
                return sl, (f"CA{sl}", n)

            for t in range(min(NT, C_MAX_T)):
                t0 = t * 512
                ti = t % 2
                sl_a, af_a = arena_load(r128(s_at_h)[:, :, t0:t0 + 512], pin=True)
                E.wait("sync", st.get("cx_rel"))
                for qs in range(4):
                    xn = E.op("sync", lambda e, t0=t0, qs=qs: e.dma_start(
                        out=xc_sb[:, qs, :], in_=x_h[t0 + qs * 128:t0 + qs * 128 + 128, :]),
                        inc=f"CX{ti}", k=16)
                cx_rdy = (f"CX{ti}", xn)
                for fo in range(4):
                    sl, af = arena_load(r128(wo_b_h)[:, :, fo * 512:fo * 512 + 512])
                    E.wait("tensor", af)
                    E.wait("tensor", af_a)
                    for qs in range(4):
                        pi = (fo * 4 + qs) % 2
                        E.wait("tensor", st.get(f"cpg{pi}"))
                        for kc in range(16):
                            E.op("tensor", lambda e, kc=kc, qs=qs, sl=sl, pi=pi, sl_a=sl_a: e.matmul(
                                pG[pi][:, :], arena[sl_a][:, kc, qs * 128:qs * 128 + 128],
                                arena[sl][:, kc, :], start=(kc == 0), stop=(kc == 15)),
                                inc=("TE" if kc == 15 else None))
                        mmn = E.cnt("TE")
                        if qs == 3:
                            st[f"car{sl}"] = ("TE", mmn)
                            if fo == 3:
                                st[f"car{sl_a}"] = ("TE", mmn)
                        E.wait("vector", ("TE", mmn))
                        E.wait("vector", st.get("cyz_w"))
                        ev = E.op("vector", lambda e, qs=qs, fo=fo, pi=pi: e.tensor_copy(
                            yz_sb[:, qs, fo * 512:fo * 512 + 512], pG[pi][:, :]), inc="VE")
                        st[f"cpg{pi}"] = ("VE", ev)
                yrdy = ("VE", E.cnt("VE"))
                if dbg:
                    E.wait("sync", yrdy)
                    E.op("sync", lambda e, t0=t0: e.dma_start(
                        out=d_wo_h.rearrange("(a p) d -> p a d", p=128)[:, t0 // 128:t0 // 128 + 4, :],
                        in_=yz_sb[:, :, :]), inc="DBG", k=16)
                for qs in range(4):
                    E.wait("vector", st.get("ct_rel"))
                    apf = ln_chain(yz_sb[:, qs, :], t_scr[:, :], D, epsln, s43, "lnC",
                                   src_f=(yrdy if qs == 0 else None))
                    E.wait("gpsimd", apf)
                    E.wait("gpsimd", cx_rdy)
                    gadd = E.op("gpsimd", lambda e, qs=qs: e.tensor_tensor(
                        xc_sb[:, qs, :], xc_sb[:, qs, :], t_scr[:, :], ALU.add), inc="GE")
                    st["ct_rel"] = ("GE", gadd)
                    E.wait("vector", ("GE", gadd))
                    E.wait("scalar", ("GE", gadd))
                    E.wait("vector", st.get("ch2"))
                    apf2 = ln_chain(xc_sb[:, qs, :], h2_bf[:, :],
                                    D, epsln, s43, "lnC", src_f=None)
                    E.wait("tensor", apf2)
                    E.wait("tensor", st.get("cpH"))
                    for cc in range(16):
                        E.op("tensor", lambda e, cc=cc: e.transpose(
                            pH[:, cc * 128:(cc + 1) * 128], h2_bf[:, cc * 128:(cc + 1) * 128],
                            id_sb[:, :]), inc=("TE" if cc == 15 else None))
                    tn = E.cnt("TE")
                    st["ch2"] = ("TE", tn)
                    E.wait("vector", ("TE", tn))
                    E.wait("vector", st.get("ch2T"))
                    cph = E.op("vector", lambda e, qs=qs: e.tensor_copy(
                        h2T_sb[:, :, qs * 128:qs * 128 + 128],
                        pH[:, :]), inc="VE")
                    st["cpH"] = ("VE", cph)
                st["cyz_w2"] = ("VE", E.cnt("VE"))
                if dbg:
                    E.wait("sync", st.get("ct_rel"))
                    E.op("sync", lambda e, t0=t0: e.dma_start(
                        out=d_x2_h.rearrange("(a p) d -> p a d", p=128)[:, t0 // 128:t0 // 128 + 4, :],
                        in_=xc_sb[:, :, :]), inc="DBG", k=16)
                h2T_rdy = ("VE", E.cnt("VE"))
                E.wait("tensor", h2T_rdy)
                for half in range(2):
                    hb = half * 32
                    for hcg in range(8):
                        slg, afg = arena_load(
                            r128(wg_b_h)[:, :, (hb + hcg * 4) * 128:(hb + hcg * 4) * 128 + 512])
                        slu, afu = arena_load(
                            r128(wu_b_h)[:, :, (hb + hcg * 4) * 128:(hb + hcg * 4) * 128 + 512])
                        for hl in range(4):
                            hc = hcg * 4 + hl
                            pi = hc % 2
                            E.wait("tensor", afg)
                            E.wait("tensor", st.get(f"cpg{pi}"))
                            for kc in range(16):
                                E.op("tensor", lambda e, kc=kc, hl=hl, slg=slg, pi=pi: e.matmul(
                                    pG[pi][:, :], arena[slg][:, kc, hl * 128:hl * 128 + 128],
                                    h2T_sb[:, kc, :], start=(kc == 0), stop=(kc == 15)),
                                    inc=("TE" if kc == 15 else None))
                            gmm = E.cnt("TE")
                            E.wait("tensor", afu)
                            E.wait("tensor", st.get(f"cpu{pi}"))
                            for kc in range(16):
                                E.op("tensor", lambda e, kc=kc, hl=hl, slu=slu, pi=pi: e.matmul(
                                    pU[pi][:, :], arena[slu][:, kc, hl * 128:hl * 128 + 128],
                                    h2T_sb[:, kc, :], start=(kc == 0), stop=(kc == 15)),
                                    inc=("TE" if kc == 15 else None))
                            umm = E.cnt("TE")
                            if hl == 3:
                                st[f"car{slg}"] = ("TE", gmm)
                                st[f"car{slu}"] = ("TE", umm)
                            E.wait("scalar", ("TE", gmm))
                            E.wait("scalar", st.get(f"csg{pi}"))
                            sg_n = E.op("scalar", lambda e, pi=pi: e.activation(
                                sg_sb[pi][:, :], pG[pi][:, :], AF.Silu), inc="AE")
                            st[f"cpg{pi}"] = ("AE", sg_n)
                            E.wait("vector", ("AE", sg_n))
                            E.wait("vector", ("TE", umm))
                            E.wait("vector", st.get("cgT_rel"))
                            gv = E.op("vector", lambda e, hc=hc, pi=pi: e.tensor_tensor(
                                gT_sb[:, hc % 32, :], sg_sb[pi][:, :], pU[pi][:, :], ALU.mult),
                                inc="VE")
                            st[f"csg{pi}"] = ("VE", gv)
                            st[f"cpu{pi}"] = ("VE", gv)
                    gT_rdy = ("VE", E.cnt("VE"))
                    E.wait("tensor", gT_rdy)
                    for fo in range(4):
                        wdsl = []
                        for j in range(2):
                            slw, afw = arena_load(
                                r128(wd_b_h)[:, (hb + j * 16):(hb + j * 16) + 16,
                                             fo * 512:fo * 512 + 512])
                            wdsl.append((slw, afw))
                        for qs in range(4):
                            pi = qs % 2
                            E.wait("tensor", st.get(f"cpd{pi}"))
                            for j in range(2):
                                slw, afw = wdsl[j]
                                E.wait("tensor", afw)
                                for hl in range(16):
                                    hc = j * 16 + hl
                                    E.op("tensor", lambda e, hc=hc, hl=hl, qs=qs, slw=slw, pi=pi, j=j: e.matmul(
                                        pD[pi][:, :], gT_sb[:, hc, qs * 128:qs * 128 + 128],
                                        arena[slw][:, hl, :],
                                        start=(j == 0 and hl == 0), stop=(j == 1 and hl == 15)),
                                        inc=("TE" if (j == 1 and hl == 15) else None))
                            dmm = E.cnt("TE")
                            if qs == 3:
                                for (slw, afw) in wdsl:
                                    st[f"car{slw}"] = ("TE", dmm)
                                if fo == 3:
                                    st["cgT_rel"] = ("TE", dmm)
                            E.wait("vector", ("TE", dmm))
                            E.wait("vector", st.get("cyz_w2"))
                            if half == 0:
                                ev = E.op("vector", lambda e, qs=qs, fo=fo, pi=pi: e.tensor_copy(
                                    yz_sb[:, qs, fo * 512:fo * 512 + 512], pD[pi][:, :]),
                                    inc="VE")
                            else:
                                ev = E.op("vector", lambda e, qs=qs, fo=fo, pi=pi: e.tensor_tensor(
                                    yz_sb[:, qs, fo * 512:fo * 512 + 512],
                                    yz_sb[:, qs, fo * 512:fo * 512 + 512], pD[pi][:, :],
                                    ALU.add), inc="VE")
                            st[f"cpd{pi}"] = ("VE", ev)
                zrdy = ("VE", E.cnt("VE"))
                if dbg:
                    E.wait("sync", zrdy)
                    E.op("sync", lambda e, t0=t0: e.dma_start(
                        out=d_z_h.rearrange("(a p) d -> p a d", p=128)[:, t0 // 128:t0 // 128 + 4, :],
                        in_=yz_sb[:, :, :]), inc="DBG", k=16)
                for qs in range(4):
                    E.wait("vector", st.get("ct_rel"))
                    apf = ln_chain(yz_sb[:, qs, :], t_scr[:, :], D, epsln, s43, "lnC",
                                   src_f=(zrdy if qs == 0 else None))
                    E.wait("gpsimd", apf)
                    E.wait("gpsimd", st.get("cout"))
                    E.sbar("gpsimd", "GE")
                    gadd = E.op("gpsimd", lambda e, qs=qs: e.tensor_tensor(
                        out_sb[:, :], xc_sb[:, qs, :], t_scr[:, :], ALU.add), inc="GE")
                    st["ct_rel"] = ("GE", gadd)
                    if qs == 3:
                        st["cx_rel"] = ("GE", gadd)
                    E.wait("sync", ("GE", gadd))
                    yo = E.op("sync", lambda e, t0=t0, qs=qs: e.dma_start(
                        out=y_h[t0 + qs * 128:t0 + qs * 128 + 128, :], in_=out_sb[:, :]),
                        inc="YO", k=16)
                    st["cout"] = ("YO", yo)
                st["cyz_w"] = ("VE", E.cnt("VE"))
            E.wait("sync", ("YO", E.cnt("YO")))
            if dbg:
                E.wait("sync", ("DBG", E.cnt("DBG")))

        with nc.Block() as block:
            @block.sync
            def _(e):
                for f in E.ops["sync"]:
                    f(e)

            @block.tensor
            def _(e):
                for f in E.ops["tensor"]:
                    f(e)

            @block.vector
            def _(e):
                for f in E.ops["vector"]:
                    f(e)

            @block.scalar
            def _(e):
                for f in E.ops["scalar"]:
                    f(e)

            @block.gpsimd
            def _(e):
                for f in E.ops["gpsimd"]:
                    f(e)

    return nc


def _prep_inputs(inputs, S):
    T = lambda a: np.ascontiguousarray(np.asarray(a).T.astype(np.float32))
    wqr = T(inputs["L0_attn_wq_rope"])
    tq = np.empty(NH * DR, dtype=np.int64)
    for d in range(NH * DR):
        H, r = d // DR, d % DR
        tq[d] = (512 if r % 2 else 0) + H * 32 + r // 2
    wqr_p = np.empty_like(wqr)
    wqr_p[:, tq] = wqr
    wkr = T(inputs["L0_attn_wk_rope"])
    tk = np.empty(DR, dtype=np.int64)
    for r in range(DR):
        tk[r] = (32 if r % 2 else 0) + r // 2
    wkr_p = np.empty_like(wkr)
    wkr_p[:, tk] = wkr
    ident = np.eye(128, dtype=np.float32)
    ii = np.arange(128)
    trimask = np.where(ii[None, :] <= ii[:, None], 0.0, NEGBIG).astype(np.float32)
    inv = 10000.0 ** (-(np.arange(32, dtype=np.float64) * 2.0 / DR))
    fr = np.outer(inv, np.arange(S, dtype=np.float64))
    rcos = np.tile(np.cos(fr), (4, 1)).astype(np.float32)
    rsin = np.tile(np.sin(fr), (4, 1)).astype(np.float32)
    return dict(
        wqd=T(inputs["L0_attn_wq_down"]), wqu=T(inputs["L0_attn_wq_up"]), wqr=wqr_p,
        qnw=np.ascontiguousarray(np.asarray(inputs["L0_attn_q_norm_w"], dtype=np.float32).reshape(QR // 128, 128).T),
        wkd=T(inputs["L0_attn_wkv_down"]),
        kvw=np.ascontiguousarray(np.asarray(inputs["L0_attn_kv_norm_w"], dtype=np.float32).reshape(KVR // 128, 128).T),
        wku=T(inputs["L0_attn_wkv_up"]), wkr=wkr_p, wo=T(inputs["L0_attn_wo"]),
        wg=T(inputs["L0_ffn_w_gate"]), wu=T(inputs["L0_ffn_w_up"]),
        wd=T(inputs["L0_ffn_w_down"]), ident=ident, trimask=trimask,
        rcos=rcos, rsin=rsin)


def kernel(**inputs):
    S = 4096
    x = np.asarray(inputs["L0_attn_x"])
    B = x.shape[0]
    common = _prep_inputs(inputs, S)
    in_maps = [dict(common, x=np.ascontiguousarray(x[c], dtype=np.float32)) for c in range(B)]
    nc = build_program(S)
    res = run_bass_kernel_spmd(nc, in_maps, core_ids=list(range(B)))
    out = np.stack([np.asarray(res.results[c]["y"]) for c in range(B)], axis=0)
    return out.astype(np.float32)


# version 04
# speedup vs baseline: 1.3938x; 1.0422x over previous
import math
import contextlib
import numpy as np
import concourse.bass as bass
import concourse.mybir as mybir
from concourse.bass_utils import run_bass_kernel_spmd

F32 = mybir.dt.float32
BF16 = mybir.dt.bfloat16
AF = mybir.ActivationFunctionType
ALU = mybir.AluOpType
AX = mybir.AxisListType

D = 2048
NH = 16
QR = 1536
KVR = 512
DN = 128
DR = 64
HID = 8192
SCALE = 1.0 / math.sqrt(DN + DR)
MSHIFT = -10.0
EPS_LN = 1e-5
EPS_RMS = 1e-6
NEGBIG = -1.0e30
B_MAX_QT = 10**9
B_MAX_CH = 10**9
B_PREFETCH = True
B_H_MAX = 16
B_FIN = True
C_MAX_T = 10**9
A_MAX_T = 10**9


class Emit:
    ENGINES = ("sync", "tensor", "vector", "scalar", "gpsimd")

    def __init__(self, nc, stack):
        self.nc = nc
        self.stack = stack
        self.ops = {e: [] for e in self.ENGINES}
        self.h = {}
        self.n = {}

    def sem(self, name):
        if name not in self.h:
            self.h[name] = self.stack.enter_context(self.nc.semaphore(name))
            self.n[name] = 0
        return self.h[name]

    def op(self, eng, fn, inc=None, k=1):
        if inc is None:
            self.ops[eng].append(fn)
            return None
        hh = self.sem(inc)
        self.n[inc] += k
        self.ops[eng].append(lambda e, f=fn, hh=hh, k=k: f(e).then_inc(hh, k))
        return self.n[inc]

    def wait(self, eng, fence):
        if fence is None:
            return
        name, val = fence
        if val is None or val <= 0:
            return
        hh = self.sem(name)
        self.ops[eng].append(lambda e, hh=hh, v=val: e.wait_ge(hh, v))

    def cnt(self, name):
        self.sem(name)
        return self.n[name]

    def sbar(self, eng, ctr):
        self.wait(eng, (ctr, self.n.get(ctr, 0)))

    def barrier(self, fences):
        for eng in self.ENGINES:
            for other, f in fences.items():
                if other != eng:
                    self.wait(eng, f)


def build_program(S, dbg=False):
    nc = bass.Bass("TRN2", target_bir_lowering=False)
    NT = S // 512
    skind = "ExternalOutput" if dbg else "Internal"

    x_h = nc.dram_tensor("x", [S, D], F32, kind="ExternalInput").ap()
    wqd_h = nc.dram_tensor("wqd", [D, QR], F32, kind="ExternalInput").ap()
    wqu_h = nc.dram_tensor("wqu", [QR, NH * DN], F32, kind="ExternalInput").ap()
    wqr_h = nc.dram_tensor("wqr", [QR, NH * DR], F32, kind="ExternalInput").ap()
    qnw_h = nc.dram_tensor("qnw", [128, QR // 128], F32, kind="ExternalInput").ap()
    wkd_h = nc.dram_tensor("wkd", [D, KVR], F32, kind="ExternalInput").ap()
    kvw_h = nc.dram_tensor("kvw", [128, KVR // 128], F32, kind="ExternalInput").ap()
    wku_h = nc.dram_tensor("wku", [KVR, NH * 256], F32, kind="ExternalInput").ap()
    wkr_h = nc.dram_tensor("wkr", [D, DR], F32, kind="ExternalInput").ap()
    wo_h = nc.dram_tensor("wo", [D, D], F32, kind="ExternalInput").ap()
    wg_h = nc.dram_tensor("wg", [D, HID], F32, kind="ExternalInput").ap()
    wu_h = nc.dram_tensor("wu", [D, HID], F32, kind="ExternalInput").ap()
    wd_h = nc.dram_tensor("wd", [HID, D], F32, kind="ExternalInput").ap()
    id_h = nc.dram_tensor("ident", [128, 128], F32, kind="ExternalInput").ap()
    tm_h = nc.dram_tensor("trimask", [128, 128], F32, kind="ExternalInput").ap()
    rc_h = nc.dram_tensor("rcos", [128, S], F32, kind="ExternalInput").ap()
    rs_h = nc.dram_tensor("rsin", [128, S], F32, kind="ExternalInput").ap()
    y_h = nc.dram_tensor("y", [S, D], F32, kind="ExternalOutput").ap()
    s_h_h = nc.dram_tensor("s_h", [D, S], BF16, kind=skind).ap()
    s_qn_h = nc.dram_tensor("s_qn", [NH * DN, S], BF16, kind=skind).ap()
    s_qp_h = nc.dram_tensor("s_qp", [NH * DR, S], BF16, kind=skind).ap()
    s_kn_h = nc.dram_tensor("s_kn", [NH * DN, S], BF16, kind=skind).ap()
    s_v_h = nc.dram_tensor("s_v", [S, NH * DN], BF16, kind=skind).ap()
    s_at_h = nc.dram_tensor("s_at", [D, S], BF16, kind=skind).ap()
    if dbg:
        d_wo_h = nc.dram_tensor("d_wo", [S, D], F32, kind="ExternalOutput").ap()
        d_x2_h = nc.dram_tensor("d_x2", [S, D], F32, kind="ExternalOutput").ap()
        d_z_h = nc.dram_tensor("d_z", [S, D], F32, kind="ExternalOutput").ap()
    wo_b_h = nc.dram_tensor("wo_b", [D, D], BF16, kind="Internal").ap()
    wg_b_h = nc.dram_tensor("wg_b", [D, HID], BF16, kind="Internal").ap()
    wu_b_h = nc.dram_tensor("wu_b", [D, HID], BF16, kind="Internal").ap()
    wd_b_h = nc.dram_tensor("wd_b", [HID, D], BF16, kind="Internal").ap()

    r128 = lambda ap_: ap_.rearrange("(c p) t -> p c t", p=128)
    r64 = lambda ap_: ap_.rearrange("(c p) t -> p c t", p=64)

    stack = contextlib.ExitStack()
    with stack:
        E = Emit(nc, stack)
        sb = lambda name, shape, dt: stack.enter_context(nc.sbuf_tensor(name, shape, dt))
        st = {}

        id_sb = sb("id_sb", [128, 128], BF16)
        tm_sb = sb("tm_sb", [128, 128], F32)
        krope = sb("krope", [64, S], BF16)
        epsln = sb("epsln", [128, 1], F32)
        epsrm = sb("epsrm", [128, 1], F32)
        mshift = sb("mshift", [128, 1], F32)
        junk = sb("junk", [128, 2048], BF16)

        E.op("gpsimd", lambda e: e.dma_start(out=id_sb[:, :], in_=id_h[:, :]), inc="LW0", k=16)
        E.op("gpsimd", lambda e: e.dma_start(out=tm_sb[:, :], in_=tm_h[:, :]), inc="LW0", k=16)
        E.op("gpsimd", lambda e: e.memset(epsln[:, :], EPS_LN), inc="GE")
        E.op("gpsimd", lambda e: e.memset(epsrm[:, :], EPS_RMS), inc="GE")
        E.op("gpsimd", lambda e: e.memset(mshift[:, :], MSHIFT), inc="GE")
        P0_LW = ("LW0", E.cnt("LW0"))
        P0_GE = ("GE", E.cnt("GE"))

        def ln_chain(src, out, width, eps_tile, s4, tag, src_f=None, rms=False,
                     ss_ready=None, nss=1):
            sums, ss, nmu, rst = s4
            prev = st.get(tag)
            if src_f is not None:
                E.wait("vector", src_f)
                E.wait("scalar", src_f)
            if prev is not None:
                E.wait("vector", ("AE", prev[1]))
                E.wait("scalar", ("VE", prev[0]))
            if not rms:
                E.op("vector", lambda e: e.reduce_sum(sums, src, axis=AX.X), inc="VE")
            if ss_ready is None:
                E.sbar("scalar", "AE")
                a_sq = E.op("scalar", lambda e: e.activation(junk[:, 0:width], src, AF.Square,
                                                             accum_out=ss[:, 0:1]), inc="AE")
            else:
                a_sq = ss_ready[1]
                E.wait("scalar", ss_ready)
            E.wait("vector", ("AE", a_sq))
            if nss > 1:
                E.op("vector", lambda e: e.reduce_sum(sums if rms else rst, ss[:, 0:nss],
                                                      axis=AX.X), inc="VE")
            if rms:
                srcss = sums if nss > 1 else ss
                E.sbar("vector", "VE")
                E.op("vector", lambda e: e.tensor_scalar_mul(rst, srcss[:, 0:1], 1.0 / width),
                     inc="VE")
            else:
                E.sbar("vector", "VE")
                E.op("vector", lambda e: e.tensor_scalar_mul(nmu, sums, -1.0 / width), inc="VE")
                if nss > 1:
                    E.op("vector", lambda e: e.tensor_scalar_mul(rst, rst, 1.0 / width),
                         inc="VE")
                else:
                    E.op("vector", lambda e: e.tensor_scalar_mul(rst, ss[:, 0:1], 1.0 / width),
                         inc="VE")
                E.sbar("vector", "VE")
                E.op("vector", lambda e: e.tensor_tensor(sums, nmu, nmu, ALU.mult), inc="VE")
                E.sbar("vector", "VE")
                E.op("vector", lambda e: e.tensor_tensor(rst, rst, sums, ALU.subtract), inc="VE")
            v3 = E.cnt("VE")
            E.wait("scalar", ("VE", v3))
            a_rt = E.op("scalar", lambda e: e.activation(rst, rst, AF.Sqrt,
                                                         bias=eps_tile[:, 0:1]), inc="AE")
            E.wait("vector", ("AE", a_rt))
            E.op("vector", lambda e: e.reciprocal(rst, rst), inc="VE")
            E.sbar("vector", "VE")
            if rms:
                vap = E.op("vector", lambda e: e.tensor_scalar_mul(out, src, rst[:, 0:1]),
                           inc="VE")
            else:
                vap = E.op("vector", lambda e: e.tensor_scalar(out, src, nmu[:, 0:1], rst[:, 0:1],
                                                               ALU.add, ALU.mult), inc="VE")
            st[tag] = (vap, a_rt)
            return ("VE", vap)

        conv_jobs = []
        for (src, dst, rows, rstep) in ((wo_h, wo_b_h, D, 1024), (wg_h, wg_b_h, D, 256),
                                        (wu_h, wu_b_h, D, 256), (wd_h, wd_b_h, HID, 1024)):
            for r0 in range(0, rows, rstep):
                conv_jobs.append((src, dst, r0, r0 + rstep))
        conv_it = iter(conv_jobs)

        def drip_conv(n):
            for _ in range(n):
                job = next(conv_it, None)
                if job is None:
                    return
                src, dst, r0, r1 = job
                E.op("gpsimd", lambda e, src=src, dst=dst, r0=r0, r1=r1: e.dma_start(
                    out=dst[r0:r1, :], in_=src[r0:r1, :]), inc="CVT", k=16)

        with contextlib.ExitStack() as pst:
            psb = lambda name, shape, dt: pst.enter_context(nc.sbuf_tensor(name, shape, dt))
            pps = lambda name, shape, dt: pst.enter_context(nc.psum_tensor(name, shape, dt))
            wqd_sb = psb("wqd_sb", [128, 16, QR], BF16)
            wqu_s = [psb(f"wqu_s{i}", [128, 12, 512], BF16) for i in range(2)]
            wqr_sb = psb("wqr_sb", [128, 12, NH * DR], BF16)
            qnw_sb = psb("qnw_sb", [128, 12], F32)
            x_sb = [psb(f"x_sb{i}", [128, D], F32) for i in range(2)]
            h_sb = [psb(f"h_sb{i}", [128, D], BF16) for i in range(2)]
            hT_sb = psb("hT_sb", [128, 16, 512], BF16)
            qc_sb = psb("qc_sb", [128, QR], F32)
            qcn_sb = psb("qcn_sb", [128, QR], BF16)
            qcT_sb = psb("qcT_sb", [128, 12, 512], BF16)
            qn_st = [psb(f"qn_st{i}", [128, 4, 512], BF16) for i in range(2)]
            qpe_sb = psb("qpe_sb", [128, 8, 512], BF16)
            ro_e = psb("ro_e", [128, 512], BF16)
            ro_o = psb("ro_o", [128, 512], BF16)
            rt1 = psb("rt1", [128, 512], BF16)
            rt2 = psb("rt2", [128, 512], BF16)
            rc_sb = psb("rc_sb", [128, 512], F32)
            rs_sb = psb("rs_sb", [128, 512], F32)
            sums = psb("sums", [128, 1], F32)
            ss = psb("ss", [128, 3], F32)
            nmu = psb("nmu", [128, 1], F32)
            rst = psb("rst", [128, 1], F32)
            s4 = (sums[:, 0:1], ss, nmu[:, 0:1], rst[:, 0:1])
            pA = [pps(f"pA{i}", [128, 512], F32) for i in range(2)]
            pT = [pps(f"pT{i}", [128, 2048], BF16) for i in range(2)]

            E.op("gpsimd", lambda e: e.dma_start(out=wqd_sb[:, :, :], in_=r128(wqd_h)),
                 inc="LW1", k=16)
            E.op("gpsimd", lambda e: e.dma_start(out=wqr_sb[:, :, :], in_=r128(wqr_h)),
                 inc="LW1", k=16)
            E.op("gpsimd", lambda e: e.dma_start(out=qnw_sb[:, :], in_=qnw_h[:, :]),
                 inc="LW1", k=16)
            wA1 = ("LW1", E.cnt("LW1"))
            E.wait("vector", wA1)
            E.wait("tensor", wA1)
            E.wait("tensor", P0_LW)
            E.wait("vector", P0_LW)

            def load_wqu(t, g):
                sl = (t * 4 + g) % 2
                E.wait("gpsimd", st.get(f"wqu{sl}"))
                n = E.op("gpsimd", lambda e, g=g, sl=sl: e.dma_start(
                    out=wqu_s[sl][:, :, :],
                    in_=r128(wqu_h)[:, :, g * 512:g * 512 + 512]), inc=f"WQ{sl}", k=16)
                drip_conv(1)
                return sl, (f"WQ{sl}", n)

            for t in range(NT):
                t0 = t * 512
                E.wait("sync", st.get("a1rope"))
                rs_n = None
                E.op("sync", lambda e, t0=t0: e.dma_start(out=rc_sb[:, :],
                     in_=rc_h[:, t0:t0 + 512]), inc="RT1", k=16)
                rs_n = E.op("sync", lambda e, t0=t0: e.dma_start(out=rs_sb[:, :],
                     in_=rs_h[:, t0:t0 + 512]), inc="RT1", k=16)
                for qs in range(4):
                    xi = qs % 2
                    E.wait("sync", st.get(f"a1x{xi}"))
                    xn = E.op("sync", lambda e, t0=t0, qs=qs, xi=xi: e.dma_start(
                        out=x_sb[xi][:, :], in_=x_h[t0 + qs * 128:t0 + qs * 128 + 128, :]),
                        inc=f"XA{xi}", k=16)
                    E.wait("vector", st.get(f"a1h{xi}"))
                    apf = ln_chain(x_sb[xi][:, :], h_sb[xi][:, :], D, epsln, s4, "sA1",
                                   src_f=(f"XA{xi}", xn))
                    st[f"a1x{xi}"] = apf
                    E.wait("tensor", apf)
                    E.wait("tensor", st.get(f"a1pt{qs % 2}"))
                    for c in range(16):
                        E.op("tensor", lambda e, c=c, qs=qs, xi=xi: e.transpose(
                            pT[qs % 2][:, c * 128:(c + 1) * 128],
                            h_sb[xi][:, c * 128:(c + 1) * 128], id_sb[:, :]),
                            inc=("TE" if c == 15 else None))
                    tn = E.cnt("TE")
                    st[f"a1h{xi}"] = ("TE", tn)
                    E.wait("vector", ("TE", tn))
                    E.wait("vector", st.get(f"a1hT{qs}"))
                    E.wait("vector", st.get(f"a1hTs{qs}"))
                    cp = E.op("vector", lambda e, qs=qs: e.tensor_copy(
                        hT_sb[:, :, qs * 128:qs * 128 + 128],
                        pT[qs % 2][:, :]), inc="VE")
                    st[f"a1pt{qs % 2}"] = ("VE", cp)
                    E.wait("sync", ("VE", cp))
                    shn = E.op("sync", lambda e, t0=t0, qs=qs: e.dma_start(
                        out=r128(s_h_h)[:, :, t0 + qs * 128:t0 + qs * 128 + 128],
                        in_=hT_sb[:, :, qs * 128:qs * 128 + 128]), inc=f"SH{qs}", k=16)
                    st[f"a1hTs{qs}"] = (f"SH{qs}", shn)

                    for fo in range(3):
                        pi = (qs * 3 + fo) % 2
                        E.wait("tensor", st.get(f"a1pa{pi}"))
                        E.wait("tensor", ("VE", cp))
                        for kc in range(16):
                            E.op("tensor", lambda e, kc=kc, fo=fo, qs=qs, pi=pi: e.matmul(
                                pA[pi][:, :], hT_sb[:, kc, qs * 128:qs * 128 + 128],
                                wqd_sb[:, kc, fo * 512:fo * 512 + 512],
                                start=(kc == 0), stop=(kc == 15)),
                                inc=("TE" if kc == 15 else None))
                        mmn = E.cnt("TE")
                        E.wait("vector", ("TE", mmn))
                        if fo == 0:
                            E.wait("vector", st.get("a1qcr"))
                        ev = E.op("vector", lambda e, fo=fo, pi=pi: e.tensor_copy(
                            qc_sb[:, fo * 512:fo * 512 + 512], pA[pi][:, :]), inc="VE")
                        st[f"a1pa{pi}"] = ("VE", ev)
                        E.wait("scalar", ("VE", ev))
                        E.wait("scalar", st.get("a1ssr"))
                        E.sbar("scalar", "AE")
                        aq = E.op("scalar", lambda e, fo=fo: e.activation(
                            junk[:, 0:512], qc_sb[:, fo * 512:fo * 512 + 512], AF.Square,
                            accum_out=ss[:, fo:fo + 1]), inc="AE")
                    E.wait("vector", st.get("a1qcn"))
                    apf = ln_chain(qc_sb[:, :], qcn_sb[:, :], QR, epsrm, s4, "sA1",
                                   rms=True, ss_ready=("AE", aq), nss=3)
                    st["a1qcr"] = ("AE", aq)
                    st["a1ssr"] = ("VE", E.cnt("VE"))
                    E.wait("tensor", apf)
                    E.wait("tensor", st.get(f"a1pt{qs % 2}"))
                    for c in range(12):
                        E.op("tensor", lambda e, c=c, qs=qs: e.transpose(
                            pT[qs % 2][:, c * 128:(c + 1) * 128],
                            qcn_sb[:, c * 128:(c + 1) * 128], id_sb[:, :]),
                            inc=("TE" if c == 11 else None))
                    tn = E.cnt("TE")
                    st["a1qcn"] = ("TE", tn)
                    E.wait("vector", ("TE", tn))
                    E.wait("vector", st.get("a1qcT"))
                    for kc in range(12):
                        cp2 = E.op("vector", lambda e, qs=qs, kc=kc: e.tensor_scalar_mul(
                            qcT_sb[:, kc, qs * 128:qs * 128 + 128],
                            pT[qs % 2][:, kc * 128:kc * 128 + 128],
                            qnw_sb[:, kc:kc + 1]), inc="VE")
                    st[f"a1pt{qs % 2}"] = ("VE", cp2)
                    st[f"a1hT{qs}"] = ("TE", E.cnt("TE"))
                qcT_rdy = ("VE", E.cnt("VE"))
                E.wait("tensor", qcT_rdy)
                wq_slots = {}
                wq_slots[0] = load_wqu(t, 0)
                for oc in range(16):
                    g = oc // 4
                    if oc % 4 == 0 and g + 1 < 4:
                        wq_slots[g + 1] = load_wqu(t, g + 1)
                    sl_w, af_w = wq_slots[g]
                    pi = oc % 2
                    E.wait("tensor", st.get(f"a1pa{pi}"))
                    E.wait("tensor", af_w)
                    for kc in range(12):
                        E.op("tensor", lambda e, kc=kc, oc=oc, pi=pi, sl_w=sl_w: e.matmul(
                            pA[pi][:, :], wqu_s[sl_w][:, kc, (oc % 4) * 128:(oc % 4) * 128 + 128],
                            qcT_sb[:, kc, :], start=(kc == 0), stop=(kc == 11)),
                            inc=("TE" if kc == 11 else None))
                    mmn = E.cnt("TE")
                    if oc % 4 == 3:
                        st[f"wqu{sl_w}"] = ("TE", mmn)
                    E.wait("vector", ("TE", mmn))
                    sl, bi = oc % 4, (oc // 4) % 2
                    if sl == 0:
                        E.wait("vector", st.get(f"a1qn{bi}"))
                    ev = E.op("vector", lambda e, sl=sl, bi=bi, pi=pi: e.tensor_copy(
                        qn_st[bi][:, sl, :], pA[pi][:, :]), inc="VE")
                    st[f"a1pa{pi}"] = ("VE", ev)
                    if sl == 3:
                        E.wait("sync", ("VE", ev))
                        stn = E.op("sync", lambda e, t0=t0, oc=oc, bi=bi: e.dma_start(
                            out=r128(s_qn_h)[:, oc - 3:oc + 1, t0:t0 + 512],
                            in_=qn_st[bi][:, :, :]), inc=f"SA{bi}", k=16)
                        st[f"a1qn{bi}"] = (f"SA{bi}", stn)
                E.wait("vector", st.get("a1ropeV"))
                for oc in range(8):
                    pi = oc % 2
                    E.wait("tensor", st.get(f"a1pa{pi}"))
                    for kc in range(12):
                        E.op("tensor", lambda e, kc=kc, oc=oc, pi=pi: e.matmul(
                            pA[pi][:, :], wqr_sb[:, kc, oc * 128:oc * 128 + 128],
                            qcT_sb[:, kc, :], start=(kc == 0), stop=(kc == 11)),
                            inc=("TE" if kc == 11 else None))
                    mmn = E.cnt("TE")
                    E.wait("vector", ("TE", mmn))
                    ev = E.op("vector", lambda e, oc=oc, pi=pi: e.tensor_copy(
                        qpe_sb[:, oc, :], pA[pi][:, :]), inc="VE")
                    st[f"a1pa{pi}"] = ("VE", ev)
                st["a1qcT"] = ("TE", E.cnt("TE"))
                E.wait("vector", ("RT1", rs_n))
                qp_g = s_qp_h.rearrange("(c q r) t -> c q r t", c=4, q=4, r=64)
                for c in range(4):
                    E.wait("vector", st.get("a1roS"))
                    E.op("vector", lambda e, c=c: e.tensor_tensor(rt1[:, :], qpe_sb[:, c, :],
                         rc_sb[:, :], ALU.mult), inc="VE")
                    E.op("vector", lambda e, c=c: e.tensor_tensor(rt2[:, :], qpe_sb[:, c + 4, :],
                         rs_sb[:, :], ALU.mult), inc="VE")
                    E.sbar("vector", "VE")
                    E.op("vector", lambda e: e.tensor_tensor(ro_e[:, :], rt1[:, :], rt2[:, :],
                         ALU.subtract), inc="VE")
                    E.sbar("vector", "VE")
                    E.op("vector", lambda e, c=c: e.tensor_tensor(rt1[:, :], qpe_sb[:, c, :],
                         rs_sb[:, :], ALU.mult), inc="VE")
                    E.op("vector", lambda e, c=c: e.tensor_tensor(rt2[:, :], qpe_sb[:, c + 4, :],
                         rc_sb[:, :], ALU.mult), inc="VE")
                    E.sbar("vector", "VE")
                    rov = E.op("vector", lambda e: e.tensor_tensor(ro_o[:, :], rt1[:, :], rt2[:, :],
                               ALU.add), inc="VE")
                    E.wait("sync", ("VE", rov))
                    E.op("sync", lambda e, c=c, t0=t0: e.dma_start(
                        out=qp_g[c, :, 0:32, t0:t0 + 512], in_=ro_e[:, :]), inc="RO", k=16)
                    ron = E.op("sync", lambda e, c=c, t0=t0: e.dma_start(
                        out=qp_g[c, :, 32:64, t0:t0 + 512], in_=ro_o[:, :]), inc="RO", k=16)
                    st["a1roS"] = ("RO", ron)
                st["a1ropeV"] = ("VE", E.cnt("VE"))
                st["a1rope"] = ("VE", E.cnt("VE"))
            drip_conv(100)
            A1_f = {"sync": ("RO", E.cnt("RO")), "tensor": ("TE", E.cnt("TE")),
                    "vector": ("VE", E.cnt("VE")), "scalar": ("AE", E.cnt("AE")),
                    "gpsimd": ("LW1", E.cnt("LW1"))}
            A1_extra = [(f"SH{q}", E.cnt(f"SH{q}")) for q in range(4)] + \
                       [("SA0", E.cnt("SA0")), ("SA1", E.cnt("SA1"))]
        E.barrier(A1_f)
        for f in A1_extra:
            for eng in Emit.ENGINES:
                E.wait(eng, f)

        with contextlib.ExitStack() as pst:
            psb = lambda name, shape, dt: pst.enter_context(nc.sbuf_tensor(name, shape, dt))
            pps = lambda name, shape, dt: pst.enter_context(nc.psum_tensor(name, shape, dt))
            wkd_sb = psb("wkd_sb", [128, 16, KVR], BF16)
            wku_sb = psb("wku_sb", [128, 4, NH * 256], BF16)
            wkr_sb = psb("wkr_sb", [128, 16, DR], BF16)
            kvw_sb = psb("kvw_sb", [128, 4], F32)
            hT2 = [psb(f"hT2_{i}", [128, 16, 512], BF16) for i in range(2)]
            kvc_sb = psb("kvc_sb", [128, KVR], F32)
            kvn_sb = psb("kvn_sb", [128, KVR], BF16)
            kvT_sb = psb("kvT_sb", [128, 4, 512], BF16)
            kn_st = [psb(f"kn_st{i}", [128, 4, 512], BF16) for i in range(2)]
            v_bf = psb("v_bf", [128, 512], BF16)
            v_st = psb("v_st", [128, 4, NH * DN], BF16)
            rt1b = psb("rt1b", [32, 512], F32)
            rt2b = psb("rt2b", [32, 512], F32)
            rc2 = psb("rc2", [128, 512], F32)
            rs2 = psb("rs2", [128, 512], F32)
            sums2 = psb("sums2", [128, 1], F32)
            ss2 = psb("ss2", [128, 1], F32)
            nmu2 = psb("nmu2", [128, 1], F32)
            rst2 = psb("rst2", [128, 1], F32)
            s42 = (sums2[:, 0:1], ss2, nmu2[:, 0:1], rst2[:, 0:1])
            pA2 = [pps(f"pA2_{i}", [128, 512], F32) for i in range(2)]
            pT2 = [pps(f"pT2_{i}", [128, 2048], BF16) for i in range(2)]
            pV2 = pps("pV2", [128, 512], BF16)
            pK2 = pps("pK2", [128, 512], F32)

            E.op("gpsimd", lambda e: e.dma_start(out=wkd_sb[:, :, :], in_=r128(wkd_h)),
                 inc="LW2", k=16)
            E.op("gpsimd", lambda e: e.dma_start(out=wku_sb[:, :, :], in_=r128(wku_h)),
                 inc="LW2", k=16)
            E.op("gpsimd", lambda e: e.dma_start(out=wkr_sb[:, :, :], in_=r128(wkr_h)),
                 inc="LW2", k=16)
            E.op("gpsimd", lambda e: e.dma_start(out=kvw_sb[:, :], in_=kvw_h[:, :]),
                 inc="LW2", k=16)
            wA2 = ("LW2", E.cnt("LW2"))
            E.wait("vector", wA2)
            for kc in range(4):
                E.op("vector", lambda e, kc=kc: e.tensor_scalar_mul(
                    wku_sb[:, kc, :], wku_sb[:, kc, :], kvw_sb[:, kc:kc + 1]), inc="VE")
            foldkv = ("VE", E.cnt("VE"))
            E.wait("tensor", foldkv)

            for t in range(NT):
                t0 = t * 512
                ti = t % 2
                E.wait("sync", st.get("a2rope"))
                rs_n = None
                E.op("sync", lambda e, t0=t0: e.dma_start(out=rc2[:, :],
                     in_=rc_h[:, t0:t0 + 512]), inc="RT2", k=16)
                rs_n = E.op("sync", lambda e, t0=t0: e.dma_start(out=rs2[:, :],
                     in_=rs_h[:, t0:t0 + 512]), inc="RT2", k=16)
                E.wait("sync", st.get(f"a2h{ti}"))
                hn = E.op("sync", lambda e, t0=t0, ti=ti: e.dma_start(
                    out=hT2[ti][:, :, :], in_=r128(s_h_h)[:, :, t0:t0 + 512]),
                    inc=f"XB{ti}", k=16)
                for qs in range(4):
                    pi = qs % 2
                    E.wait("tensor", st.get(f"a2pa{pi}"))
                    E.wait("tensor", (f"XB{ti}", hn))
                    for kc in range(16):
                        E.op("tensor", lambda e, kc=kc, qs=qs, pi=pi, ti=ti: e.matmul(
                            pA2[pi][:, :], hT2[ti][:, kc, qs * 128:qs * 128 + 128],
                            wkd_sb[:, kc, :], start=(kc == 0), stop=(kc == 15)),
                            inc=("TE" if kc == 15 else None))
                    mmn = E.cnt("TE")
                    E.wait("vector", ("TE", mmn))
                    E.wait("vector", st.get("a2kvr"))
                    ev = E.op("vector", lambda e, pi=pi: e.tensor_copy(kvc_sb[:, :],
                              pA2[pi][:, :]), inc="VE")
                    st[f"a2pa{pi}"] = ("VE", ev)
                    E.wait("vector", st.get("a2kvn"))
                    apf = ln_chain(kvc_sb[:, :], kvn_sb[:, :], KVR, epsrm, s42, "sA2",
                                   src_f=("VE", ev), rms=True)
                    st["a2kvr"] = ("AE", E.cnt("AE"))
                    E.wait("tensor", apf)
                    E.wait("tensor", st.get(f"a2pt{pi}"))
                    for c in range(4):
                        E.op("tensor", lambda e, c=c, pi=pi: e.transpose(
                            pT2[pi][:, c * 128:(c + 1) * 128],
                            kvn_sb[:, c * 128:(c + 1) * 128], id_sb[:, :]),
                            inc=("TE" if c == 3 else None))
                    tn = E.cnt("TE")
                    st["a2kvn"] = ("TE", tn)
                    E.wait("vector", ("TE", tn))
                    E.wait("vector", st.get("a2kvT"))
                    cp = E.op("vector", lambda e, qs=qs, pi=pi: e.tensor_copy(
                        kvT_sb[:, :, qs * 128:qs * 128 + 128],
                        pT2[pi][:, 0:512]), inc="VE")
                    st[f"a2pt{pi}"] = ("VE", cp)
                kvT_rdy = ("VE", E.cnt("VE"))
                E.wait("tensor", kvT_rdy)
                for oc in range(32):
                    pi = oc % 2
                    E.wait("tensor", st.get(f"a2pa{pi}"))
                    for kc in range(4):
                        E.op("tensor", lambda e, kc=kc, oc=oc, pi=pi: e.matmul(
                            pA2[pi][:, :], wku_sb[:, kc, oc * 128:oc * 128 + 128],
                            kvT_sb[:, kc, :], start=(kc == 0), stop=(kc == 3)),
                            inc=("TE" if kc == 3 else None))
                    mmn = E.cnt("TE")
                    h_ = oc // 2
                    E.wait("vector", ("TE", mmn))
                    if oc % 2 == 0:
                        sl, bi = h_ % 4, (h_ // 4) % 2
                        if sl == 0:
                            E.wait("vector", st.get(f"a2kn{bi}"))
                        ev = E.op("vector", lambda e, sl=sl, bi=bi, pi=pi: e.tensor_copy(
                            kn_st[bi][:, sl, :], pA2[pi][:, :]), inc="VE")
                        st[f"a2pa{pi}"] = ("VE", ev)
                        if sl == 3:
                            E.wait("sync", ("VE", ev))
                            stn = E.op("sync", lambda e, t0=t0, h_=h_, bi=bi: e.dma_start(
                                out=r128(s_kn_h)[:, h_ - 3:h_ + 1, t0:t0 + 512],
                                in_=kn_st[bi][:, :, :]), inc=f"SK{bi}", k=16)
                            st[f"a2kn{bi}"] = (f"SK{bi}", stn)
                    else:
                        E.wait("vector", st.get("a2vbf"))
                        ev = E.op("vector", lambda e, pi=pi: e.tensor_copy(
                            v_bf[:, :], pA2[pi][:, :]), inc="VE")
                        st[f"a2pa{pi}"] = ("VE", ev)
                        E.wait("tensor", ("VE", ev))
                        E.wait("tensor", st.get("a2pv"))
                        for ts_ in range(4):
                            E.op("tensor", lambda e, ts_=ts_: e.transpose(
                                pV2[:, ts_ * 128:ts_ * 128 + 128],
                                v_bf[:, ts_ * 128:ts_ * 128 + 128], id_sb[:, :]),
                                inc=("TE" if ts_ == 3 else None))
                        tn = E.cnt("TE")
                        st["a2vbf"] = ("TE", tn)
                        E.wait("vector", ("TE", tn))
                        if h_ == 0:
                            E.wait("vector", st.get("a2vst"))
                        cpv = E.op("vector", lambda e, h_=h_: e.tensor_copy(
                            v_st[:, :, h_ * 128:h_ * 128 + 128],
                            pV2[:, :]), inc="VE")
                        st["a2pv"] = ("VE", cpv)
                        if h_ == 15:
                            E.wait("sync", ("VE", cpv))
                            stn = E.op("sync", lambda e, t0=t0: e.dma_start(
                                out=s_v_h.rearrange("(s p) d -> p s d", p=128)[:, t0 // 128:t0 // 128 + 4, :],
                                in_=v_st[:, :, :]), inc="SV", k=16)
                            st["a2vst"] = ("SV", stn)
                st["a2kvT"] = ("TE", E.cnt("TE"))
                E.wait("tensor", st.get("a2krP"))
                for kc in range(16):
                    E.op("tensor", lambda e, kc=kc, ti=ti: e.matmul(
                        pK2[0:64, :], wkr_sb[:, kc, :], hT2[ti][:, kc, :],
                        start=(kc == 0), stop=(kc == 15)), inc=("TE" if kc == 15 else None))
                mmn = E.cnt("TE")
                st[f"a2h{ti}"] = ("TE", mmn)
                E.wait("vector", ("TE", mmn))
                E.wait("vector", ("RT2", rs_n))
                E.op("vector", lambda e: e.tensor_tensor(rt1b[:, :], pK2[0:32, :], rc2[0:32, :],
                     ALU.mult), inc="VE")
                E.op("vector", lambda e: e.tensor_tensor(rt2b[:, :], pK2[32:64, :], rs2[0:32, :],
                     ALU.mult), inc="VE")
                E.sbar("vector", "VE")
                E.op("vector", lambda e, t0=t0: e.tensor_tensor(krope[0:32, t0:t0 + 512],
                     rt1b[:, :], rt2b[:, :], ALU.subtract), inc="VE")
                E.sbar("vector", "VE")
                E.op("vector", lambda e: e.tensor_tensor(rt1b[:, :], pK2[0:32, :], rs2[0:32, :],
                     ALU.mult), inc="VE")
                E.op("vector", lambda e: e.tensor_tensor(rt2b[:, :], pK2[32:64, :], rc2[0:32, :],
                     ALU.mult), inc="VE")
                E.sbar("vector", "VE")
                krn = E.op("vector", lambda e, t0=t0: e.tensor_tensor(krope[32:64, t0:t0 + 512],
                           rt1b[:, :], rt2b[:, :], ALU.add), inc="VE")
                st["a2krP"] = ("VE", krn)
                st["a2rope"] = ("VE", krn)
            A2_f = {"sync": ("SV", E.cnt("SV")), "tensor": ("TE", E.cnt("TE")),
                    "vector": ("VE", E.cnt("VE")), "scalar": ("AE", E.cnt("AE")),
                    "gpsimd": ("LW2", E.cnt("LW2"))}
            A2_extra = [("SK0", E.cnt("SK0")), ("SK1", E.cnt("SK1"))]
        E.barrier(A2_f)
        for f in A2_extra:
            for eng in Emit.ENGINES:
                E.wait(eng, f)

        with contextlib.ExitStack() as pst:
            psb = lambda name, shape, dt: pst.enter_context(nc.sbuf_tensor(name, shape, dt))
            pps = lambda name, shape, dt: pst.enter_context(nc.psum_tensor(name, shape, dt))
            qn_sb = psb("qn_sb", [128, 16, 512], BF16)
            qp_sb = psb("qp_sb", [64, 16, 512], BF16)
            kn_g = [psb(f"kn_g{i}", [128, 16, 512], BF16) for i in range(3)]
            v_g = [psb(f"v_g{i}", [128, 4, NH * DN], BF16) for i in range(3)]
            probs = [psb(f"probs{i}", [128, 512], BF16) for i in range(3)]
            probsT = [psb(f"probsT{i}", [128, 512], BF16) for i in range(3)]
            Oacc = psb("Oacc", [128, 64 * 128], F32)
            l_sb = psb("l_sb", [128, 64], F32)
            rl_sb = psb("rl_sb", [128, 64], F32)
            cs_sb = psb("cs_sb", [128, 3], F32)
            obf = psb("obf", [128, 128], BF16)
            att_st = psb("att_st", [128, 16, 512], BF16)
            pS = [pps(f"pS{i}", [128, 512], F32) for i in range(3)]
            pP = [pps(f"pP{i}", [128, 512], BF16) for i in range(3)]
            pO = [pps(f"pO{i}", [128, 512], F32) for i in range(2)]

            inst = 0
            pend_b = []
            pend_c = []
            E.wait("scalar", P0_GE)

            def stage_sb(p):
                (si, W, nsub, c_, h, qs, col, first, ex, rels) = p
                E.wait("tensor", ("AE", ex))
                E.wait("tensor", st.get(f"bpp{si}"))
                for j in range(nsub):
                    E.op("tensor", lambda e, si=si, j=j: e.transpose(
                        pP[si][:, j * 128:j * 128 + 128],
                        probs[si][:, j * 128:j * 128 + 128], id_sb[:, :]),
                        inc=("TE" if j == nsub - 1 else None))
                ptn = E.cnt("TE")
                st[f"bprb{si}"] = ("TE", ptn)
                if si != 1:
                    E.wait("vector", ("TE", ptn))
                    E.wait("vector", st.get(f"bptT{si}"))
                    pc = E.op("vector", lambda e, si=si, W=W: e.tensor_copy(
                        probsT[si][:, 0:W], pP[si][:, 0:W]), inc="VE")
                    pcf = ("VE", pc)
                else:
                    E.wait("scalar", ("TE", ptn))
                    E.wait("scalar", st.get(f"bptT{si}"))
                    pc = E.op("scalar", lambda e, si=si, W=W: e.copy(
                        probsT[si][:, 0:W], pP[si][:, 0:W]), inc="AE")
                    pcf = ("AE", pc)
                st[f"bpp{si}"] = pcf
                p2 = p + (pcf,)
                return p2

            def stage_sc(p):
                (si, W, nsub, c_, h, qs, col, first, ex, rels, pcf) = p
                po = c_ % 2 if False else (sc_ctr[0] % 2)
                sc_ctr[0] += 1
                E.wait("tensor", pcf)
                E.wait("tensor", st.get(f"bpo{po}"))
                for j in range(nsub):
                    gi = c_ % 3
                    E.op("tensor", lambda e, si=si, j=j, gi=gi, po=po, h=h, nsub=nsub: e.matmul(
                        pO[po][:, 0:128], probsT[si][:, j * 128:j * 128 + 128],
                        v_g[gi][:, j, h * 128:h * 128 + 128],
                        start=(j == 0), stop=(j == nsub - 1)),
                        inc=("TE" if j == nsub - 1 else None))
                on = E.cnt("TE")
                st[f"bptT{si}"] = ("TE", on)
                for key in rels:
                    st[key] = ("TE", on)
                E.wait("vector", ("TE", on))
                if first:
                    ov = E.op("vector", lambda e, col=col, po=po: e.tensor_copy(
                        Oacc[:, col * 128:col * 128 + 128], pO[po][:, 0:128]),
                        inc="VE")
                else:
                    ov = E.op("vector", lambda e, col=col, po=po: e.tensor_tensor(
                        Oacc[:, col * 128:col * 128 + 128],
                        Oacc[:, col * 128:col * 128 + 128], pO[po][:, 0:128],
                        ALU.add), inc="VE")
                st[f"bpo{po}"] = ("VE", ov)

            sc_ctr = [0]

            def push_inst(p):
                pend_b.append(p)
                if len(pend_b) > 2:
                    pend_c.append(stage_sb(pend_b.pop(0)))
                if len(pend_c) > 2:
                    stage_sc(pend_c.pop(0))

            def flush_insts():
                while pend_b:
                    pend_c.append(stage_sb(pend_b.pop(0)))
                while pend_c:
                    stage_sc(pend_c.pop(0))

            for qt in range(min(NT, B_MAX_QT)):
                t0 = qt * 512
                E.wait("sync", st.get("bq_rel"))
                E.op("sync", lambda e, t0=t0: e.dma_start(out=qn_sb[:, :, :],
                     in_=r128(s_qn_h)[:, :, t0:t0 + 512]), inc="BQ", k=16)
                qld = E.op("sync", lambda e, t0=t0: e.dma_start(out=qp_sb[:, :, :],
                     in_=r64(s_qp_h)[:, :, t0:t0 + 512]), inc="BQ", k=16)
                ngr = qt + 1

                def load_gran(g):
                    gi = g % 3
                    E.wait("sync", st.get(f"bg{gi}"))
                    E.op("sync", lambda e, g=g, gi=gi: e.dma_start(out=kn_g[gi][:, :, :],
                         in_=r128(s_kn_h)[:, :, g * 512:g * 512 + 512]), inc=f"BK{gi}", k=16)
                    n = E.op("sync", lambda e, g=g, gi=gi: e.dma_start(out=v_g[gi][:, :, :],
                         in_=s_v_h.rearrange("(s p) d -> p s d", p=128)[:, g * 4:g * 4 + 4, :]),
                         inc=f"BK{gi}", k=16)
                    st[f"bgr{gi}"] = (f"BK{gi}", n)

                load_gran(0)
                for c in range(ngr):
                    if c + 1 < ngr:
                        load_gran(c + 1)
                    k0 = c * 512
                    gci = c % 3
                    rdy0 = st[f"bgr{gci}"]
                    for h in range(16):
                        for qs in range(4):
                            limit = t0 + qs * 128 + 128
                            if k0 >= limit:
                                continue
                            W = min(512, limit - k0)
                            si = inst % 3
                            first = (c == 0)
                            E.wait("tensor", ("BQ", qld))
                            E.wait("tensor", rdy0)
                            E.wait("tensor", st.get(f"bexp{si}"))
                            E.op("tensor", lambda e, h=h, qs=qs, si=si, gci=gci: e.matmul(
                                pS[si][:, :], qn_sb[:, h, qs * 128:qs * 128 + 128],
                                kn_g[gci][:, h, :], start=True, stop=False), inc=None)
                            E.op("tensor", lambda e, h=h, qs=qs, si=si, c=c: e.matmul(
                                pS[si][:, :], qp_sb[:, h, qs * 128:qs * 128 + 128],
                                krope[:, c * 512:c * 512 + 512], start=False, stop=True),
                                inc="TE")
                            smm = E.cnt("TE")
                            diag = (k0 + W == limit)
                            E.wait("vector", ("TE", smm))
                            if diag:
                                moff = W - 128
                                E.op("vector", lambda e, si=si, moff=moff: e.tensor_tensor(
                                    pS[si][:, moff:moff + 128], pS[si][:, moff:moff + 128],
                                    tm_sb[:, :], ALU.add), inc="VE")
                            msk = ("VE", E.cnt("VE"))
                            E.wait("scalar", ("TE", smm))
                            if diag:
                                E.wait("scalar", msk)
                            E.wait("scalar", st.get(f"bcs{si}"))
                            E.wait("scalar", st.get(f"bprb{si}"))
                            ex = E.op("scalar", lambda e, si=si, W=W: e.activation(
                                probs[si][:, 0:W], pS[si][:, 0:W], AF.Exp,
                                bias=mshift[:, 0:1], scale=SCALE,
                                accum_out=cs_sb[:, si:si + 1]), inc="AE")
                            st[f"bexp{si}"] = ("AE", ex)
                            col = h * 4 + qs
                            E.wait("vector", ("AE", ex))
                            if first:
                                lv = E.op("vector", lambda e, col=col, si=si: e.tensor_copy(
                                    l_sb[:, col:col + 1], cs_sb[:, si:si + 1]), inc="VE")
                            else:
                                lv = E.op("vector", lambda e, col=col, si=si: e.tensor_tensor(
                                    l_sb[:, col:col + 1], l_sb[:, col:col + 1],
                                    cs_sb[:, si:si + 1], ALU.add), inc="VE")
                            st[f"bcs{si}"] = ("VE", lv)
                            nsub = (W + 127) // 128
                            rels = [f"bg{gci}"]
                            if c == ngr - 1:
                                rels.append("bq_rel")
                            push_inst((si, W, nsub, c, h, qs, col, first, ex, rels))
                            inst += 1
                flush_insts()
                if not B_FIN:
                    continue
                E.sbar("vector", "VE")
                E.op("vector", lambda e: e.reciprocal(rl_sb[:, :], l_sb[:, :]), inc="VE")
                E.sbar("vector", "VE")
                E.wait("vector", st.get("bat_rel"))
                for h in range(16):
                    for qs in range(4):
                        col = h * 4 + qs
                        E.wait("vector", st.get("bobf"))
                        ob = E.op("vector", lambda e, col=col: e.tensor_scalar_mul(
                            obf[:, :], Oacc[:, col * 128:col * 128 + 128],
                            rl_sb[:, col:col + 1]), inc="VE")
                        E.wait("tensor", ("VE", ob))
                        E.wait("tensor", st.get("bfinc"))
                        E.wait("tensor", st.get("bpp0"))
                        E.op("tensor", lambda e, qs=qs: e.transpose(
                            pP[0][:, qs * 128:qs * 128 + 128], obf[:, :], id_sb[:, :]),
                            inc="TE")
                        tn = E.cnt("TE")
                        st["bobf"] = ("TE", tn)
                        E.wait("vector", ("TE", tn))
                        cpa = E.op("vector", lambda e, h=h, qs=qs: e.tensor_copy(
                            att_st[:, h, qs * 128:qs * 128 + 128],
                            pP[0][:, qs * 128:qs * 128 + 128]), inc="VE")
                        st["bfinc"] = ("VE", cpa)
                        st["bpp0"] = ("VE", cpa)
                fin = ("VE", E.cnt("VE"))
                E.wait("sync", fin)
                atn = E.op("sync", lambda e, t0=t0: e.dma_start(
                    out=r128(s_at_h)[:, :, t0:t0 + 512], in_=att_st[:, :, :]), inc="SAT", k=16)
                st["bat_rel"] = ("SAT", atn)
            B_f = {"sync": ("SAT", E.cnt("SAT")), "tensor": ("TE", E.cnt("TE")),
                   "vector": ("VE", E.cnt("VE")), "scalar": ("AE", E.cnt("AE"))}
        E.barrier(B_f)

        with contextlib.ExitStack() as pst:
            psb = lambda name, shape, dt: pst.enter_context(nc.sbuf_tensor(name, shape, dt))
            pps = lambda name, shape, dt: pst.enter_context(nc.psum_tensor(name, shape, dt))
            arena = [psb(f"arena{i}", [128, 16, 512], BF16) for i in range(3)]
            xc_sb = psb("xc_sbC", [128, 4, D], F32)
            yz_sb = psb("yz_sb", [128, 4, D], F32)
            gT_sb = psb("gT_sb", [128, 32, 512], BF16)
            h2_bf = psb("h2_bf", [128, D], BF16)
            h2T_sb = psb("h2T_sb", [128, 16, 512], BF16)
            sg_sb = [psb(f"sg_sb{i}", [128, 512], BF16) for i in range(2)]
            t_scr = psb("t_scr", [128, D], F32)
            out_sb = psb("out_sb", [128, D], F32)
            sums3 = psb("sums3", [128, 1], F32)
            ss3 = psb("ss3", [128, 1], F32)
            nmu3 = psb("nmu3", [128, 1], F32)
            rst3 = psb("rst3", [128, 1], F32)
            s43 = (sums3[:, 0:1], ss3, nmu3[:, 0:1], rst3[:, 0:1])
            pG = [pps(f"pG{i}", [128, 512], F32) for i in range(2)]
            pU = [pps(f"pU{i}", [128, 512], F32) for i in range(2)]
            pD = [pps(f"pD{i}", [128, 512], F32) for i in range(2)]
            pH = pps("pH", [128, 2048], BF16)

            E.wait("sync", ("CVT", E.cnt("CVT")))
            arena_i = 0

            def arena_load(src_ap, pin=False, ring3=False, q="sync"):
                nonlocal arena_i
                if pin:
                    sl = 2
                else:
                    sl = arena_i % (3 if ring3 else 2)
                    arena_i += 1
                E.wait(q, st.get(f"car{sl}"))
                n = E.op(q, lambda e, sl=sl, src_ap=src_ap: e.dma_start(
                    out=arena[sl][:, :, :], in_=src_ap), inc=f"CA{sl}", k=16)
                return sl, (f"CA{sl}", n)

            for t in range(min(NT, C_MAX_T)):
                t0 = t * 512
                ti = t % 2
                arena_i = 0
                sl_a, af_a = arena_load(r128(s_at_h)[:, :, t0:t0 + 512], pin=True)
                E.wait("sync", st.get("cx_rel"))
                for qs in range(4):
                    xn = E.op("sync", lambda e, t0=t0, qs=qs: e.dma_start(
                        out=xc_sb[:, qs, :], in_=x_h[t0 + qs * 128:t0 + qs * 128 + 128, :]),
                        inc=f"CX{ti}", k=16)
                cx_rdy = (f"CX{ti}", xn)
                for fo in range(4):
                    sl, af = arena_load(r128(wo_b_h)[:, :, fo * 512:fo * 512 + 512])
                    E.wait("tensor", af)
                    E.wait("tensor", af_a)
                    for qs in range(4):
                        pi = (fo * 4 + qs) % 2
                        E.wait("tensor", st.get(f"cpg{pi}"))
                        for kc in range(16):
                            E.op("tensor", lambda e, kc=kc, qs=qs, sl=sl, pi=pi, sl_a=sl_a: e.matmul(
                                pG[pi][:, :], arena[sl_a][:, kc, qs * 128:qs * 128 + 128],
                                arena[sl][:, kc, :], start=(kc == 0), stop=(kc == 15)),
                                inc=("TE" if kc == 15 else None))
                        mmn = E.cnt("TE")
                        if qs == 3:
                            st[f"car{sl}"] = ("TE", mmn)
                            if fo == 3:
                                st[f"car{sl_a}"] = ("TE", mmn)
                        E.wait("vector", ("TE", mmn))
                        E.wait("vector", st.get("cyz_w"))
                        ev = E.op("vector", lambda e, qs=qs, fo=fo, pi=pi: e.tensor_copy(
                            yz_sb[:, qs, fo * 512:fo * 512 + 512], pG[pi][:, :]), inc="VE")
                        st[f"cpg{pi}"] = ("VE", ev)
                yrdy = ("VE", E.cnt("VE"))
                if dbg:
                    E.wait("sync", yrdy)
                    E.op("sync", lambda e, t0=t0: e.dma_start(
                        out=d_wo_h.rearrange("(a p) d -> p a d", p=128)[:, t0 // 128:t0 // 128 + 4, :],
                        in_=yz_sb[:, :, :]), inc="DBG", k=16)
                for qs in range(4):
                    E.wait("vector", st.get("ct_rel"))
                    apf = ln_chain(yz_sb[:, qs, :], t_scr[:, :], D, epsln, s43, "lnC",
                                   src_f=(yrdy if qs == 0 else None))
                    E.wait("gpsimd", apf)
                    E.wait("gpsimd", cx_rdy)
                    gadd = E.op("gpsimd", lambda e, qs=qs: e.tensor_tensor(
                        xc_sb[:, qs, :], xc_sb[:, qs, :], t_scr[:, :], ALU.add), inc="GE")
                    st["ct_rel"] = ("GE", gadd)
                    E.wait("vector", ("GE", gadd))
                    E.wait("scalar", ("GE", gadd))
                    E.wait("vector", st.get("ch2"))
                    apf2 = ln_chain(xc_sb[:, qs, :], h2_bf[:, :],
                                    D, epsln, s43, "lnC", src_f=None)
                    E.wait("tensor", apf2)
                    E.wait("tensor", st.get("cpH"))
                    for cc in range(16):
                        E.op("tensor", lambda e, cc=cc: e.transpose(
                            pH[:, cc * 128:(cc + 1) * 128], h2_bf[:, cc * 128:(cc + 1) * 128],
                            id_sb[:, :]), inc=("TE" if cc == 15 else None))
                    tn = E.cnt("TE")
                    st["ch2"] = ("TE", tn)
                    E.wait("vector", ("TE", tn))
                    E.wait("vector", st.get("ch2T"))
                    cph = E.op("vector", lambda e, qs=qs: e.tensor_copy(
                        h2T_sb[:, :, qs * 128:qs * 128 + 128],
                        pH[:, :]), inc="VE")
                    st["cpH"] = ("VE", cph)
                st["cyz_w2"] = ("VE", E.cnt("VE"))
                if dbg:
                    E.wait("sync", st.get("ct_rel"))
                    E.op("sync", lambda e, t0=t0: e.dma_start(
                        out=d_x2_h.rearrange("(a p) d -> p a d", p=128)[:, t0 // 128:t0 // 128 + 4, :],
                        in_=xc_sb[:, :, :]), inc="DBG", k=16)
                h2T_rdy = ("VE", E.cnt("VE"))
                E.wait("tensor", h2T_rdy)
                for half in range(2):
                    hb = half * 32
                    for hcg in range(8):
                        slg, afg = arena_load(
                            r128(wg_b_h)[:, :, (hb + hcg * 4) * 128:(hb + hcg * 4) * 128 + 512],
                            ring3=True)
                        slu, afu = arena_load(
                            r128(wu_b_h)[:, :, (hb + hcg * 4) * 128:(hb + hcg * 4) * 128 + 512],
                            ring3=True)
                        for hl in range(4):
                            hc = hcg * 4 + hl
                            pi = hc % 2
                            E.wait("tensor", afg)
                            E.wait("tensor", st.get(f"cpg{pi}"))
                            for kc in range(16):
                                E.op("tensor", lambda e, kc=kc, hl=hl, slg=slg, pi=pi: e.matmul(
                                    pG[pi][:, :], arena[slg][:, kc, hl * 128:hl * 128 + 128],
                                    h2T_sb[:, kc, :], start=(kc == 0), stop=(kc == 15)),
                                    inc=("TE" if kc == 15 else None))
                            gmm = E.cnt("TE")
                            E.wait("tensor", afu)
                            E.wait("tensor", st.get(f"cpu{pi}"))
                            for kc in range(16):
                                E.op("tensor", lambda e, kc=kc, hl=hl, slu=slu, pi=pi: e.matmul(
                                    pU[pi][:, :], arena[slu][:, kc, hl * 128:hl * 128 + 128],
                                    h2T_sb[:, kc, :], start=(kc == 0), stop=(kc == 15)),
                                    inc=("TE" if kc == 15 else None))
                            umm = E.cnt("TE")
                            if hl == 3:
                                st[f"car{slg}"] = ("TE", gmm)
                                st[f"car{slu}"] = ("TE", umm)
                            E.wait("scalar", ("TE", gmm))
                            E.wait("scalar", st.get(f"csg{pi}"))
                            sg_n = E.op("scalar", lambda e, pi=pi: e.activation(
                                sg_sb[pi][:, :], pG[pi][:, :], AF.Silu), inc="AE")
                            st[f"cpg{pi}"] = ("AE", sg_n)
                            E.wait("vector", ("AE", sg_n))
                            E.wait("vector", ("TE", umm))
                            E.wait("vector", st.get("cgT_rel"))
                            gv = E.op("vector", lambda e, hc=hc, pi=pi: e.tensor_tensor(
                                gT_sb[:, hc % 32, :], sg_sb[pi][:, :], pU[pi][:, :], ALU.mult),
                                inc="VE")
                            st[f"csg{pi}"] = ("VE", gv)
                            st[f"cpu{pi}"] = ("VE", gv)
                    gT_rdy = ("VE", E.cnt("VE"))
                    E.wait("tensor", gT_rdy)
                    for fo in range(4):
                        wdsl = []
                        for j in range(2):
                            slw, afw = arena_load(
                                r128(wd_b_h)[:, (hb + j * 16):(hb + j * 16) + 16,
                                             fo * 512:fo * 512 + 512],
                                ring3=True, q="scalar")
                            wdsl.append((slw, afw))
                        for qs in range(4):
                            pi = qs % 2
                            E.wait("tensor", st.get(f"cpd{pi}"))
                            for j in range(2):
                                slw, afw = wdsl[j]
                                E.wait("tensor", afw)
                                for hl in range(16):
                                    hc = j * 16 + hl
                                    E.op("tensor", lambda e, hc=hc, hl=hl, qs=qs, slw=slw, pi=pi, j=j: e.matmul(
                                        pD[pi][:, :], gT_sb[:, hc, qs * 128:qs * 128 + 128],
                                        arena[slw][:, hl, :],
                                        start=(j == 0 and hl == 0), stop=(j == 1 and hl == 15)),
                                        inc=("TE" if (j == 1 and hl == 15) else None))
                            dmm = E.cnt("TE")
                            if qs == 3:
                                for (slw, afw) in wdsl:
                                    st[f"car{slw}"] = ("TE", dmm)
                                if fo == 3:
                                    st["cgT_rel"] = ("TE", dmm)
                            E.wait("vector", ("TE", dmm))
                            E.wait("vector", st.get("cyz_w2"))
                            if half == 0:
                                ev = E.op("vector", lambda e, qs=qs, fo=fo, pi=pi: e.tensor_copy(
                                    yz_sb[:, qs, fo * 512:fo * 512 + 512], pD[pi][:, :]),
                                    inc="VE")
                            else:
                                ev = E.op("vector", lambda e, qs=qs, fo=fo, pi=pi: e.tensor_tensor(
                                    yz_sb[:, qs, fo * 512:fo * 512 + 512],
                                    yz_sb[:, qs, fo * 512:fo * 512 + 512], pD[pi][:, :],
                                    ALU.add), inc="VE")
                            st[f"cpd{pi}"] = ("VE", ev)
                zrdy = ("VE", E.cnt("VE"))
                if dbg:
                    E.wait("sync", zrdy)
                    E.op("sync", lambda e, t0=t0: e.dma_start(
                        out=d_z_h.rearrange("(a p) d -> p a d", p=128)[:, t0 // 128:t0 // 128 + 4, :],
                        in_=yz_sb[:, :, :]), inc="DBG", k=16)
                for qs in range(4):
                    E.wait("vector", st.get("ct_rel"))
                    apf = ln_chain(yz_sb[:, qs, :], t_scr[:, :], D, epsln, s43, "lnC",
                                   src_f=(zrdy if qs == 0 else None))
                    E.wait("gpsimd", apf)
                    E.wait("gpsimd", st.get("cout"))
                    E.sbar("gpsimd", "GE")
                    gadd = E.op("gpsimd", lambda e, qs=qs: e.tensor_tensor(
                        out_sb[:, :], xc_sb[:, qs, :], t_scr[:, :], ALU.add), inc="GE")
                    st["ct_rel"] = ("GE", gadd)
                    if qs == 3:
                        st["cx_rel"] = ("GE", gadd)
                    E.wait("sync", ("GE", gadd))
                    yo = E.op("sync", lambda e, t0=t0, qs=qs: e.dma_start(
                        out=y_h[t0 + qs * 128:t0 + qs * 128 + 128, :], in_=out_sb[:, :]),
                        inc="YO", k=16)
                    st["cout"] = ("YO", yo)
                st["cyz_w"] = ("VE", E.cnt("VE"))
            E.wait("sync", ("YO", E.cnt("YO")))
            if dbg:
                E.wait("sync", ("DBG", E.cnt("DBG")))

        with nc.Block() as block:
            @block.sync
            def _(e):
                for f in E.ops["sync"]:
                    f(e)

            @block.tensor
            def _(e):
                for f in E.ops["tensor"]:
                    f(e)

            @block.vector
            def _(e):
                for f in E.ops["vector"]:
                    f(e)

            @block.scalar
            def _(e):
                for f in E.ops["scalar"]:
                    f(e)

            @block.gpsimd
            def _(e):
                for f in E.ops["gpsimd"]:
                    f(e)

    return nc


def _prep_inputs(inputs, S):
    T = lambda a: np.ascontiguousarray(np.asarray(a).T.astype(np.float32))
    wqr = T(inputs["L0_attn_wq_rope"])
    tq = np.empty(NH * DR, dtype=np.int64)
    for d in range(NH * DR):
        H, r = d // DR, d % DR
        tq[d] = (512 if r % 2 else 0) + H * 32 + r // 2
    wqr_p = np.empty_like(wqr)
    wqr_p[:, tq] = wqr
    wkr = T(inputs["L0_attn_wk_rope"])
    tk = np.empty(DR, dtype=np.int64)
    for r in range(DR):
        tk[r] = (32 if r % 2 else 0) + r // 2
    wkr_p = np.empty_like(wkr)
    wkr_p[:, tk] = wkr
    ident = np.eye(128, dtype=np.float32)
    ii = np.arange(128)
    trimask = np.where(ii[None, :] <= ii[:, None], 0.0, NEGBIG).astype(np.float32)
    inv = 10000.0 ** (-(np.arange(32, dtype=np.float64) * 2.0 / DR))
    fr = np.outer(inv, np.arange(S, dtype=np.float64))
    rcos = np.tile(np.cos(fr), (4, 1)).astype(np.float32)
    rsin = np.tile(np.sin(fr), (4, 1)).astype(np.float32)
    return dict(
        wqd=T(inputs["L0_attn_wq_down"]), wqu=T(inputs["L0_attn_wq_up"]), wqr=wqr_p,
        qnw=np.ascontiguousarray(np.asarray(inputs["L0_attn_q_norm_w"], dtype=np.float32).reshape(QR // 128, 128).T),
        wkd=T(inputs["L0_attn_wkv_down"]),
        kvw=np.ascontiguousarray(np.asarray(inputs["L0_attn_kv_norm_w"], dtype=np.float32).reshape(KVR // 128, 128).T),
        wku=T(inputs["L0_attn_wkv_up"]), wkr=wkr_p, wo=T(inputs["L0_attn_wo"]),
        wg=T(inputs["L0_ffn_w_gate"]), wu=T(inputs["L0_ffn_w_up"]),
        wd=T(inputs["L0_ffn_w_down"]), ident=ident, trimask=trimask,
        rcos=rcos, rsin=rsin)


def kernel(**inputs):
    S = 4096
    x = np.asarray(inputs["L0_attn_x"])
    B = x.shape[0]
    common = _prep_inputs(inputs, S)
    in_maps = [dict(common, x=np.ascontiguousarray(x[c], dtype=np.float32)) for c in range(B)]
    nc = build_program(S)
    res = run_bass_kernel_spmd(nc, in_maps, core_ids=list(range(B)))
    out = np.stack([np.asarray(res.results[c]["y"]) for c in range(B)], axis=0)
    return out.astype(np.float32)


# version 05
# speedup vs baseline: 1.4003x; 1.0046x over previous
import math
import contextlib
import numpy as np
import concourse.bass as bass
import concourse.mybir as mybir
from concourse.bass_utils import run_bass_kernel_spmd

F32 = mybir.dt.float32
BF16 = mybir.dt.bfloat16
AF = mybir.ActivationFunctionType
ALU = mybir.AluOpType
AX = mybir.AxisListType

D = 2048
NH = 16
QR = 1536
KVR = 512
DN = 128
DR = 64
HID = 8192
SCALE = 1.0 / math.sqrt(DN + DR)
MSHIFT = -10.0
EPS_LN = 1e-5
EPS_RMS = 1e-6
NEGBIG = -1.0e30
B_MAX_QT = 10**9
B_MAX_CH = 10**9
B_PREFETCH = True
B_H_MAX = 16
B_FIN = True
C_MAX_T = 10**9
A_MAX_T = 10**9


class Emit:
    ENGINES = ("sync", "tensor", "vector", "scalar", "gpsimd")

    def __init__(self, nc, stack):
        self.nc = nc
        self.stack = stack
        self.ops = {e: [] for e in self.ENGINES}
        self.h = {}
        self.n = {}

    def sem(self, name):
        if name not in self.h:
            self.h[name] = self.stack.enter_context(self.nc.semaphore(name))
            self.n[name] = 0
        return self.h[name]

    def op(self, eng, fn, inc=None, k=1):
        if inc is None:
            self.ops[eng].append(fn)
            return None
        hh = self.sem(inc)
        self.n[inc] += k
        self.ops[eng].append(lambda e, f=fn, hh=hh, k=k: f(e).then_inc(hh, k))
        return self.n[inc]

    def wait(self, eng, fence):
        if fence is None:
            return
        name, val = fence
        if val is None or val <= 0:
            return
        hh = self.sem(name)
        self.ops[eng].append(lambda e, hh=hh, v=val: e.wait_ge(hh, v))

    def cnt(self, name):
        self.sem(name)
        return self.n[name]

    def sbar(self, eng, ctr):
        self.wait(eng, (ctr, self.n.get(ctr, 0)))

    def barrier(self, fences):
        for eng in self.ENGINES:
            for other, f in fences.items():
                if other != eng:
                    self.wait(eng, f)


def build_program(S, dbg=False):
    nc = bass.Bass("TRN2", target_bir_lowering=False)
    NT = S // 512
    skind = "ExternalOutput" if dbg else "Internal"

    x_h = nc.dram_tensor("x", [S, D], F32, kind="ExternalInput").ap()
    wqd_h = nc.dram_tensor("wqd", [D, QR], F32, kind="ExternalInput").ap()
    wqu_h = nc.dram_tensor("wqu", [QR, NH * DN], F32, kind="ExternalInput").ap()
    wqr_h = nc.dram_tensor("wqr", [QR, NH * DR], F32, kind="ExternalInput").ap()
    qnw_h = nc.dram_tensor("qnw", [128, QR // 128], F32, kind="ExternalInput").ap()
    wkd_h = nc.dram_tensor("wkd", [D, KVR], F32, kind="ExternalInput").ap()
    kvw_h = nc.dram_tensor("kvw", [128, KVR // 128], F32, kind="ExternalInput").ap()
    wku_h = nc.dram_tensor("wku", [KVR, NH * 256], F32, kind="ExternalInput").ap()
    wkr_h = nc.dram_tensor("wkr", [D, DR], F32, kind="ExternalInput").ap()
    wo_h = nc.dram_tensor("wo", [D, D], F32, kind="ExternalInput").ap()
    wg_h = nc.dram_tensor("wg", [D, HID], F32, kind="ExternalInput").ap()
    wu_h = nc.dram_tensor("wu", [D, HID], F32, kind="ExternalInput").ap()
    wd_h = nc.dram_tensor("wd", [HID, D], F32, kind="ExternalInput").ap()
    id_h = nc.dram_tensor("ident", [128, 128], F32, kind="ExternalInput").ap()
    tm_h = nc.dram_tensor("trimask", [128, 128], F32, kind="ExternalInput").ap()
    rc_h = nc.dram_tensor("rcos", [128, S], F32, kind="ExternalInput").ap()
    rs_h = nc.dram_tensor("rsin", [128, S], F32, kind="ExternalInput").ap()
    y_h = nc.dram_tensor("y", [S, D], F32, kind="ExternalOutput").ap()
    s_h_h = nc.dram_tensor("s_h", [D, S], BF16, kind=skind).ap()
    s_qn_h = nc.dram_tensor("s_qn", [NH * DN, S], BF16, kind=skind).ap()
    s_qp_h = nc.dram_tensor("s_qp", [NH * DR, S], BF16, kind=skind).ap()
    s_kn_h = nc.dram_tensor("s_kn", [NH * DN, S], BF16, kind=skind).ap()
    s_v_h = nc.dram_tensor("s_v", [S, NH * DN], BF16, kind=skind).ap()
    s_at_h = nc.dram_tensor("s_at", [D, S], BF16, kind=skind).ap()
    if dbg:
        d_wo_h = nc.dram_tensor("d_wo", [S, D], F32, kind="ExternalOutput").ap()
        d_x2_h = nc.dram_tensor("d_x2", [S, D], F32, kind="ExternalOutput").ap()
        d_z_h = nc.dram_tensor("d_z", [S, D], F32, kind="ExternalOutput").ap()
    wo_b_h = nc.dram_tensor("wo_b", [D, D], BF16, kind="Internal").ap()
    wg_b_h = nc.dram_tensor("wg_b", [D, HID], BF16, kind="Internal").ap()
    wu_b_h = nc.dram_tensor("wu_b", [D, HID], BF16, kind="Internal").ap()
    wd_b_h = nc.dram_tensor("wd_b", [HID, D], BF16, kind="Internal").ap()

    r128 = lambda ap_: ap_.rearrange("(c p) t -> p c t", p=128)
    r64 = lambda ap_: ap_.rearrange("(c p) t -> p c t", p=64)

    stack = contextlib.ExitStack()
    with stack:
        E = Emit(nc, stack)
        sb = lambda name, shape, dt: stack.enter_context(nc.sbuf_tensor(name, shape, dt))
        st = {}

        id_sb = sb("id_sb", [128, 128], BF16)
        tm_sb = sb("tm_sb", [128, 128], F32)
        krope = sb("krope", [64, S], BF16)
        epsln = sb("epsln", [128, 1], F32)
        epsrm = sb("epsrm", [128, 1], F32)
        mshift = sb("mshift", [128, 1], F32)
        junk = sb("junk", [128, 2048], BF16)

        E.op("gpsimd", lambda e: e.dma_start(out=id_sb[:, :], in_=id_h[:, :]), inc="LW0", k=16)
        E.op("gpsimd", lambda e: e.dma_start(out=tm_sb[:, :], in_=tm_h[:, :]), inc="LW0", k=16)
        E.op("gpsimd", lambda e: e.memset(epsln[:, :], EPS_LN), inc="GE")
        E.op("gpsimd", lambda e: e.memset(epsrm[:, :], EPS_RMS), inc="GE")
        E.op("gpsimd", lambda e: e.memset(mshift[:, :], MSHIFT), inc="GE")
        P0_LW = ("LW0", E.cnt("LW0"))
        P0_GE = ("GE", E.cnt("GE"))

        def ln_chain(src, out, width, eps_tile, s4, tag, src_f=None, rms=False,
                     ss_ready=None, nss=1):
            sums, ss, nmu, rst = s4
            prev = st.get(tag)
            if src_f is not None:
                E.wait("vector", src_f)
                E.wait("scalar", src_f)
            if prev is not None:
                E.wait("vector", ("AE", prev[1]))
                E.wait("scalar", ("VE", prev[0]))
            if not rms:
                E.op("vector", lambda e: e.reduce_sum(sums, src, axis=AX.X), inc="VE")
            if ss_ready is None:
                E.sbar("scalar", "AE")
                a_sq = E.op("scalar", lambda e: e.activation(junk[:, 0:width], src, AF.Square,
                                                             accum_out=ss[:, 0:1]), inc="AE")
            else:
                a_sq = ss_ready[1]
                E.wait("scalar", ss_ready)
            E.wait("vector", ("AE", a_sq))
            if nss > 1:
                E.op("vector", lambda e: e.reduce_sum(sums if rms else rst, ss[:, 0:nss],
                                                      axis=AX.X), inc="VE")
            if rms:
                srcss = sums if nss > 1 else ss
                E.sbar("vector", "VE")
                E.op("vector", lambda e: e.tensor_scalar_mul(rst, srcss[:, 0:1], 1.0 / width),
                     inc="VE")
            else:
                E.sbar("vector", "VE")
                E.op("vector", lambda e: e.tensor_scalar_mul(nmu, sums, -1.0 / width), inc="VE")
                if nss > 1:
                    E.op("vector", lambda e: e.tensor_scalar_mul(rst, rst, 1.0 / width),
                         inc="VE")
                else:
                    E.op("vector", lambda e: e.tensor_scalar_mul(rst, ss[:, 0:1], 1.0 / width),
                         inc="VE")
                E.sbar("vector", "VE")
                E.op("vector", lambda e: e.tensor_tensor(sums, nmu, nmu, ALU.mult), inc="VE")
                E.sbar("vector", "VE")
                E.op("vector", lambda e: e.tensor_tensor(rst, rst, sums, ALU.subtract), inc="VE")
            v3 = E.cnt("VE")
            E.wait("scalar", ("VE", v3))
            a_rt = E.op("scalar", lambda e: e.activation(rst, rst, AF.Sqrt,
                                                         bias=eps_tile[:, 0:1]), inc="AE")
            E.wait("vector", ("AE", a_rt))
            E.op("vector", lambda e: e.reciprocal(rst, rst), inc="VE")
            E.sbar("vector", "VE")
            if rms:
                vap = E.op("vector", lambda e: e.tensor_scalar_mul(out, src, rst[:, 0:1]),
                           inc="VE")
            else:
                vap = E.op("vector", lambda e: e.tensor_scalar(out, src, nmu[:, 0:1], rst[:, 0:1],
                                                               ALU.add, ALU.mult), inc="VE")
            st[tag] = (vap, a_rt)
            return ("VE", vap)

        conv_jobs = []
        for (src, dst, rows, rstep) in ((wo_h, wo_b_h, D, 1024), (wg_h, wg_b_h, D, 256),
                                        (wu_h, wu_b_h, D, 256), (wd_h, wd_b_h, HID, 1024)):
            for r0 in range(0, rows, rstep):
                conv_jobs.append((src, dst, r0, r0 + rstep))
        conv_it = iter(conv_jobs)

        def drip_conv(n):
            for _ in range(n):
                job = next(conv_it, None)
                if job is None:
                    return
                src, dst, r0, r1 = job
                E.op("gpsimd", lambda e, src=src, dst=dst, r0=r0, r1=r1: e.dma_start(
                    out=dst[r0:r1, :], in_=src[r0:r1, :]), inc="CVT", k=16)

        with contextlib.ExitStack() as pst:
            psb = lambda name, shape, dt: pst.enter_context(nc.sbuf_tensor(name, shape, dt))
            pps = lambda name, shape, dt: pst.enter_context(nc.psum_tensor(name, shape, dt))
            wqd_sb = psb("wqd_sb", [128, 16, QR], BF16)
            wqu_s = [psb(f"wqu_s{i}", [128, 12, 512], BF16) for i in range(2)]
            wqr_sb = psb("wqr_sb", [128, 12, NH * DR], BF16)
            qnw_sb = psb("qnw_sb", [128, 12], F32)
            x_sb = [psb(f"x_sb{i}", [128, D], F32) for i in range(2)]
            h_sb = [psb(f"h_sb{i}", [128, D], BF16) for i in range(2)]
            hT_sb = psb("hT_sb", [128, 16, 512], BF16)
            qc_sb = psb("qc_sb", [128, QR], F32)
            qcn_sb = psb("qcn_sb", [128, QR], BF16)
            qcT_sb = psb("qcT_sb", [128, 12, 512], BF16)
            qn_st = [psb(f"qn_st{i}", [128, 4, 512], BF16) for i in range(2)]
            qpe_sb = psb("qpe_sb", [128, 8, 512], BF16)
            ro_e = psb("ro_e", [128, 512], BF16)
            ro_o = psb("ro_o", [128, 512], BF16)
            rt1 = psb("rt1", [128, 512], BF16)
            rt2 = psb("rt2", [128, 512], BF16)
            rc_sb = psb("rc_sb", [128, 512], F32)
            rs_sb = psb("rs_sb", [128, 512], F32)
            sums = psb("sums", [128, 4], F32)
            ss = psb("ss", [128, 12], F32)
            nmu = psb("nmu", [128, 4], F32)
            rst = psb("rst", [128, 4], F32)
            s4sets = [(sums[:, i:i + 1], ss[:, 3 * i:3 * i + 3], nmu[:, i:i + 1],
                       rst[:, i:i + 1]) for i in range(4)]
            pA = [pps(f"pA{i}", [128, 512], F32) for i in range(2)]
            pT = [pps(f"pT{i}", [128, 2048], BF16) for i in range(2)]

            E.op("gpsimd", lambda e: e.dma_start(out=wqd_sb[:, :, :], in_=r128(wqd_h)),
                 inc="LW1", k=16)
            E.op("gpsimd", lambda e: e.dma_start(out=wqr_sb[:, :, :], in_=r128(wqr_h)),
                 inc="LW1", k=16)
            E.op("gpsimd", lambda e: e.dma_start(out=qnw_sb[:, :], in_=qnw_h[:, :]),
                 inc="LW1", k=16)
            wA1 = ("LW1", E.cnt("LW1"))
            E.wait("vector", wA1)
            E.wait("tensor", wA1)
            E.wait("tensor", P0_LW)
            E.wait("vector", P0_LW)

            def load_wqu(t, g):
                sl = (t * 4 + g) % 2
                E.wait("gpsimd", st.get(f"wqu{sl}"))
                n = E.op("gpsimd", lambda e, g=g, sl=sl: e.dma_start(
                    out=wqu_s[sl][:, :, :],
                    in_=r128(wqu_h)[:, :, g * 512:g * 512 + 512]), inc=f"WQ{sl}", k=16)
                drip_conv(1)
                return sl, (f"WQ{sl}", n)

            for t in range(NT):
                t0 = t * 512
                E.wait("sync", st.get("a1rope"))
                rs_n = None
                E.op("sync", lambda e, t0=t0: e.dma_start(out=rc_sb[:, :],
                     in_=rc_h[:, t0:t0 + 512]), inc="RT1", k=16)
                rs_n = E.op("sync", lambda e, t0=t0: e.dma_start(out=rs_sb[:, :],
                     in_=rs_h[:, t0:t0 + 512]), inc="RT1", k=16)
                for qs in range(4):
                    xi = qs % 2
                    E.wait("sync", st.get(f"a1x{xi}"))
                    xn = E.op("sync", lambda e, t0=t0, qs=qs, xi=xi: e.dma_start(
                        out=x_sb[xi][:, :], in_=x_h[t0 + qs * 128:t0 + qs * 128 + 128, :]),
                        inc=f"XA{xi}", k=16)
                    E.wait("vector", st.get(f"a1h{xi}"))
                    apf = ln_chain(x_sb[xi][:, :], h_sb[xi][:, :], D, epsln,
                                   s4sets[(qs % 2) * 2], f"sA1_{(qs % 2) * 2}",
                                   src_f=(f"XA{xi}", xn))
                    st[f"a1x{xi}"] = apf
                    E.wait("tensor", apf)
                    E.wait("tensor", st.get(f"a1pt{qs % 2}"))
                    for c in range(16):
                        E.op("tensor", lambda e, c=c, qs=qs, xi=xi: e.transpose(
                            pT[qs % 2][:, c * 128:(c + 1) * 128],
                            h_sb[xi][:, c * 128:(c + 1) * 128], id_sb[:, :]),
                            inc=("TE" if c == 15 else None))
                    tn = E.cnt("TE")
                    st[f"a1h{xi}"] = ("TE", tn)
                    E.wait("vector", ("TE", tn))
                    E.wait("vector", st.get(f"a1hT{qs}"))
                    E.wait("vector", st.get(f"a1hTs{qs}"))
                    cp = E.op("vector", lambda e, qs=qs: e.tensor_copy(
                        hT_sb[:, :, qs * 128:qs * 128 + 128],
                        pT[qs % 2][:, :]), inc="VE")
                    st[f"a1pt{qs % 2}"] = ("VE", cp)
                    E.wait("sync", ("VE", cp))
                    shn = E.op("sync", lambda e, t0=t0, qs=qs: e.dma_start(
                        out=r128(s_h_h)[:, :, t0 + qs * 128:t0 + qs * 128 + 128],
                        in_=hT_sb[:, :, qs * 128:qs * 128 + 128]), inc=f"SH{qs}", k=16)
                    st[f"a1hTs{qs}"] = (f"SH{qs}", shn)

                    for fo in range(3):
                        pi = (qs * 3 + fo) % 2
                        E.wait("tensor", st.get(f"a1pa{pi}"))
                        E.wait("tensor", ("VE", cp))
                        for kc in range(16):
                            E.op("tensor", lambda e, kc=kc, fo=fo, qs=qs, pi=pi: e.matmul(
                                pA[pi][:, :], hT_sb[:, kc, qs * 128:qs * 128 + 128],
                                wqd_sb[:, kc, fo * 512:fo * 512 + 512],
                                start=(kc == 0), stop=(kc == 15)),
                                inc=("TE" if kc == 15 else None))
                        mmn = E.cnt("TE")
                        E.wait("vector", ("TE", mmn))
                        if fo == 0:
                            E.wait("vector", st.get("a1qcr"))
                        ev = E.op("vector", lambda e, fo=fo, pi=pi: e.tensor_copy(
                            qc_sb[:, fo * 512:fo * 512 + 512], pA[pi][:, :]), inc="VE")
                        st[f"a1pa{pi}"] = ("VE", ev)
                        E.wait("scalar", ("VE", ev))
                        E.wait("scalar", st.get("a1ssr"))
                        E.sbar("scalar", "AE")
                        sscol = ((qs % 2) * 2 + 1) * 3 + fo
                        aq = E.op("scalar", lambda e, fo=fo, sscol=sscol: e.activation(
                            junk[:, 0:512], qc_sb[:, fo * 512:fo * 512 + 512], AF.Square,
                            accum_out=ss[:, sscol:sscol + 1]), inc="AE")
                    E.wait("vector", st.get("a1qcn"))
                    apf = ln_chain(qc_sb[:, :], qcn_sb[:, :], QR, epsrm,
                                   s4sets[(qs % 2) * 2 + 1], f"sA1_{(qs % 2) * 2 + 1}",
                                   rms=True, ss_ready=("AE", aq), nss=3)
                    st["a1qcr"] = ("AE", aq)
                    st["a1ssr"] = ("VE", E.cnt("VE"))
                    E.wait("tensor", apf)
                    E.wait("tensor", st.get(f"a1pt{qs % 2}"))
                    for c in range(12):
                        E.op("tensor", lambda e, c=c, qs=qs: e.transpose(
                            pT[qs % 2][:, c * 128:(c + 1) * 128],
                            qcn_sb[:, c * 128:(c + 1) * 128], id_sb[:, :]),
                            inc=("TE" if c == 11 else None))
                    tn = E.cnt("TE")
                    st["a1qcn"] = ("TE", tn)
                    E.wait("vector", ("TE", tn))
                    E.wait("vector", st.get("a1qcT"))
                    for kc in range(12):
                        cp2 = E.op("vector", lambda e, qs=qs, kc=kc: e.tensor_scalar_mul(
                            qcT_sb[:, kc, qs * 128:qs * 128 + 128],
                            pT[qs % 2][:, kc * 128:kc * 128 + 128],
                            qnw_sb[:, kc:kc + 1]), inc="VE")
                    st[f"a1pt{qs % 2}"] = ("VE", cp2)
                    st[f"a1hT{qs}"] = ("TE", E.cnt("TE"))
                qcT_rdy = ("VE", E.cnt("VE"))
                E.wait("tensor", qcT_rdy)
                wq_slots = {}
                wq_slots[0] = load_wqu(t, 0)
                for oc in range(16):
                    g = oc // 4
                    if oc % 4 == 0 and g + 1 < 4:
                        wq_slots[g + 1] = load_wqu(t, g + 1)
                    sl_w, af_w = wq_slots[g]
                    pi = oc % 2
                    E.wait("tensor", st.get(f"a1pa{pi}"))
                    E.wait("tensor", af_w)
                    for kc in range(12):
                        E.op("tensor", lambda e, kc=kc, oc=oc, pi=pi, sl_w=sl_w: e.matmul(
                            pA[pi][:, :], wqu_s[sl_w][:, kc, (oc % 4) * 128:(oc % 4) * 128 + 128],
                            qcT_sb[:, kc, :], start=(kc == 0), stop=(kc == 11)),
                            inc=("TE" if kc == 11 else None))
                    mmn = E.cnt("TE")
                    if oc % 4 == 3:
                        st[f"wqu{sl_w}"] = ("TE", mmn)
                    E.wait("vector", ("TE", mmn))
                    sl, bi = oc % 4, (oc // 4) % 2
                    if sl == 0:
                        E.wait("vector", st.get(f"a1qn{bi}"))
                    ev = E.op("vector", lambda e, sl=sl, bi=bi, pi=pi: e.tensor_copy(
                        qn_st[bi][:, sl, :], pA[pi][:, :]), inc="VE")
                    st[f"a1pa{pi}"] = ("VE", ev)
                    if sl == 3:
                        E.wait("sync", ("VE", ev))
                        stn = E.op("sync", lambda e, t0=t0, oc=oc, bi=bi: e.dma_start(
                            out=r128(s_qn_h)[:, oc - 3:oc + 1, t0:t0 + 512],
                            in_=qn_st[bi][:, :, :]), inc=f"SA{bi}", k=16)
                        st[f"a1qn{bi}"] = (f"SA{bi}", stn)
                E.wait("vector", st.get("a1ropeV"))
                for oc in range(8):
                    pi = oc % 2
                    E.wait("tensor", st.get(f"a1pa{pi}"))
                    for kc in range(12):
                        E.op("tensor", lambda e, kc=kc, oc=oc, pi=pi: e.matmul(
                            pA[pi][:, :], wqr_sb[:, kc, oc * 128:oc * 128 + 128],
                            qcT_sb[:, kc, :], start=(kc == 0), stop=(kc == 11)),
                            inc=("TE" if kc == 11 else None))
                    mmn = E.cnt("TE")
                    E.wait("vector", ("TE", mmn))
                    ev = E.op("vector", lambda e, oc=oc, pi=pi: e.tensor_copy(
                        qpe_sb[:, oc, :], pA[pi][:, :]), inc="VE")
                    st[f"a1pa{pi}"] = ("VE", ev)
                st["a1qcT"] = ("TE", E.cnt("TE"))
                E.wait("vector", ("RT1", rs_n))
                qp_g = s_qp_h.rearrange("(c q r) t -> c q r t", c=4, q=4, r=64)
                for c in range(4):
                    E.wait("vector", st.get("a1roS"))
                    E.op("vector", lambda e, c=c: e.tensor_tensor(rt1[:, :], qpe_sb[:, c, :],
                         rc_sb[:, :], ALU.mult), inc="VE")
                    E.op("vector", lambda e, c=c: e.tensor_tensor(rt2[:, :], qpe_sb[:, c + 4, :],
                         rs_sb[:, :], ALU.mult), inc="VE")
                    E.sbar("vector", "VE")
                    E.op("vector", lambda e: e.tensor_tensor(ro_e[:, :], rt1[:, :], rt2[:, :],
                         ALU.subtract), inc="VE")
                    E.sbar("vector", "VE")
                    E.op("vector", lambda e, c=c: e.tensor_tensor(rt1[:, :], qpe_sb[:, c, :],
                         rs_sb[:, :], ALU.mult), inc="VE")
                    E.op("vector", lambda e, c=c: e.tensor_tensor(rt2[:, :], qpe_sb[:, c + 4, :],
                         rc_sb[:, :], ALU.mult), inc="VE")
                    E.sbar("vector", "VE")
                    rov = E.op("vector", lambda e: e.tensor_tensor(ro_o[:, :], rt1[:, :], rt2[:, :],
                               ALU.add), inc="VE")
                    E.wait("sync", ("VE", rov))
                    E.op("sync", lambda e, c=c, t0=t0: e.dma_start(
                        out=qp_g[c, :, 0:32, t0:t0 + 512], in_=ro_e[:, :]), inc="RO", k=16)
                    ron = E.op("sync", lambda e, c=c, t0=t0: e.dma_start(
                        out=qp_g[c, :, 32:64, t0:t0 + 512], in_=ro_o[:, :]), inc="RO", k=16)
                    st["a1roS"] = ("RO", ron)
                st["a1ropeV"] = ("VE", E.cnt("VE"))
                st["a1rope"] = ("VE", E.cnt("VE"))
            drip_conv(100)
            A1_f = {"sync": ("RO", E.cnt("RO")), "tensor": ("TE", E.cnt("TE")),
                    "vector": ("VE", E.cnt("VE")), "scalar": ("AE", E.cnt("AE")),
                    "gpsimd": ("LW1", E.cnt("LW1"))}
            A1_extra = [(f"SH{q}", E.cnt(f"SH{q}")) for q in range(4)] + \
                       [("SA0", E.cnt("SA0")), ("SA1", E.cnt("SA1"))]
        E.barrier(A1_f)
        for f in A1_extra:
            for eng in Emit.ENGINES:
                E.wait(eng, f)

        with contextlib.ExitStack() as pst:
            psb = lambda name, shape, dt: pst.enter_context(nc.sbuf_tensor(name, shape, dt))
            pps = lambda name, shape, dt: pst.enter_context(nc.psum_tensor(name, shape, dt))
            wkd_sb = psb("wkd_sb", [128, 16, KVR], BF16)
            wku_sb = psb("wku_sb", [128, 4, NH * 256], BF16)
            wkr_sb = psb("wkr_sb", [128, 16, DR], BF16)
            kvw_sb = psb("kvw_sb", [128, 4], F32)
            hT2 = [psb(f"hT2_{i}", [128, 16, 512], BF16) for i in range(2)]
            kvc_sb = psb("kvc_sb", [128, KVR], F32)
            kvn_sb = psb("kvn_sb", [128, KVR], BF16)
            kvT_sb = psb("kvT_sb", [128, 4, 512], BF16)
            kn_st = [psb(f"kn_st{i}", [128, 4, 512], BF16) for i in range(2)]
            v_bf = psb("v_bf", [128, 512], BF16)
            v_st = psb("v_st", [128, 4, NH * DN], BF16)
            rt1b = psb("rt1b", [32, 512], F32)
            rt2b = psb("rt2b", [32, 512], F32)
            rc2 = psb("rc2", [128, 512], F32)
            rs2 = psb("rs2", [128, 512], F32)
            sums2 = psb("sums2", [128, 2], F32)
            ss2 = psb("ss2", [128, 2], F32)
            nmu2 = psb("nmu2", [128, 2], F32)
            rst2 = psb("rst2", [128, 2], F32)
            s42sets = [(sums2[:, i:i + 1], ss2[:, i:i + 1], nmu2[:, i:i + 1],
                        rst2[:, i:i + 1]) for i in range(2)]
            pA2 = [pps(f"pA2_{i}", [128, 512], F32) for i in range(2)]
            pT2 = [pps(f"pT2_{i}", [128, 2048], BF16) for i in range(2)]
            pV2 = pps("pV2", [128, 512], BF16)
            pK2 = pps("pK2", [128, 512], F32)

            E.op("gpsimd", lambda e: e.dma_start(out=wkd_sb[:, :, :], in_=r128(wkd_h)),
                 inc="LW2", k=16)
            E.op("gpsimd", lambda e: e.dma_start(out=wku_sb[:, :, :], in_=r128(wku_h)),
                 inc="LW2", k=16)
            E.op("gpsimd", lambda e: e.dma_start(out=wkr_sb[:, :, :], in_=r128(wkr_h)),
                 inc="LW2", k=16)
            E.op("gpsimd", lambda e: e.dma_start(out=kvw_sb[:, :], in_=kvw_h[:, :]),
                 inc="LW2", k=16)
            wA2 = ("LW2", E.cnt("LW2"))
            E.wait("vector", wA2)
            for kc in range(4):
                E.op("vector", lambda e, kc=kc: e.tensor_scalar_mul(
                    wku_sb[:, kc, :], wku_sb[:, kc, :], kvw_sb[:, kc:kc + 1]), inc="VE")
            foldkv = ("VE", E.cnt("VE"))
            E.wait("tensor", foldkv)

            for t in range(NT):
                t0 = t * 512
                ti = t % 2
                E.wait("sync", st.get("a2rope"))
                rs_n = None
                E.op("sync", lambda e, t0=t0: e.dma_start(out=rc2[:, :],
                     in_=rc_h[:, t0:t0 + 512]), inc="RT2", k=16)
                rs_n = E.op("sync", lambda e, t0=t0: e.dma_start(out=rs2[:, :],
                     in_=rs_h[:, t0:t0 + 512]), inc="RT2", k=16)
                E.wait("sync", st.get(f"a2h{ti}"))
                hn = E.op("sync", lambda e, t0=t0, ti=ti: e.dma_start(
                    out=hT2[ti][:, :, :], in_=r128(s_h_h)[:, :, t0:t0 + 512]),
                    inc=f"XB{ti}", k=16)
                for qs in range(4):
                    pi = qs % 2
                    E.wait("tensor", st.get(f"a2pa{pi}"))
                    E.wait("tensor", (f"XB{ti}", hn))
                    for kc in range(16):
                        E.op("tensor", lambda e, kc=kc, qs=qs, pi=pi, ti=ti: e.matmul(
                            pA2[pi][:, :], hT2[ti][:, kc, qs * 128:qs * 128 + 128],
                            wkd_sb[:, kc, :], start=(kc == 0), stop=(kc == 15)),
                            inc=("TE" if kc == 15 else None))
                    mmn = E.cnt("TE")
                    E.wait("vector", ("TE", mmn))
                    E.wait("vector", st.get("a2kvr"))
                    ev = E.op("vector", lambda e, pi=pi: e.tensor_copy(kvc_sb[:, :],
                              pA2[pi][:, :]), inc="VE")
                    st[f"a2pa{pi}"] = ("VE", ev)
                    E.wait("vector", st.get("a2kvn"))
                    apf = ln_chain(kvc_sb[:, :], kvn_sb[:, :], KVR, epsrm,
                                   s42sets[qs % 2], f"sA2_{qs % 2}",
                                   src_f=("VE", ev), rms=True)
                    st["a2kvr"] = ("AE", E.cnt("AE"))
                    E.wait("tensor", apf)
                    E.wait("tensor", st.get(f"a2pt{pi}"))
                    for c in range(4):
                        E.op("tensor", lambda e, c=c, pi=pi: e.transpose(
                            pT2[pi][:, c * 128:(c + 1) * 128],
                            kvn_sb[:, c * 128:(c + 1) * 128], id_sb[:, :]),
                            inc=("TE" if c == 3 else None))
                    tn = E.cnt("TE")
                    st["a2kvn"] = ("TE", tn)
                    E.wait("vector", ("TE", tn))
                    E.wait("vector", st.get("a2kvT"))
                    cp = E.op("vector", lambda e, qs=qs, pi=pi: e.tensor_copy(
                        kvT_sb[:, :, qs * 128:qs * 128 + 128],
                        pT2[pi][:, 0:512]), inc="VE")
                    st[f"a2pt{pi}"] = ("VE", cp)
                kvT_rdy = ("VE", E.cnt("VE"))
                E.wait("tensor", kvT_rdy)
                for oc in range(32):
                    pi = oc % 2
                    E.wait("tensor", st.get(f"a2pa{pi}"))
                    for kc in range(4):
                        E.op("tensor", lambda e, kc=kc, oc=oc, pi=pi: e.matmul(
                            pA2[pi][:, :], wku_sb[:, kc, oc * 128:oc * 128 + 128],
                            kvT_sb[:, kc, :], start=(kc == 0), stop=(kc == 3)),
                            inc=("TE" if kc == 3 else None))
                    mmn = E.cnt("TE")
                    h_ = oc // 2
                    E.wait("vector", ("TE", mmn))
                    if oc % 2 == 0:
                        sl, bi = h_ % 4, (h_ // 4) % 2
                        if sl == 0:
                            E.wait("vector", st.get(f"a2kn{bi}"))
                        ev = E.op("vector", lambda e, sl=sl, bi=bi, pi=pi: e.tensor_copy(
                            kn_st[bi][:, sl, :], pA2[pi][:, :]), inc="VE")
                        st[f"a2pa{pi}"] = ("VE", ev)
                        if sl == 3:
                            E.wait("sync", ("VE", ev))
                            stn = E.op("sync", lambda e, t0=t0, h_=h_, bi=bi: e.dma_start(
                                out=r128(s_kn_h)[:, h_ - 3:h_ + 1, t0:t0 + 512],
                                in_=kn_st[bi][:, :, :]), inc=f"SK{bi}", k=16)
                            st[f"a2kn{bi}"] = (f"SK{bi}", stn)
                    else:
                        E.wait("vector", st.get("a2vbf"))
                        ev = E.op("vector", lambda e, pi=pi: e.tensor_copy(
                            v_bf[:, :], pA2[pi][:, :]), inc="VE")
                        st[f"a2pa{pi}"] = ("VE", ev)
                        E.wait("tensor", ("VE", ev))
                        E.wait("tensor", st.get("a2pv"))
                        for ts_ in range(4):
                            E.op("tensor", lambda e, ts_=ts_: e.transpose(
                                pV2[:, ts_ * 128:ts_ * 128 + 128],
                                v_bf[:, ts_ * 128:ts_ * 128 + 128], id_sb[:, :]),
                                inc=("TE" if ts_ == 3 else None))
                        tn = E.cnt("TE")
                        st["a2vbf"] = ("TE", tn)
                        E.wait("vector", ("TE", tn))
                        if h_ == 0:
                            E.wait("vector", st.get("a2vst"))
                        cpv = E.op("vector", lambda e, h_=h_: e.tensor_copy(
                            v_st[:, :, h_ * 128:h_ * 128 + 128],
                            pV2[:, :]), inc="VE")
                        st["a2pv"] = ("VE", cpv)
                        if h_ == 15:
                            E.wait("sync", ("VE", cpv))
                            stn = E.op("sync", lambda e, t0=t0: e.dma_start(
                                out=s_v_h.rearrange("(s p) d -> p s d", p=128)[:, t0 // 128:t0 // 128 + 4, :],
                                in_=v_st[:, :, :]), inc="SV", k=16)
                            st["a2vst"] = ("SV", stn)
                st["a2kvT"] = ("TE", E.cnt("TE"))
                E.wait("tensor", st.get("a2krP"))
                for kc in range(16):
                    E.op("tensor", lambda e, kc=kc, ti=ti: e.matmul(
                        pK2[0:64, :], wkr_sb[:, kc, :], hT2[ti][:, kc, :],
                        start=(kc == 0), stop=(kc == 15)), inc=("TE" if kc == 15 else None))
                mmn = E.cnt("TE")
                st[f"a2h{ti}"] = ("TE", mmn)
                E.wait("vector", ("TE", mmn))
                E.wait("vector", ("RT2", rs_n))
                E.op("vector", lambda e: e.tensor_tensor(rt1b[:, :], pK2[0:32, :], rc2[0:32, :],
                     ALU.mult), inc="VE")
                E.op("vector", lambda e: e.tensor_tensor(rt2b[:, :], pK2[32:64, :], rs2[0:32, :],
                     ALU.mult), inc="VE")
                E.sbar("vector", "VE")
                E.op("vector", lambda e, t0=t0: e.tensor_tensor(krope[0:32, t0:t0 + 512],
                     rt1b[:, :], rt2b[:, :], ALU.subtract), inc="VE")
                E.sbar("vector", "VE")
                E.op("vector", lambda e: e.tensor_tensor(rt1b[:, :], pK2[0:32, :], rs2[0:32, :],
                     ALU.mult), inc="VE")
                E.op("vector", lambda e: e.tensor_tensor(rt2b[:, :], pK2[32:64, :], rc2[0:32, :],
                     ALU.mult), inc="VE")
                E.sbar("vector", "VE")
                krn = E.op("vector", lambda e, t0=t0: e.tensor_tensor(krope[32:64, t0:t0 + 512],
                           rt1b[:, :], rt2b[:, :], ALU.add), inc="VE")
                st["a2krP"] = ("VE", krn)
                st["a2rope"] = ("VE", krn)
            A2_f = {"sync": ("SV", E.cnt("SV")), "tensor": ("TE", E.cnt("TE")),
                    "vector": ("VE", E.cnt("VE")), "scalar": ("AE", E.cnt("AE")),
                    "gpsimd": ("LW2", E.cnt("LW2"))}
            A2_extra = [("SK0", E.cnt("SK0")), ("SK1", E.cnt("SK1"))]
        E.barrier(A2_f)
        for f in A2_extra:
            for eng in Emit.ENGINES:
                E.wait(eng, f)

        with contextlib.ExitStack() as pst:
            psb = lambda name, shape, dt: pst.enter_context(nc.sbuf_tensor(name, shape, dt))
            pps = lambda name, shape, dt: pst.enter_context(nc.psum_tensor(name, shape, dt))
            qn_sb = psb("qn_sb", [128, 16, 512], BF16)
            qp_sb = psb("qp_sb", [64, 16, 512], BF16)
            kn_g = [psb(f"kn_g{i}", [128, 16, 512], BF16) for i in range(3)]
            v_g = [psb(f"v_g{i}", [128, 4, NH * DN], BF16) for i in range(3)]
            probs = [psb(f"probs{i}", [128, 512], BF16) for i in range(3)]
            probsT = [psb(f"probsT{i}", [128, 512], BF16) for i in range(3)]
            Oacc = psb("Oacc", [128, 64 * 128], F32)
            l_sb = psb("l_sb", [128, 64], F32)
            rl_sb = psb("rl_sb", [128, 64], F32)
            cs_sb = psb("cs_sb", [128, 3], F32)
            obf = psb("obf", [128, 128], BF16)
            att_st = psb("att_st", [128, 16, 512], BF16)
            pS = [pps(f"pS{i}", [128, 512], F32) for i in range(3)]
            pP = [pps(f"pP{i}", [128, 512], BF16) for i in range(3)]
            pO = [pps(f"pO{i}", [128, 512], F32) for i in range(2)]

            inst = 0
            pend_b = []
            pend_c = []
            E.wait("scalar", P0_GE)

            def stage_sb(p):
                (si, W, nsub, c_, h, qs, col, first, ex, rels) = p
                E.wait("tensor", ("AE", ex))
                E.wait("tensor", st.get(f"bpp{si}"))
                for j in range(nsub):
                    E.op("tensor", lambda e, si=si, j=j: e.transpose(
                        pP[si][:, j * 128:j * 128 + 128],
                        probs[si][:, j * 128:j * 128 + 128], id_sb[:, :]),
                        inc=("TE" if j == nsub - 1 else None))
                ptn = E.cnt("TE")
                st[f"bprb{si}"] = ("TE", ptn)
                if si != 1:
                    E.wait("vector", ("TE", ptn))
                    E.wait("vector", st.get(f"bptT{si}"))
                    pc = E.op("vector", lambda e, si=si, W=W: e.tensor_copy(
                        probsT[si][:, 0:W], pP[si][:, 0:W]), inc="VE")
                    pcf = ("VE", pc)
                else:
                    E.wait("scalar", ("TE", ptn))
                    E.wait("scalar", st.get(f"bptT{si}"))
                    pc = E.op("scalar", lambda e, si=si, W=W: e.copy(
                        probsT[si][:, 0:W], pP[si][:, 0:W]), inc="AE")
                    pcf = ("AE", pc)
                st[f"bpp{si}"] = pcf
                p2 = p + (pcf,)
                return p2

            def stage_sc(p):
                (si, W, nsub, c_, h, qs, col, first, ex, rels, pcf) = p
                po = c_ % 2 if False else (sc_ctr[0] % 2)
                sc_ctr[0] += 1
                E.wait("tensor", pcf)
                E.wait("tensor", st.get(f"bpo{po}"))
                for j in range(nsub):
                    gi = c_ % 3
                    E.op("tensor", lambda e, si=si, j=j, gi=gi, po=po, h=h, nsub=nsub: e.matmul(
                        pO[po][:, 0:128], probsT[si][:, j * 128:j * 128 + 128],
                        v_g[gi][:, j, h * 128:h * 128 + 128],
                        start=(j == 0), stop=(j == nsub - 1)),
                        inc=("TE" if j == nsub - 1 else None))
                on = E.cnt("TE")
                st[f"bptT{si}"] = ("TE", on)
                for key in rels:
                    st[key] = ("TE", on)
                E.wait("vector", ("TE", on))
                if first:
                    ov = E.op("vector", lambda e, col=col, po=po: e.tensor_copy(
                        Oacc[:, col * 128:col * 128 + 128], pO[po][:, 0:128]),
                        inc="VE")
                else:
                    ov = E.op("vector", lambda e, col=col, po=po: e.tensor_tensor(
                        Oacc[:, col * 128:col * 128 + 128],
                        Oacc[:, col * 128:col * 128 + 128], pO[po][:, 0:128],
                        ALU.add), inc="VE")
                st[f"bpo{po}"] = ("VE", ov)

            sc_ctr = [0]

            def push_inst(p):
                pend_b.append(p)
                if len(pend_b) > 2:
                    pend_c.append(stage_sb(pend_b.pop(0)))
                if len(pend_c) > 2:
                    stage_sc(pend_c.pop(0))

            def flush_insts():
                while pend_b:
                    pend_c.append(stage_sb(pend_b.pop(0)))
                while pend_c:
                    stage_sc(pend_c.pop(0))

            for qt in range(min(NT, B_MAX_QT)):
                t0 = qt * 512
                E.wait("sync", st.get("bq_rel"))
                E.op("sync", lambda e, t0=t0: e.dma_start(out=qn_sb[:, :, :],
                     in_=r128(s_qn_h)[:, :, t0:t0 + 512]), inc="BQ", k=16)
                qld = E.op("sync", lambda e, t0=t0: e.dma_start(out=qp_sb[:, :, :],
                     in_=r64(s_qp_h)[:, :, t0:t0 + 512]), inc="BQ", k=16)
                ngr = qt + 1

                def load_gran(g):
                    gi = g % 3
                    E.wait("sync", st.get(f"bg{gi}"))
                    E.op("sync", lambda e, g=g, gi=gi: e.dma_start(out=kn_g[gi][:, :, :],
                         in_=r128(s_kn_h)[:, :, g * 512:g * 512 + 512]), inc=f"BK{gi}", k=16)
                    n = E.op("sync", lambda e, g=g, gi=gi: e.dma_start(out=v_g[gi][:, :, :],
                         in_=s_v_h.rearrange("(s p) d -> p s d", p=128)[:, g * 4:g * 4 + 4, :]),
                         inc=f"BK{gi}", k=16)
                    st[f"bgr{gi}"] = (f"BK{gi}", n)

                load_gran(0)
                for c in range(ngr):
                    if c + 1 < ngr:
                        load_gran(c + 1)
                    k0 = c * 512
                    gci = c % 3
                    rdy0 = st[f"bgr{gci}"]
                    for h in range(16):
                        for qs in range(4):
                            limit = t0 + qs * 128 + 128
                            if k0 >= limit:
                                continue
                            W = min(512, limit - k0)
                            si = inst % 3
                            first = (c == 0)
                            E.wait("tensor", ("BQ", qld))
                            E.wait("tensor", rdy0)
                            E.wait("tensor", st.get(f"bexp{si}"))
                            E.op("tensor", lambda e, h=h, qs=qs, si=si, gci=gci: e.matmul(
                                pS[si][:, :], qn_sb[:, h, qs * 128:qs * 128 + 128],
                                kn_g[gci][:, h, :], start=True, stop=False), inc=None)
                            E.op("tensor", lambda e, h=h, qs=qs, si=si, c=c: e.matmul(
                                pS[si][:, :], qp_sb[:, h, qs * 128:qs * 128 + 128],
                                krope[:, c * 512:c * 512 + 512], start=False, stop=True),
                                inc="TE")
                            smm = E.cnt("TE")
                            diag = (k0 + W == limit)
                            E.wait("vector", ("TE", smm))
                            if diag:
                                moff = W - 128
                                E.op("vector", lambda e, si=si, moff=moff: e.tensor_tensor(
                                    pS[si][:, moff:moff + 128], pS[si][:, moff:moff + 128],
                                    tm_sb[:, :], ALU.add), inc="VE")
                            msk = ("VE", E.cnt("VE"))
                            E.wait("scalar", ("TE", smm))
                            if diag:
                                E.wait("scalar", msk)
                            E.wait("scalar", st.get(f"bcs{si}"))
                            E.wait("scalar", st.get(f"bprb{si}"))
                            ex = E.op("scalar", lambda e, si=si, W=W: e.activation(
                                probs[si][:, 0:W], pS[si][:, 0:W], AF.Exp,
                                bias=mshift[:, 0:1], scale=SCALE,
                                accum_out=cs_sb[:, si:si + 1]), inc="AE")
                            st[f"bexp{si}"] = ("AE", ex)
                            col = h * 4 + qs
                            E.wait("vector", ("AE", ex))
                            if first:
                                lv = E.op("vector", lambda e, col=col, si=si: e.tensor_copy(
                                    l_sb[:, col:col + 1], cs_sb[:, si:si + 1]), inc="VE")
                            else:
                                lv = E.op("vector", lambda e, col=col, si=si: e.tensor_tensor(
                                    l_sb[:, col:col + 1], l_sb[:, col:col + 1],
                                    cs_sb[:, si:si + 1], ALU.add), inc="VE")
                            st[f"bcs{si}"] = ("VE", lv)
                            nsub = (W + 127) // 128
                            rels = [f"bg{gci}"]
                            if c == ngr - 1:
                                rels.append("bq_rel")
                            push_inst((si, W, nsub, c, h, qs, col, first, ex, rels))
                            inst += 1
                flush_insts()
                if not B_FIN:
                    continue
                E.sbar("vector", "VE")
                E.op("vector", lambda e: e.reciprocal(rl_sb[:, :], l_sb[:, :]), inc="VE")
                E.sbar("vector", "VE")
                E.wait("vector", st.get("bat_rel"))
                for h in range(16):
                    for qs in range(4):
                        col = h * 4 + qs
                        E.wait("vector", st.get("bobf"))
                        ob = E.op("vector", lambda e, col=col: e.tensor_scalar_mul(
                            obf[:, :], Oacc[:, col * 128:col * 128 + 128],
                            rl_sb[:, col:col + 1]), inc="VE")
                        E.wait("tensor", ("VE", ob))
                        E.wait("tensor", st.get("bfinc"))
                        E.wait("tensor", st.get("bpp0"))
                        E.op("tensor", lambda e, qs=qs: e.transpose(
                            pP[0][:, qs * 128:qs * 128 + 128], obf[:, :], id_sb[:, :]),
                            inc="TE")
                        tn = E.cnt("TE")
                        st["bobf"] = ("TE", tn)
                        E.wait("vector", ("TE", tn))
                        cpa = E.op("vector", lambda e, h=h, qs=qs: e.tensor_copy(
                            att_st[:, h, qs * 128:qs * 128 + 128],
                            pP[0][:, qs * 128:qs * 128 + 128]), inc="VE")
                        st["bfinc"] = ("VE", cpa)
                        st["bpp0"] = ("VE", cpa)
                fin = ("VE", E.cnt("VE"))
                E.wait("sync", fin)
                atn = E.op("sync", lambda e, t0=t0: e.dma_start(
                    out=r128(s_at_h)[:, :, t0:t0 + 512], in_=att_st[:, :, :]), inc="SAT", k=16)
                st["bat_rel"] = ("SAT", atn)
            B_f = {"sync": ("SAT", E.cnt("SAT")), "tensor": ("TE", E.cnt("TE")),
                   "vector": ("VE", E.cnt("VE")), "scalar": ("AE", E.cnt("AE"))}
        E.barrier(B_f)

        with contextlib.ExitStack() as pst:
            psb = lambda name, shape, dt: pst.enter_context(nc.sbuf_tensor(name, shape, dt))
            pps = lambda name, shape, dt: pst.enter_context(nc.psum_tensor(name, shape, dt))
            arena = [psb(f"arena{i}", [128, 16, 512], BF16) for i in range(3)]
            xc_sb = psb("xc_sbC", [128, 4, D], F32)
            yz_sb = psb("yz_sb", [128, 4, D], F32)
            gT_sb = psb("gT_sb", [128, 32, 512], BF16)
            h2_bf = psb("h2_bf", [128, D], BF16)
            h2T_sb = psb("h2T_sb", [128, 16, 512], BF16)
            sg_sb = [psb(f"sg_sb{i}", [128, 512], BF16) for i in range(2)]
            t_scr = psb("t_scr", [128, D], F32)
            out_sb = psb("out_sb", [128, D], F32)
            sums3 = psb("sums3", [128, 4], F32)
            ss3 = psb("ss3", [128, 4], F32)
            nmu3 = psb("nmu3", [128, 4], F32)
            rst3 = psb("rst3", [128, 4], F32)
            s43sets = [(sums3[:, i:i + 1], ss3[:, i:i + 1], nmu3[:, i:i + 1],
                        rst3[:, i:i + 1]) for i in range(4)]
            pG = [pps(f"pG{i}", [128, 512], F32) for i in range(2)]
            pU = [pps(f"pU{i}", [128, 512], F32) for i in range(2)]
            pD = [pps(f"pD{i}", [128, 512], F32) for i in range(2)]
            pH = pps("pH", [128, 2048], BF16)

            E.wait("sync", ("CVT", E.cnt("CVT")))
            arena_i = 0

            def arena_load(src_ap, pin=False, ring3=False, q="sync"):
                nonlocal arena_i
                if pin:
                    sl = 2
                else:
                    sl = arena_i % (3 if ring3 else 2)
                    arena_i += 1
                E.wait(q, st.get(f"car{sl}"))
                n = E.op(q, lambda e, sl=sl, src_ap=src_ap: e.dma_start(
                    out=arena[sl][:, :, :], in_=src_ap), inc=f"CA{sl}", k=16)
                return sl, (f"CA{sl}", n)

            for t in range(min(NT, C_MAX_T)):
                t0 = t * 512
                ti = t % 2
                arena_i = 0
                sl_a, af_a = arena_load(r128(s_at_h)[:, :, t0:t0 + 512], pin=True)
                E.wait("sync", st.get("cx_rel"))
                for qs in range(4):
                    xn = E.op("sync", lambda e, t0=t0, qs=qs: e.dma_start(
                        out=xc_sb[:, qs, :], in_=x_h[t0 + qs * 128:t0 + qs * 128 + 128, :]),
                        inc=f"CX{ti}", k=16)
                cx_rdy = (f"CX{ti}", xn)
                for fo in range(4):
                    sl, af = arena_load(r128(wo_b_h)[:, :, fo * 512:fo * 512 + 512])
                    E.wait("tensor", af)
                    E.wait("tensor", af_a)
                    for qs in range(4):
                        pi = (fo * 4 + qs) % 2
                        E.wait("tensor", st.get(f"cpg{pi}"))
                        for kc in range(16):
                            E.op("tensor", lambda e, kc=kc, qs=qs, sl=sl, pi=pi, sl_a=sl_a: e.matmul(
                                pG[pi][:, :], arena[sl_a][:, kc, qs * 128:qs * 128 + 128],
                                arena[sl][:, kc, :], start=(kc == 0), stop=(kc == 15)),
                                inc=("TE" if kc == 15 else None))
                        mmn = E.cnt("TE")
                        if qs == 3:
                            st[f"car{sl}"] = ("TE", mmn)
                            if fo == 3:
                                st[f"car{sl_a}"] = ("TE", mmn)
                        E.wait("vector", ("TE", mmn))
                        E.wait("vector", st.get("cyz_w"))
                        ev = E.op("vector", lambda e, qs=qs, fo=fo, pi=pi: e.tensor_copy(
                            yz_sb[:, qs, fo * 512:fo * 512 + 512], pG[pi][:, :]), inc="VE")
                        st[f"cpg{pi}"] = ("VE", ev)
                yrdy = ("VE", E.cnt("VE"))
                if dbg:
                    E.wait("sync", yrdy)
                    E.op("sync", lambda e, t0=t0: e.dma_start(
                        out=d_wo_h.rearrange("(a p) d -> p a d", p=128)[:, t0 // 128:t0 // 128 + 4, :],
                        in_=yz_sb[:, :, :]), inc="DBG", k=16)
                for qs in range(4):
                    E.wait("vector", st.get("ct_rel"))
                    apf = ln_chain(yz_sb[:, qs, :], t_scr[:, :], D, epsln,
                                   s43sets[qs % 4], f"lnC{qs % 4}",
                                   src_f=(yrdy if qs == 0 else None))
                    E.wait("gpsimd", apf)
                    E.wait("gpsimd", cx_rdy)
                    gadd = E.op("gpsimd", lambda e, qs=qs: e.tensor_tensor(
                        xc_sb[:, qs, :], xc_sb[:, qs, :], t_scr[:, :], ALU.add), inc="GE")
                    st["ct_rel"] = ("GE", gadd)
                    E.wait("vector", ("GE", gadd))
                    E.wait("scalar", ("GE", gadd))
                    E.wait("vector", st.get("ch2"))
                    apf2 = ln_chain(xc_sb[:, qs, :], h2_bf[:, :],
                                    D, epsln, s43sets[(qs + 2) % 4], f"lnC{(qs + 2) % 4}",
                                    src_f=None)
                    E.wait("tensor", apf2)
                    E.wait("tensor", st.get("cpH"))
                    for cc in range(16):
                        E.op("tensor", lambda e, cc=cc: e.transpose(
                            pH[:, cc * 128:(cc + 1) * 128], h2_bf[:, cc * 128:(cc + 1) * 128],
                            id_sb[:, :]), inc=("TE" if cc == 15 else None))
                    tn = E.cnt("TE")
                    st["ch2"] = ("TE", tn)
                    E.wait("vector", ("TE", tn))
                    E.wait("vector", st.get("ch2T"))
                    cph = E.op("vector", lambda e, qs=qs: e.tensor_copy(
                        h2T_sb[:, :, qs * 128:qs * 128 + 128],
                        pH[:, :]), inc="VE")
                    st["cpH"] = ("VE", cph)
                st["cyz_w2"] = ("VE", E.cnt("VE"))
                if dbg:
                    E.wait("sync", st.get("ct_rel"))
                    E.op("sync", lambda e, t0=t0: e.dma_start(
                        out=d_x2_h.rearrange("(a p) d -> p a d", p=128)[:, t0 // 128:t0 // 128 + 4, :],
                        in_=xc_sb[:, :, :]), inc="DBG", k=16)
                h2T_rdy = ("VE", E.cnt("VE"))
                E.wait("tensor", h2T_rdy)
                for half in range(2):
                    hb = half * 32
                    for hcg in range(8):
                        slg, afg = arena_load(
                            r128(wg_b_h)[:, :, (hb + hcg * 4) * 128:(hb + hcg * 4) * 128 + 512],
                            ring3=True)
                        slu, afu = arena_load(
                            r128(wu_b_h)[:, :, (hb + hcg * 4) * 128:(hb + hcg * 4) * 128 + 512],
                            ring3=True)
                        for hl in range(4):
                            hc = hcg * 4 + hl
                            pi = hc % 2
                            E.wait("tensor", afg)
                            E.wait("tensor", st.get(f"cpg{pi}"))
                            for kc in range(16):
                                E.op("tensor", lambda e, kc=kc, hl=hl, slg=slg, pi=pi: e.matmul(
                                    pG[pi][:, :], arena[slg][:, kc, hl * 128:hl * 128 + 128],
                                    h2T_sb[:, kc, :], start=(kc == 0), stop=(kc == 15)),
                                    inc=("TE" if kc == 15 else None))
                            gmm = E.cnt("TE")
                            E.wait("tensor", afu)
                            E.wait("tensor", st.get(f"cpu{pi}"))
                            for kc in range(16):
                                E.op("tensor", lambda e, kc=kc, hl=hl, slu=slu, pi=pi: e.matmul(
                                    pU[pi][:, :], arena[slu][:, kc, hl * 128:hl * 128 + 128],
                                    h2T_sb[:, kc, :], start=(kc == 0), stop=(kc == 15)),
                                    inc=("TE" if kc == 15 else None))
                            umm = E.cnt("TE")
                            if hl == 3:
                                st[f"car{slg}"] = ("TE", gmm)
                                st[f"car{slu}"] = ("TE", umm)
                            E.wait("scalar", ("TE", gmm))
                            E.wait("scalar", st.get(f"csg{pi}"))
                            sg_n = E.op("scalar", lambda e, pi=pi: e.activation(
                                sg_sb[pi][:, :], pG[pi][:, :], AF.Silu), inc="AE")
                            st[f"cpg{pi}"] = ("AE", sg_n)
                            E.wait("vector", ("AE", sg_n))
                            E.wait("vector", ("TE", umm))
                            E.wait("vector", st.get("cgT_rel"))
                            gv = E.op("vector", lambda e, hc=hc, pi=pi: e.tensor_tensor(
                                gT_sb[:, hc % 32, :], sg_sb[pi][:, :], pU[pi][:, :], ALU.mult),
                                inc="VE")
                            st[f"csg{pi}"] = ("VE", gv)
                            st[f"cpu{pi}"] = ("VE", gv)
                    gT_rdy = ("VE", E.cnt("VE"))
                    E.wait("tensor", gT_rdy)
                    for fo in range(4):
                        wdsl = []
                        for j in range(2):
                            slw, afw = arena_load(
                                r128(wd_b_h)[:, (hb + j * 16):(hb + j * 16) + 16,
                                             fo * 512:fo * 512 + 512],
                                ring3=True, q="scalar")
                            wdsl.append((slw, afw))
                        for qs in range(4):
                            pi = qs % 2
                            E.wait("tensor", st.get(f"cpd{pi}"))
                            for j in range(2):
                                slw, afw = wdsl[j]
                                E.wait("tensor", afw)
                                for hl in range(16):
                                    hc = j * 16 + hl
                                    E.op("tensor", lambda e, hc=hc, hl=hl, qs=qs, slw=slw, pi=pi, j=j: e.matmul(
                                        pD[pi][:, :], gT_sb[:, hc, qs * 128:qs * 128 + 128],
                                        arena[slw][:, hl, :],
                                        start=(j == 0 and hl == 0), stop=(j == 1 and hl == 15)),
                                        inc=("TE" if (j == 1 and hl == 15) else None))
                            dmm = E.cnt("TE")
                            if qs == 3:
                                for (slw, afw) in wdsl:
                                    st[f"car{slw}"] = ("TE", dmm)
                                if fo == 3:
                                    st["cgT_rel"] = ("TE", dmm)
                            E.wait("vector", ("TE", dmm))
                            E.wait("vector", st.get("cyz_w2"))
                            if half == 0:
                                ev = E.op("vector", lambda e, qs=qs, fo=fo, pi=pi: e.tensor_copy(
                                    yz_sb[:, qs, fo * 512:fo * 512 + 512], pD[pi][:, :]),
                                    inc="VE")
                            else:
                                ev = E.op("vector", lambda e, qs=qs, fo=fo, pi=pi: e.tensor_tensor(
                                    yz_sb[:, qs, fo * 512:fo * 512 + 512],
                                    yz_sb[:, qs, fo * 512:fo * 512 + 512], pD[pi][:, :],
                                    ALU.add), inc="VE")
                            st[f"cpd{pi}"] = ("VE", ev)
                zrdy = ("VE", E.cnt("VE"))
                if dbg:
                    E.wait("sync", zrdy)
                    E.op("sync", lambda e, t0=t0: e.dma_start(
                        out=d_z_h.rearrange("(a p) d -> p a d", p=128)[:, t0 // 128:t0 // 128 + 4, :],
                        in_=yz_sb[:, :, :]), inc="DBG", k=16)
                for qs in range(4):
                    E.wait("vector", st.get("ct_rel"))
                    apf = ln_chain(yz_sb[:, qs, :], t_scr[:, :], D, epsln,
                                   s43sets[qs % 4], f"lnC{qs % 4}",
                                   src_f=(zrdy if qs == 0 else None))
                    E.wait("gpsimd", apf)
                    E.wait("gpsimd", st.get("cout"))
                    E.sbar("gpsimd", "GE")
                    gadd = E.op("gpsimd", lambda e, qs=qs: e.tensor_tensor(
                        out_sb[:, :], xc_sb[:, qs, :], t_scr[:, :], ALU.add), inc="GE")
                    st["ct_rel"] = ("GE", gadd)
                    if qs == 3:
                        st["cx_rel"] = ("GE", gadd)
                    E.wait("sync", ("GE", gadd))
                    yo = E.op("sync", lambda e, t0=t0, qs=qs: e.dma_start(
                        out=y_h[t0 + qs * 128:t0 + qs * 128 + 128, :], in_=out_sb[:, :]),
                        inc="YO", k=16)
                    st["cout"] = ("YO", yo)
                st["cyz_w"] = ("VE", E.cnt("VE"))
            E.wait("sync", ("YO", E.cnt("YO")))
            if dbg:
                E.wait("sync", ("DBG", E.cnt("DBG")))

        with nc.Block() as block:
            @block.sync
            def _(e):
                for f in E.ops["sync"]:
                    f(e)

            @block.tensor
            def _(e):
                for f in E.ops["tensor"]:
                    f(e)

            @block.vector
            def _(e):
                for f in E.ops["vector"]:
                    f(e)

            @block.scalar
            def _(e):
                for f in E.ops["scalar"]:
                    f(e)

            @block.gpsimd
            def _(e):
                for f in E.ops["gpsimd"]:
                    f(e)

    return nc


def _prep_inputs(inputs, S):
    T = lambda a: np.ascontiguousarray(np.asarray(a).T.astype(np.float32))
    wqr = T(inputs["L0_attn_wq_rope"])
    tq = np.empty(NH * DR, dtype=np.int64)
    for d in range(NH * DR):
        H, r = d // DR, d % DR
        tq[d] = (512 if r % 2 else 0) + H * 32 + r // 2
    wqr_p = np.empty_like(wqr)
    wqr_p[:, tq] = wqr
    wkr = T(inputs["L0_attn_wk_rope"])
    tk = np.empty(DR, dtype=np.int64)
    for r in range(DR):
        tk[r] = (32 if r % 2 else 0) + r // 2
    wkr_p = np.empty_like(wkr)
    wkr_p[:, tk] = wkr
    ident = np.eye(128, dtype=np.float32)
    ii = np.arange(128)
    trimask = np.where(ii[None, :] <= ii[:, None], 0.0, NEGBIG).astype(np.float32)
    inv = 10000.0 ** (-(np.arange(32, dtype=np.float64) * 2.0 / DR))
    fr = np.outer(inv, np.arange(S, dtype=np.float64))
    rcos = np.tile(np.cos(fr), (4, 1)).astype(np.float32)
    rsin = np.tile(np.sin(fr), (4, 1)).astype(np.float32)
    return dict(
        wqd=T(inputs["L0_attn_wq_down"]), wqu=T(inputs["L0_attn_wq_up"]), wqr=wqr_p,
        qnw=np.ascontiguousarray(np.asarray(inputs["L0_attn_q_norm_w"], dtype=np.float32).reshape(QR // 128, 128).T),
        wkd=T(inputs["L0_attn_wkv_down"]),
        kvw=np.ascontiguousarray(np.asarray(inputs["L0_attn_kv_norm_w"], dtype=np.float32).reshape(KVR // 128, 128).T),
        wku=T(inputs["L0_attn_wkv_up"]), wkr=wkr_p, wo=T(inputs["L0_attn_wo"]),
        wg=T(inputs["L0_ffn_w_gate"]), wu=T(inputs["L0_ffn_w_up"]),
        wd=T(inputs["L0_ffn_w_down"]), ident=ident, trimask=trimask,
        rcos=rcos, rsin=rsin)


def kernel(**inputs):
    S = 4096
    x = np.asarray(inputs["L0_attn_x"])
    B = x.shape[0]
    common = _prep_inputs(inputs, S)
    in_maps = [dict(common, x=np.ascontiguousarray(x[c], dtype=np.float32)) for c in range(B)]
    nc = build_program(S)
    res = run_bass_kernel_spmd(nc, in_maps, core_ids=list(range(B)))
    out = np.stack([np.asarray(res.results[c]["y"]) for c in range(B)], axis=0)
    return out.astype(np.float32)


# version 06
# speedup vs baseline: 1.4266x; 1.0188x over previous
import math
import contextlib
import numpy as np
import concourse.bass as bass
import concourse.mybir as mybir
from concourse.bass_utils import run_bass_kernel_spmd

F32 = mybir.dt.float32
BF16 = mybir.dt.bfloat16
AF = mybir.ActivationFunctionType
ALU = mybir.AluOpType
AX = mybir.AxisListType

D = 2048
NH = 16
QR = 1536
KVR = 512
DN = 128
DR = 64
HID = 8192
SCALE = 1.0 / math.sqrt(DN + DR)
MSHIFT = -10.0
EPS_LN = 1e-5
EPS_RMS = 1e-6
NEGBIG = -1.0e30
B_MAX_QT = 10**9
B_MAX_CH = 10**9
B_PREFETCH = True
B_H_MAX = 16
B_FIN = True
C_MAX_T = 10**9
A_MAX_T = 10**9


class Emit:
    ENGINES = ("sync", "tensor", "vector", "scalar", "gpsimd")

    def __init__(self, nc, stack):
        self.nc = nc
        self.stack = stack
        self.ops = {e: [] for e in self.ENGINES}
        self.h = {}
        self.n = {}

    def sem(self, name):
        if name not in self.h:
            self.h[name] = self.stack.enter_context(self.nc.semaphore(name))
            self.n[name] = 0
        return self.h[name]

    def op(self, eng, fn, inc=None, k=1):
        if inc is None:
            self.ops[eng].append(fn)
            return None
        hh = self.sem(inc)
        self.n[inc] += k
        self.ops[eng].append(lambda e, f=fn, hh=hh, k=k: f(e).then_inc(hh, k))
        return self.n[inc]

    def wait(self, eng, fence):
        if fence is None:
            return
        name, val = fence
        if val is None or val <= 0:
            return
        hh = self.sem(name)
        self.ops[eng].append(lambda e, hh=hh, v=val: e.wait_ge(hh, v))

    def cnt(self, name):
        self.sem(name)
        return self.n[name]

    def sbar(self, eng, ctr):
        self.wait(eng, (ctr, self.n.get(ctr, 0)))

    def barrier(self, fences):
        for eng in self.ENGINES:
            for other, f in fences.items():
                if other != eng:
                    self.wait(eng, f)


def build_program(S, dbg=False):
    nc = bass.Bass("TRN2", target_bir_lowering=False)
    NT = S // 512
    skind = "ExternalOutput" if dbg else "Internal"

    x_h = nc.dram_tensor("x", [S, D], F32, kind="ExternalInput").ap()
    wqd_h = nc.dram_tensor("wqd", [D, QR], F32, kind="ExternalInput").ap()
    wqu_h = nc.dram_tensor("wqu", [QR, NH * DN], F32, kind="ExternalInput").ap()
    wqr_h = nc.dram_tensor("wqr", [QR, NH * DR], F32, kind="ExternalInput").ap()
    qnw_h = nc.dram_tensor("qnw", [128, QR // 128], F32, kind="ExternalInput").ap()
    wkd_h = nc.dram_tensor("wkd", [D, KVR], F32, kind="ExternalInput").ap()
    kvw_h = nc.dram_tensor("kvw", [128, KVR // 128], F32, kind="ExternalInput").ap()
    wku_h = nc.dram_tensor("wku", [KVR, NH * 256], F32, kind="ExternalInput").ap()
    wkr_h = nc.dram_tensor("wkr", [D, DR], F32, kind="ExternalInput").ap()
    wo_h = nc.dram_tensor("wo", [D, D], F32, kind="ExternalInput").ap()
    wg_h = nc.dram_tensor("wg", [D, HID], F32, kind="ExternalInput").ap()
    wu_h = nc.dram_tensor("wu", [D, HID], F32, kind="ExternalInput").ap()
    wd_h = nc.dram_tensor("wd", [HID, D], F32, kind="ExternalInput").ap()
    id_h = nc.dram_tensor("ident", [128, 128], F32, kind="ExternalInput").ap()
    tm_h = nc.dram_tensor("trimask", [128, 128], F32, kind="ExternalInput").ap()
    rc_h = nc.dram_tensor("rcos", [128, S], F32, kind="ExternalInput").ap()
    rs_h = nc.dram_tensor("rsin", [128, S], F32, kind="ExternalInput").ap()
    y_h = nc.dram_tensor("y", [S, D], F32, kind="ExternalOutput").ap()
    s_h_h = nc.dram_tensor("s_h", [D, S], BF16, kind=skind).ap()
    s_qn_h = nc.dram_tensor("s_qn", [NH * DN, S], BF16, kind=skind).ap()
    s_qp_h = nc.dram_tensor("s_qp", [NH * DR, S], BF16, kind=skind).ap()
    s_kn_h = nc.dram_tensor("s_kn", [NH * DN, S], BF16, kind=skind).ap()
    s_v_h = nc.dram_tensor("s_v", [S, NH * DN], BF16, kind=skind).ap()
    s_at_h = nc.dram_tensor("s_at", [D, S], BF16, kind=skind).ap()
    if dbg:
        d_wo_h = nc.dram_tensor("d_wo", [S, D], F32, kind="ExternalOutput").ap()
        d_x2_h = nc.dram_tensor("d_x2", [S, D], F32, kind="ExternalOutput").ap()
        d_z_h = nc.dram_tensor("d_z", [S, D], F32, kind="ExternalOutput").ap()
    wo_b_h = nc.dram_tensor("wo_b", [D, D], BF16, kind="Internal").ap()
    wg_b_h = nc.dram_tensor("wg_b", [D, HID], BF16, kind="Internal").ap()
    wu_b_h = nc.dram_tensor("wu_b", [D, HID], BF16, kind="Internal").ap()
    wd_b_h = nc.dram_tensor("wd_b", [HID, D], BF16, kind="Internal").ap()

    r128 = lambda ap_: ap_.rearrange("(c p) t -> p c t", p=128)
    r64 = lambda ap_: ap_.rearrange("(c p) t -> p c t", p=64)

    stack = contextlib.ExitStack()
    with stack:
        E = Emit(nc, stack)
        sb = lambda name, shape, dt: stack.enter_context(nc.sbuf_tensor(name, shape, dt))
        st = {}

        id_sb = sb("id_sb", [128, 128], BF16)
        tm_sb = sb("tm_sb", [128, 128], F32)
        krope = sb("krope", [64, S], BF16)
        epsln = sb("epsln", [128, 1], F32)
        epsrm = sb("epsrm", [128, 1], F32)
        mshift = sb("mshift", [128, 1], F32)
        junk = sb("junk", [128, 2048], BF16)

        E.op("gpsimd", lambda e: e.dma_start(out=id_sb[:, :], in_=id_h[:, :]), inc="LW0", k=16)
        E.op("gpsimd", lambda e: e.dma_start(out=tm_sb[:, :], in_=tm_h[:, :]), inc="LW0", k=16)
        E.op("gpsimd", lambda e: e.memset(epsln[:, :], EPS_LN), inc="GE")
        E.op("gpsimd", lambda e: e.memset(epsrm[:, :], EPS_RMS), inc="GE")
        E.op("gpsimd", lambda e: e.memset(mshift[:, :], MSHIFT), inc="GE")
        P0_LW = ("LW0", E.cnt("LW0"))
        P0_GE = ("GE", E.cnt("GE"))

        def ln_chain(src, out, width, eps_tile, s4, tag, src_f=None, rms=False,
                     ss_ready=None, nss=1):
            sums, ss, nmu, rst = s4
            prev = st.get(tag)
            if src_f is not None:
                E.wait("vector", src_f)
                E.wait("scalar", src_f)
            if prev is not None:
                E.wait("vector", ("AE", prev[1]))
                E.wait("scalar", ("VE", prev[0]))
            if not rms:
                E.op("vector", lambda e: e.reduce_sum(sums, src, axis=AX.X), inc="VE")
            if ss_ready is None:
                E.sbar("scalar", "AE")
                a_sq = E.op("scalar", lambda e: e.activation(junk[:, 0:width], src, AF.Square,
                                                             accum_out=ss[:, 0:1]), inc="AE")
            else:
                a_sq = ss_ready[1]
                E.wait("scalar", ss_ready)
            E.wait("vector", ("AE", a_sq))
            if nss > 1:
                E.op("vector", lambda e: e.reduce_sum(sums if rms else rst, ss[:, 0:nss],
                                                      axis=AX.X), inc="VE")
            if rms:
                srcss = sums if nss > 1 else ss
                E.sbar("vector", "VE")
                E.op("vector", lambda e: e.tensor_scalar_mul(rst, srcss[:, 0:1], 1.0 / width),
                     inc="VE")
            else:
                E.sbar("vector", "VE")
                E.op("vector", lambda e: e.tensor_scalar_mul(nmu, sums, -1.0 / width), inc="VE")
                if nss > 1:
                    E.op("vector", lambda e: e.tensor_scalar_mul(rst, rst, 1.0 / width),
                         inc="VE")
                else:
                    E.op("vector", lambda e: e.tensor_scalar_mul(rst, ss[:, 0:1], 1.0 / width),
                         inc="VE")
                E.sbar("vector", "VE")
                E.op("vector", lambda e: e.tensor_tensor(sums, nmu, nmu, ALU.mult), inc="VE")
                E.sbar("vector", "VE")
                E.op("vector", lambda e: e.tensor_tensor(rst, rst, sums, ALU.subtract), inc="VE")
            v3 = E.cnt("VE")
            E.wait("scalar", ("VE", v3))
            a_rt = E.op("scalar", lambda e: e.activation(rst, rst, AF.Sqrt,
                                                         bias=eps_tile[:, 0:1]), inc="AE")
            E.wait("vector", ("AE", a_rt))
            E.op("vector", lambda e: e.reciprocal(rst, rst), inc="VE")
            E.sbar("vector", "VE")
            if rms:
                vap = E.op("vector", lambda e: e.tensor_scalar_mul(out, src, rst[:, 0:1]),
                           inc="VE")
            else:
                vap = E.op("vector", lambda e: e.tensor_scalar(out, src, nmu[:, 0:1], rst[:, 0:1],
                                                               ALU.add, ALU.mult), inc="VE")
            st[tag] = (vap, a_rt)
            return ("VE", vap)

        conv_jobs = []
        for (src, dst, rows, rstep) in ((wo_h, wo_b_h, D, 1024), (wg_h, wg_b_h, D, 256),
                                        (wu_h, wu_b_h, D, 256), (wd_h, wd_b_h, HID, 1024)):
            for r0 in range(0, rows, rstep):
                conv_jobs.append((src, dst, r0, r0 + rstep))
        conv_it = iter(conv_jobs)

        def drip_conv(n):
            for _ in range(n):
                job = next(conv_it, None)
                if job is None:
                    return
                src, dst, r0, r1 = job
                E.op("gpsimd", lambda e, src=src, dst=dst, r0=r0, r1=r1: e.dma_start(
                    out=dst[r0:r1, :], in_=src[r0:r1, :]), inc="CVT", k=16)

        with contextlib.ExitStack() as pst:
            psb = lambda name, shape, dt: pst.enter_context(nc.sbuf_tensor(name, shape, dt))
            pps = lambda name, shape, dt: pst.enter_context(nc.psum_tensor(name, shape, dt))
            wqd_sb = psb("wqd_sb", [128, 16, QR], BF16)
            wqu_s = [psb(f"wqu_s{i}", [128, 12, 512], BF16) for i in range(2)]
            wqr_sb = psb("wqr_sb", [128, 12, NH * DR], BF16)
            qnw_sb = psb("qnw_sb", [128, 12], F32)
            x_sb = [psb(f"x_sb{i}", [128, D], F32) for i in range(2)]
            h_sb = [psb(f"h_sb{i}", [128, D], BF16) for i in range(2)]
            hT_sb = psb("hT_sb", [128, 16, 512], BF16)
            qc_sb = psb("qc_sb", [128, QR], F32)
            qcn_sb = psb("qcn_sb", [128, QR], BF16)
            qcT_sb = [psb(f"qcT_sb{i}", [128, 12, 512], BF16) for i in range(2)]
            qn_st = [psb(f"qn_st{i}", [128, 4, 512], BF16) for i in range(2)]
            qpe_sb = psb("qpe_sb", [128, 8, 512], BF16)
            ro_e = psb("ro_e", [128, 512], BF16)
            ro_o = psb("ro_o", [128, 512], BF16)
            rt1 = psb("rt1", [128, 512], BF16)
            rt2 = psb("rt2", [128, 512], BF16)
            rc_sb = psb("rc_sb", [128, 512], F32)
            rs_sb = psb("rs_sb", [128, 512], F32)
            sums = psb("sums", [128, 1], F32)
            ss = psb("ss", [128, 3], F32)
            nmu = psb("nmu", [128, 1], F32)
            rst = psb("rst", [128, 1], F32)
            s4 = (sums[:, 0:1], ss, nmu[:, 0:1], rst[:, 0:1])
            pA = [pps(f"pA{i}", [128, 512], F32) for i in range(2)]
            pT = [pps(f"pT{i}", [128, 2048], BF16) for i in range(2)]

            E.op("gpsimd", lambda e: e.dma_start(out=wqd_sb[:, :, :], in_=r128(wqd_h)),
                 inc="LW1", k=16)
            E.op("gpsimd", lambda e: e.dma_start(out=wqr_sb[:, :, :], in_=r128(wqr_h)),
                 inc="LW1", k=16)
            E.op("gpsimd", lambda e: e.dma_start(out=qnw_sb[:, :], in_=qnw_h[:, :]),
                 inc="LW1", k=16)
            wA1 = ("LW1", E.cnt("LW1"))
            E.wait("vector", wA1)
            E.wait("tensor", wA1)
            E.wait("tensor", P0_LW)
            E.wait("vector", P0_LW)

            def load_wqu(t, g):
                sl = (t * 4 + g) % 2
                E.wait("gpsimd", st.get(f"wqu{sl}"))
                n = E.op("gpsimd", lambda e, g=g, sl=sl: e.dma_start(
                    out=wqu_s[sl][:, :, :],
                    in_=r128(wqu_h)[:, :, g * 512:g * 512 + 512]), inc=f"WQ{sl}", k=16)
                drip_conv(1)
                return sl, (f"WQ{sl}", n)

            def a1_front(t):
                t0 = t * 512
                qslot = t % 2
                E.wait("sync", st.get("a1ropeV"))
                E.op("sync", lambda e, t0=t0: e.dma_start(out=rc_sb[:, :],
                     in_=rc_h[:, t0:t0 + 512]), inc="RT1", k=16)
                rs_n = E.op("sync", lambda e, t0=t0: e.dma_start(out=rs_sb[:, :],
                     in_=rs_h[:, t0:t0 + 512]), inc="RT1", k=16)
                st["a1rt"] = ("RT1", rs_n)
                for qs in range(4):
                    xi = qs % 2
                    E.wait("sync", st.get(f"a1x{xi}"))
                    xn = E.op("sync", lambda e, t0=t0, qs=qs, xi=xi: e.dma_start(
                        out=x_sb[xi][:, :], in_=x_h[t0 + qs * 128:t0 + qs * 128 + 128, :]),
                        inc=f"XA{xi}", k=16)
                    E.wait("vector", st.get(f"a1h{xi}"))
                    apf = ln_chain(x_sb[xi][:, :], h_sb[xi][:, :], D, epsln, s4, "sA1",
                                   src_f=(f"XA{xi}", xn))
                    st[f"a1x{xi}"] = apf
                    E.wait("tensor", apf)
                    E.wait("tensor", st.get(f"a1pt{qs % 2}"))
                    for c in range(16):
                        E.op("tensor", lambda e, c=c, qs=qs, xi=xi: e.transpose(
                            pT[qs % 2][:, c * 128:(c + 1) * 128],
                            h_sb[xi][:, c * 128:(c + 1) * 128], id_sb[:, :]),
                            inc=("TE" if c == 15 else None))
                    tn = E.cnt("TE")
                    st[f"a1h{xi}"] = ("TE", tn)
                    E.wait("vector", ("TE", tn))
                    E.wait("vector", st.get(f"a1hT{qs}"))
                    E.wait("vector", st.get(f"a1hTs{qs}"))
                    cp = E.op("vector", lambda e, qs=qs: e.tensor_copy(
                        hT_sb[:, :, qs * 128:qs * 128 + 128],
                        pT[qs % 2][:, :]), inc="VE")
                    st[f"a1pt{qs % 2}"] = ("VE", cp)
                    E.wait("sync", ("VE", cp))
                    shn = E.op("sync", lambda e, t0=t0, qs=qs: e.dma_start(
                        out=r128(s_h_h)[:, :, t0 + qs * 128:t0 + qs * 128 + 128],
                        in_=hT_sb[:, :, qs * 128:qs * 128 + 128]), inc=f"SH{qs}", k=16)
                    st[f"a1hTs{qs}"] = (f"SH{qs}", shn)

                    for fo in range(3):
                        pi = (qs * 3 + fo) % 2
                        E.wait("tensor", st.get(f"a1pa{pi}"))
                        E.wait("tensor", ("VE", cp))
                        for kc in range(16):
                            E.op("tensor", lambda e, kc=kc, fo=fo, qs=qs, pi=pi: e.matmul(
                                pA[pi][:, :], hT_sb[:, kc, qs * 128:qs * 128 + 128],
                                wqd_sb[:, kc, fo * 512:fo * 512 + 512],
                                start=(kc == 0), stop=(kc == 15)),
                                inc=("TE" if kc == 15 else None))
                        mmn = E.cnt("TE")
                        E.wait("vector", ("TE", mmn))
                        if fo == 0:
                            E.wait("vector", st.get("a1qcr"))
                        ev = E.op("vector", lambda e, fo=fo, pi=pi: e.tensor_copy(
                            qc_sb[:, fo * 512:fo * 512 + 512], pA[pi][:, :]), inc="VE")
                        st[f"a1pa{pi}"] = ("VE", ev)
                        E.wait("scalar", ("VE", ev))
                        E.wait("scalar", st.get("a1ssr"))
                        E.sbar("scalar", "AE")
                        aq = E.op("scalar", lambda e, fo=fo: e.activation(
                            junk[:, 0:512], qc_sb[:, fo * 512:fo * 512 + 512], AF.Square,
                            accum_out=ss[:, fo:fo + 1]), inc="AE")
                    E.wait("vector", st.get("a1qcn"))
                    apf = ln_chain(qc_sb[:, :], qcn_sb[:, :], QR, epsrm, s4, "sA1",
                                   rms=True, ss_ready=("AE", aq), nss=3)
                    st["a1qcr"] = ("AE", aq)
                    st["a1ssr"] = ("VE", E.cnt("VE"))
                    E.wait("tensor", apf)
                    E.wait("tensor", st.get(f"a1pt{qs % 2}"))
                    for c in range(12):
                        E.op("tensor", lambda e, c=c, qs=qs: e.transpose(
                            pT[qs % 2][:, c * 128:(c + 1) * 128],
                            qcn_sb[:, c * 128:(c + 1) * 128], id_sb[:, :]),
                            inc=("TE" if c == 11 else None))
                    tn = E.cnt("TE")
                    st["a1qcn"] = ("TE", tn)
                    E.wait("vector", ("TE", tn))
                    E.wait("vector", st.get(f"a1qcT{qslot}"))
                    for kc in range(12):
                        cp2 = E.op("vector", lambda e, qs=qs, kc=kc, qslot=qslot: e.tensor_scalar_mul(
                            qcT_sb[qslot][:, kc, qs * 128:qs * 128 + 128],
                            pT[qs % 2][:, kc * 128:kc * 128 + 128],
                            qnw_sb[:, kc:kc + 1]), inc="VE")
                    st[f"a1pt{qs % 2}"] = ("VE", cp2)
                    st[f"a1hT{qs}"] = ("TE", E.cnt("TE"))
                st[f"a1qcTr{qslot}"] = ("VE", E.cnt("VE"))

            def a1_back(t):
                t0 = t * 512
                qslot = t % 2
                E.wait("tensor", st[f"a1qcTr{qslot}"])
                wq_slots = {}
                wq_slots[0] = load_wqu(t, 0)
                for oc in range(16):
                    g = oc // 4
                    if oc % 4 == 0 and g + 1 < 4:
                        wq_slots[g + 1] = load_wqu(t, g + 1)
                    sl_w, af_w = wq_slots[g]
                    pi = oc % 2
                    E.wait("tensor", st.get(f"a1pa{pi}"))
                    E.wait("tensor", af_w)
                    for kc in range(12):
                        E.op("tensor", lambda e, kc=kc, oc=oc, pi=pi, sl_w=sl_w, qslot=qslot: e.matmul(
                            pA[pi][:, :], wqu_s[sl_w][:, kc, (oc % 4) * 128:(oc % 4) * 128 + 128],
                            qcT_sb[qslot][:, kc, :], start=(kc == 0), stop=(kc == 11)),
                            inc=("TE" if kc == 11 else None))
                    mmn = E.cnt("TE")
                    if oc % 4 == 3:
                        st[f"wqu{sl_w}"] = ("TE", mmn)
                    E.wait("vector", ("TE", mmn))
                    sl, bi = oc % 4, (oc // 4) % 2
                    if sl == 0:
                        E.wait("vector", st.get(f"a1qn{bi}"))
                    ev = E.op("vector", lambda e, sl=sl, bi=bi, pi=pi: e.tensor_copy(
                        qn_st[bi][:, sl, :], pA[pi][:, :]), inc="VE")
                    st[f"a1pa{pi}"] = ("VE", ev)
                    if sl == 3:
                        E.wait("sync", ("VE", ev))
                        stn = E.op("sync", lambda e, t0=t0, oc=oc, bi=bi: e.dma_start(
                            out=r128(s_qn_h)[:, oc - 3:oc + 1, t0:t0 + 512],
                            in_=qn_st[bi][:, :, :]), inc=f"SA{bi}", k=16)
                        st[f"a1qn{bi}"] = (f"SA{bi}", stn)
                E.wait("vector", st.get("a1ropeV"))
                for oc in range(8):
                    pi = oc % 2
                    E.wait("tensor", st.get(f"a1pa{pi}"))
                    for kc in range(12):
                        E.op("tensor", lambda e, kc=kc, oc=oc, pi=pi, qslot=qslot: e.matmul(
                            pA[pi][:, :], wqr_sb[:, kc, oc * 128:oc * 128 + 128],
                            qcT_sb[qslot][:, kc, :], start=(kc == 0), stop=(kc == 11)),
                            inc=("TE" if kc == 11 else None))
                    mmn = E.cnt("TE")
                    E.wait("vector", ("TE", mmn))
                    ev = E.op("vector", lambda e, oc=oc, pi=pi: e.tensor_copy(
                        qpe_sb[:, oc, :], pA[pi][:, :]), inc="VE")
                    st[f"a1pa{pi}"] = ("VE", ev)
                st[f"a1qcT{qslot}"] = ("TE", E.cnt("TE"))
                E.wait("vector", st["a1rt"])
                qp_g = s_qp_h.rearrange("(c q r) t -> c q r t", c=4, q=4, r=64)
                for c in range(4):
                    E.wait("vector", st.get("a1roS"))
                    E.op("vector", lambda e, c=c: e.tensor_tensor(rt1[:, :], qpe_sb[:, c, :],
                         rc_sb[:, :], ALU.mult), inc="VE")
                    E.op("vector", lambda e, c=c: e.tensor_tensor(rt2[:, :], qpe_sb[:, c + 4, :],
                         rs_sb[:, :], ALU.mult), inc="VE")
                    E.sbar("vector", "VE")
                    E.op("vector", lambda e: e.tensor_tensor(ro_e[:, :], rt1[:, :], rt2[:, :],
                         ALU.subtract), inc="VE")
                    E.sbar("vector", "VE")
                    E.op("vector", lambda e, c=c: e.tensor_tensor(rt1[:, :], qpe_sb[:, c, :],
                         rs_sb[:, :], ALU.mult), inc="VE")
                    E.op("vector", lambda e, c=c: e.tensor_tensor(rt2[:, :], qpe_sb[:, c + 4, :],
                         rc_sb[:, :], ALU.mult), inc="VE")
                    E.sbar("vector", "VE")
                    rov = E.op("vector", lambda e: e.tensor_tensor(ro_o[:, :], rt1[:, :], rt2[:, :],
                               ALU.add), inc="VE")
                    E.wait("sync", ("VE", rov))
                    E.op("sync", lambda e, c=c, t0=t0: e.dma_start(
                        out=qp_g[c, :, 0:32, t0:t0 + 512], in_=ro_e[:, :]), inc="RO", k=16)
                    ron = E.op("sync", lambda e, c=c, t0=t0: e.dma_start(
                        out=qp_g[c, :, 32:64, t0:t0 + 512], in_=ro_o[:, :]), inc="RO", k=16)
                    st["a1roS"] = ("RO", ron)
                st["a1ropeV"] = ("VE", E.cnt("VE"))

            a1_front(0)
            for t in range(NT):
                a1_back(t)
                if t + 1 < NT:
                    a1_front(t + 1)
            drip_conv(100)
            A1_f = {"sync": ("RO", E.cnt("RO")), "tensor": ("TE", E.cnt("TE")),
                    "vector": ("VE", E.cnt("VE")), "scalar": ("AE", E.cnt("AE")),
                    "gpsimd": ("LW1", E.cnt("LW1"))}
            A1_extra = [(f"SH{q}", E.cnt(f"SH{q}")) for q in range(4)] + \
                       [("SA0", E.cnt("SA0")), ("SA1", E.cnt("SA1"))]
        E.barrier(A1_f)
        for f in A1_extra:
            for eng in Emit.ENGINES:
                E.wait(eng, f)

        with contextlib.ExitStack() as pst:
            psb = lambda name, shape, dt: pst.enter_context(nc.sbuf_tensor(name, shape, dt))
            pps = lambda name, shape, dt: pst.enter_context(nc.psum_tensor(name, shape, dt))
            wkd_sb = psb("wkd_sb", [128, 16, KVR], BF16)
            wku_sb = psb("wku_sb", [128, 4, NH * 256], BF16)
            wkr_sb = psb("wkr_sb", [128, 16, DR], BF16)
            kvw_sb = psb("kvw_sb", [128, 4], F32)
            hT2 = [psb(f"hT2_{i}", [128, 16, 512], BF16) for i in range(2)]
            kvc_sb = psb("kvc_sb", [128, KVR], F32)
            kvn_sb = psb("kvn_sb", [128, KVR], BF16)
            kvT_sb = psb("kvT_sb", [128, 4, 512], BF16)
            kn_st = [psb(f"kn_st{i}", [128, 4, 512], BF16) for i in range(2)]
            v_bf = psb("v_bf", [128, 512], BF16)
            v_st = psb("v_st", [128, 4, NH * DN], BF16)
            rt1b = psb("rt1b", [32, 512], F32)
            rt2b = psb("rt2b", [32, 512], F32)
            rc2 = psb("rc2", [128, 512], F32)
            rs2 = psb("rs2", [128, 512], F32)
            sums2 = psb("sums2", [128, 1], F32)
            ss2 = psb("ss2", [128, 1], F32)
            nmu2 = psb("nmu2", [128, 1], F32)
            rst2 = psb("rst2", [128, 1], F32)
            s42 = (sums2[:, 0:1], ss2, nmu2[:, 0:1], rst2[:, 0:1])
            pA2 = [pps(f"pA2_{i}", [128, 512], F32) for i in range(2)]
            pT2 = [pps(f"pT2_{i}", [128, 2048], BF16) for i in range(2)]
            pV2 = pps("pV2", [128, 512], BF16)
            pK2 = pps("pK2", [128, 512], F32)

            E.op("gpsimd", lambda e: e.dma_start(out=wkd_sb[:, :, :], in_=r128(wkd_h)),
                 inc="LW2", k=16)
            E.op("gpsimd", lambda e: e.dma_start(out=wku_sb[:, :, :], in_=r128(wku_h)),
                 inc="LW2", k=16)
            E.op("gpsimd", lambda e: e.dma_start(out=wkr_sb[:, :, :], in_=r128(wkr_h)),
                 inc="LW2", k=16)
            E.op("gpsimd", lambda e: e.dma_start(out=kvw_sb[:, :], in_=kvw_h[:, :]),
                 inc="LW2", k=16)
            wA2 = ("LW2", E.cnt("LW2"))
            E.wait("vector", wA2)
            for kc in range(4):
                E.op("vector", lambda e, kc=kc: e.tensor_scalar_mul(
                    wku_sb[:, kc, :], wku_sb[:, kc, :], kvw_sb[:, kc:kc + 1]), inc="VE")
            foldkv = ("VE", E.cnt("VE"))
            E.wait("tensor", foldkv)

            for t in range(NT):
                t0 = t * 512
                ti = t % 2
                E.wait("sync", st.get("a2rope"))
                rs_n = None
                E.op("sync", lambda e, t0=t0: e.dma_start(out=rc2[:, :],
                     in_=rc_h[:, t0:t0 + 512]), inc="RT2", k=16)
                rs_n = E.op("sync", lambda e, t0=t0: e.dma_start(out=rs2[:, :],
                     in_=rs_h[:, t0:t0 + 512]), inc="RT2", k=16)
                E.wait("sync", st.get(f"a2h{ti}"))
                hn = E.op("sync", lambda e, t0=t0, ti=ti: e.dma_start(
                    out=hT2[ti][:, :, :], in_=r128(s_h_h)[:, :, t0:t0 + 512]),
                    inc=f"XB{ti}", k=16)
                for qs in range(4):
                    pi = qs % 2
                    E.wait("tensor", st.get(f"a2pa{pi}"))
                    E.wait("tensor", (f"XB{ti}", hn))
                    for kc in range(16):
                        E.op("tensor", lambda e, kc=kc, qs=qs, pi=pi, ti=ti: e.matmul(
                            pA2[pi][:, :], hT2[ti][:, kc, qs * 128:qs * 128 + 128],
                            wkd_sb[:, kc, :], start=(kc == 0), stop=(kc == 15)),
                            inc=("TE" if kc == 15 else None))
                    mmn = E.cnt("TE")
                    E.wait("vector", ("TE", mmn))
                    E.wait("vector", st.get("a2kvr"))
                    ev = E.op("vector", lambda e, pi=pi: e.tensor_copy(kvc_sb[:, :],
                              pA2[pi][:, :]), inc="VE")
                    st[f"a2pa{pi}"] = ("VE", ev)
                    E.wait("vector", st.get("a2kvn"))
                    apf = ln_chain(kvc_sb[:, :], kvn_sb[:, :], KVR, epsrm, s42, "sA2",
                                   src_f=("VE", ev), rms=True)
                    st["a2kvr"] = ("AE", E.cnt("AE"))
                    E.wait("tensor", apf)
                    E.wait("tensor", st.get(f"a2pt{pi}"))
                    for c in range(4):
                        E.op("tensor", lambda e, c=c, pi=pi: e.transpose(
                            pT2[pi][:, c * 128:(c + 1) * 128],
                            kvn_sb[:, c * 128:(c + 1) * 128], id_sb[:, :]),
                            inc=("TE" if c == 3 else None))
                    tn = E.cnt("TE")
                    st["a2kvn"] = ("TE", tn)
                    E.wait("vector", ("TE", tn))
                    E.wait("vector", st.get("a2kvT"))
                    cp = E.op("vector", lambda e, qs=qs, pi=pi: e.tensor_copy(
                        kvT_sb[:, :, qs * 128:qs * 128 + 128],
                        pT2[pi][:, 0:512]), inc="VE")
                    st[f"a2pt{pi}"] = ("VE", cp)
                kvT_rdy = ("VE", E.cnt("VE"))
                E.wait("tensor", kvT_rdy)
                for oc in range(32):
                    pi = oc % 2
                    E.wait("tensor", st.get(f"a2pa{pi}"))
                    for kc in range(4):
                        E.op("tensor", lambda e, kc=kc, oc=oc, pi=pi: e.matmul(
                            pA2[pi][:, :], wku_sb[:, kc, oc * 128:oc * 128 + 128],
                            kvT_sb[:, kc, :], start=(kc == 0), stop=(kc == 3)),
                            inc=("TE" if kc == 3 else None))
                    mmn = E.cnt("TE")
                    h_ = oc // 2
                    E.wait("vector", ("TE", mmn))
                    if oc % 2 == 0:
                        sl, bi = h_ % 4, (h_ // 4) % 2
                        if sl == 0:
                            E.wait("vector", st.get(f"a2kn{bi}"))
                        ev = E.op("vector", lambda e, sl=sl, bi=bi, pi=pi: e.tensor_copy(
                            kn_st[bi][:, sl, :], pA2[pi][:, :]), inc="VE")
                        st[f"a2pa{pi}"] = ("VE", ev)
                        if sl == 3:
                            E.wait("sync", ("VE", ev))
                            stn = E.op("sync", lambda e, t0=t0, h_=h_, bi=bi: e.dma_start(
                                out=r128(s_kn_h)[:, h_ - 3:h_ + 1, t0:t0 + 512],
                                in_=kn_st[bi][:, :, :]), inc=f"SK{bi}", k=16)
                            st[f"a2kn{bi}"] = (f"SK{bi}", stn)
                    else:
                        E.wait("vector", st.get("a2vbf"))
                        ev = E.op("vector", lambda e, pi=pi: e.tensor_copy(
                            v_bf[:, :], pA2[pi][:, :]), inc="VE")
                        st[f"a2pa{pi}"] = ("VE", ev)
                        E.wait("tensor", ("VE", ev))
                        E.wait("tensor", st.get("a2pv"))
                        for ts_ in range(4):
                            E.op("tensor", lambda e, ts_=ts_: e.transpose(
                                pV2[:, ts_ * 128:ts_ * 128 + 128],
                                v_bf[:, ts_ * 128:ts_ * 128 + 128], id_sb[:, :]),
                                inc=("TE" if ts_ == 3 else None))
                        tn = E.cnt("TE")
                        st["a2vbf"] = ("TE", tn)
                        E.wait("vector", ("TE", tn))
                        if h_ == 0:
                            E.wait("vector", st.get("a2vst"))
                        cpv = E.op("vector", lambda e, h_=h_: e.tensor_copy(
                            v_st[:, :, h_ * 128:h_ * 128 + 128],
                            pV2[:, :]), inc="VE")
                        st["a2pv"] = ("VE", cpv)
                        if h_ == 15:
                            E.wait("sync", ("VE", cpv))
                            stn = E.op("sync", lambda e, t0=t0: e.dma_start(
                                out=s_v_h.rearrange("(s p) d -> p s d", p=128)[:, t0 // 128:t0 // 128 + 4, :],
                                in_=v_st[:, :, :]), inc="SV", k=16)
                            st["a2vst"] = ("SV", stn)
                st["a2kvT"] = ("TE", E.cnt("TE"))
                E.wait("tensor", st.get("a2krP"))
                for kc in range(16):
                    E.op("tensor", lambda e, kc=kc, ti=ti: e.matmul(
                        pK2[0:64, :], wkr_sb[:, kc, :], hT2[ti][:, kc, :],
                        start=(kc == 0), stop=(kc == 15)), inc=("TE" if kc == 15 else None))
                mmn = E.cnt("TE")
                st[f"a2h{ti}"] = ("TE", mmn)
                E.wait("vector", ("TE", mmn))
                E.wait("vector", ("RT2", rs_n))
                E.op("vector", lambda e: e.tensor_tensor(rt1b[:, :], pK2[0:32, :], rc2[0:32, :],
                     ALU.mult), inc="VE")
                E.op("vector", lambda e: e.tensor_tensor(rt2b[:, :], pK2[32:64, :], rs2[0:32, :],
                     ALU.mult), inc="VE")
                E.sbar("vector", "VE")
                E.op("vector", lambda e, t0=t0: e.tensor_tensor(krope[0:32, t0:t0 + 512],
                     rt1b[:, :], rt2b[:, :], ALU.subtract), inc="VE")
                E.sbar("vector", "VE")
                E.op("vector", lambda e: e.tensor_tensor(rt1b[:, :], pK2[0:32, :], rs2[0:32, :],
                     ALU.mult), inc="VE")
                E.op("vector", lambda e: e.tensor_tensor(rt2b[:, :], pK2[32:64, :], rc2[0:32, :],
                     ALU.mult), inc="VE")
                E.sbar("vector", "VE")
                krn = E.op("vector", lambda e, t0=t0: e.tensor_tensor(krope[32:64, t0:t0 + 512],
                           rt1b[:, :], rt2b[:, :], ALU.add), inc="VE")
                st["a2krP"] = ("VE", krn)
                st["a2rope"] = ("VE", krn)
            A2_f = {"sync": ("SV", E.cnt("SV")), "tensor": ("TE", E.cnt("TE")),
                    "vector": ("VE", E.cnt("VE")), "scalar": ("AE", E.cnt("AE")),
                    "gpsimd": ("LW2", E.cnt("LW2"))}
            A2_extra = [("SK0", E.cnt("SK0")), ("SK1", E.cnt("SK1"))]
        E.barrier(A2_f)
        for f in A2_extra:
            for eng in Emit.ENGINES:
                E.wait(eng, f)

        with contextlib.ExitStack() as pst:
            psb = lambda name, shape, dt: pst.enter_context(nc.sbuf_tensor(name, shape, dt))
            pps = lambda name, shape, dt: pst.enter_context(nc.psum_tensor(name, shape, dt))
            qn_sb = psb("qn_sb", [128, 16, 512], BF16)
            qp_sb = psb("qp_sb", [64, 16, 512], BF16)
            kn_g = [psb(f"kn_g{i}", [128, 16, 512], BF16) for i in range(3)]
            v_g = [psb(f"v_g{i}", [128, 4, NH * DN], BF16) for i in range(3)]
            probs = [psb(f"probs{i}", [128, 512], BF16) for i in range(3)]
            probsT = [psb(f"probsT{i}", [128, 512], BF16) for i in range(3)]
            Oacc = psb("Oacc", [128, 64 * 128], F32)
            l_sb = psb("l_sb", [128, 64], F32)
            rl_sb = psb("rl_sb", [128, 64], F32)
            cs_sb = psb("cs_sb", [128, 3], F32)
            obf = psb("obf", [128, 128], BF16)
            att_st = psb("att_st", [128, 16, 512], BF16)
            pS = [pps(f"pS{i}", [128, 512], F32) for i in range(3)]
            pP = [pps(f"pP{i}", [128, 512], BF16) for i in range(3)]
            pO = [pps(f"pO{i}", [128, 512], F32) for i in range(2)]

            inst = 0
            pend_b = []
            pend_c = []
            E.wait("scalar", P0_GE)

            def stage_sb(p):
                (si, W, nsub, c_, h, qs, col, first, ex, rels) = p
                E.wait("tensor", ("AE", ex))
                E.wait("tensor", st.get(f"bpp{si}"))
                for j in range(nsub):
                    E.op("tensor", lambda e, si=si, j=j: e.transpose(
                        pP[si][:, j * 128:j * 128 + 128],
                        probs[si][:, j * 128:j * 128 + 128], id_sb[:, :]),
                        inc=("TE" if j == nsub - 1 else None))
                ptn = E.cnt("TE")
                st[f"bprb{si}"] = ("TE", ptn)
                if si != 1:
                    E.wait("vector", ("TE", ptn))
                    E.wait("vector", st.get(f"bptT{si}"))
                    pc = E.op("vector", lambda e, si=si, W=W: e.tensor_copy(
                        probsT[si][:, 0:W], pP[si][:, 0:W]), inc="VE")
                    pcf = ("VE", pc)
                else:
                    E.wait("scalar", ("TE", ptn))
                    E.wait("scalar", st.get(f"bptT{si}"))
                    pc = E.op("scalar", lambda e, si=si, W=W: e.copy(
                        probsT[si][:, 0:W], pP[si][:, 0:W]), inc="AE")
                    pcf = ("AE", pc)
                st[f"bpp{si}"] = pcf
                p2 = p + (pcf,)
                return p2

            def stage_sc(p):
                (si, W, nsub, c_, h, qs, col, first, ex, rels, pcf) = p
                po = c_ % 2 if False else (sc_ctr[0] % 2)
                sc_ctr[0] += 1
                E.wait("tensor", pcf)
                E.wait("tensor", st.get(f"bpo{po}"))
                for j in range(nsub):
                    gi = c_ % 3
                    E.op("tensor", lambda e, si=si, j=j, gi=gi, po=po, h=h, nsub=nsub: e.matmul(
                        pO[po][:, 0:128], probsT[si][:, j * 128:j * 128 + 128],
                        v_g[gi][:, j, h * 128:h * 128 + 128],
                        start=(j == 0), stop=(j == nsub - 1)),
                        inc=("TE" if j == nsub - 1 else None))
                on = E.cnt("TE")
                st[f"bptT{si}"] = ("TE", on)
                for key in rels:
                    st[key] = ("TE", on)
                E.wait("vector", ("TE", on))
                if first:
                    ov = E.op("vector", lambda e, col=col, po=po: e.tensor_copy(
                        Oacc[:, col * 128:col * 128 + 128], pO[po][:, 0:128]),
                        inc="VE")
                else:
                    ov = E.op("vector", lambda e, col=col, po=po: e.tensor_tensor(
                        Oacc[:, col * 128:col * 128 + 128],
                        Oacc[:, col * 128:col * 128 + 128], pO[po][:, 0:128],
                        ALU.add), inc="VE")
                st[f"bpo{po}"] = ("VE", ov)

            sc_ctr = [0]

            def push_inst(p):
                pend_b.append(p)
                if len(pend_b) > 2:
                    pend_c.append(stage_sb(pend_b.pop(0)))
                if len(pend_c) > 2:
                    stage_sc(pend_c.pop(0))

            def flush_insts():
                while pend_b:
                    pend_c.append(stage_sb(pend_b.pop(0)))
                while pend_c:
                    stage_sc(pend_c.pop(0))

            for qt in range(min(NT, B_MAX_QT)):
                t0 = qt * 512
                E.wait("sync", st.get("bq_rel"))
                E.op("sync", lambda e, t0=t0: e.dma_start(out=qn_sb[:, :, :],
                     in_=r128(s_qn_h)[:, :, t0:t0 + 512]), inc="BQ", k=16)
                qld = E.op("sync", lambda e, t0=t0: e.dma_start(out=qp_sb[:, :, :],
                     in_=r64(s_qp_h)[:, :, t0:t0 + 512]), inc="BQ", k=16)
                ngr = qt + 1

                def load_gran(g):
                    gi = g % 3
                    E.wait("sync", st.get(f"bg{gi}"))
                    E.op("sync", lambda e, g=g, gi=gi: e.dma_start(out=kn_g[gi][:, :, :],
                         in_=r128(s_kn_h)[:, :, g * 512:g * 512 + 512]), inc=f"BK{gi}", k=16)
                    n = E.op("sync", lambda e, g=g, gi=gi: e.dma_start(out=v_g[gi][:, :, :],
                         in_=s_v_h.rearrange("(s p) d -> p s d", p=128)[:, g * 4:g * 4 + 4, :]),
                         inc=f"BK{gi}", k=16)
                    st[f"bgr{gi}"] = (f"BK{gi}", n)

                load_gran(0)
                for c in range(ngr):
                    if c + 1 < ngr:
                        load_gran(c + 1)
                    k0 = c * 512
                    gci = c % 3
                    rdy0 = st[f"bgr{gci}"]
                    for h in range(16):
                        for qs in range(4):
                            limit = t0 + qs * 128 + 128
                            if k0 >= limit:
                                continue
                            W = min(512, limit - k0)
                            si = inst % 3
                            first = (c == 0)
                            E.wait("tensor", ("BQ", qld))
                            E.wait("tensor", rdy0)
                            E.wait("tensor", st.get(f"bexp{si}"))
                            E.op("tensor", lambda e, h=h, qs=qs, si=si, gci=gci: e.matmul(
                                pS[si][:, :], qn_sb[:, h, qs * 128:qs * 128 + 128],
                                kn_g[gci][:, h, :], start=True, stop=False), inc=None)
                            E.op("tensor", lambda e, h=h, qs=qs, si=si, c=c: e.matmul(
                                pS[si][:, :], qp_sb[:, h, qs * 128:qs * 128 + 128],
                                krope[:, c * 512:c * 512 + 512], start=False, stop=True),
                                inc="TE")
                            smm = E.cnt("TE")
                            diag = (k0 + W == limit)
                            E.wait("vector", ("TE", smm))
                            if diag:
                                moff = W - 128
                                E.op("vector", lambda e, si=si, moff=moff: e.tensor_tensor(
                                    pS[si][:, moff:moff + 128], pS[si][:, moff:moff + 128],
                                    tm_sb[:, :], ALU.add), inc="VE")
                            msk = ("VE", E.cnt("VE"))
                            E.wait("scalar", ("TE", smm))
                            if diag:
                                E.wait("scalar", msk)
                            E.wait("scalar", st.get(f"bcs{si}"))
                            E.wait("scalar", st.get(f"bprb{si}"))
                            ex = E.op("scalar", lambda e, si=si, W=W: e.activation(
                                probs[si][:, 0:W], pS[si][:, 0:W], AF.Exp,
                                bias=mshift[:, 0:1], scale=SCALE,
                                accum_out=cs_sb[:, si:si + 1]), inc="AE")
                            st[f"bexp{si}"] = ("AE", ex)
                            col = h * 4 + qs
                            E.wait("vector", ("AE", ex))
                            if first:
                                lv = E.op("vector", lambda e, col=col, si=si: e.tensor_copy(
                                    l_sb[:, col:col + 1], cs_sb[:, si:si + 1]), inc="VE")
                            else:
                                lv = E.op("vector", lambda e, col=col, si=si: e.tensor_tensor(
                                    l_sb[:, col:col + 1], l_sb[:, col:col + 1],
                                    cs_sb[:, si:si + 1], ALU.add), inc="VE")
                            st[f"bcs{si}"] = ("VE", lv)
                            nsub = (W + 127) // 128
                            rels = [f"bg{gci}"]
                            if c == ngr - 1:
                                rels.append("bq_rel")
                            push_inst((si, W, nsub, c, h, qs, col, first, ex, rels))
                            inst += 1
                flush_insts()
                if not B_FIN:
                    continue
                E.sbar("vector", "VE")
                E.op("vector", lambda e: e.reciprocal(rl_sb[:, :], l_sb[:, :]), inc="VE")
                E.sbar("vector", "VE")
                E.wait("vector", st.get("bat_rel"))
                for h in range(16):
                    for qs in range(4):
                        col = h * 4 + qs
                        E.wait("vector", st.get("bobf"))
                        ob = E.op("vector", lambda e, col=col: e.tensor_scalar_mul(
                            obf[:, :], Oacc[:, col * 128:col * 128 + 128],
                            rl_sb[:, col:col + 1]), inc="VE")
                        E.wait("tensor", ("VE", ob))
                        E.wait("tensor", st.get("bfinc"))
                        E.wait("tensor", st.get("bpp0"))
                        E.op("tensor", lambda e, qs=qs: e.transpose(
                            pP[0][:, qs * 128:qs * 128 + 128], obf[:, :], id_sb[:, :]),
                            inc="TE")
                        tn = E.cnt("TE")
                        st["bobf"] = ("TE", tn)
                        E.wait("vector", ("TE", tn))
                        cpa = E.op("vector", lambda e, h=h, qs=qs: e.tensor_copy(
                            att_st[:, h, qs * 128:qs * 128 + 128],
                            pP[0][:, qs * 128:qs * 128 + 128]), inc="VE")
                        st["bfinc"] = ("VE", cpa)
                        st["bpp0"] = ("VE", cpa)
                fin = ("VE", E.cnt("VE"))
                E.wait("sync", fin)
                atn = E.op("sync", lambda e, t0=t0: e.dma_start(
                    out=r128(s_at_h)[:, :, t0:t0 + 512], in_=att_st[:, :, :]), inc="SAT", k=16)
                st["bat_rel"] = ("SAT", atn)
            B_f = {"sync": ("SAT", E.cnt("SAT")), "tensor": ("TE", E.cnt("TE")),
                   "vector": ("VE", E.cnt("VE")), "scalar": ("AE", E.cnt("AE"))}
        E.barrier(B_f)

        with contextlib.ExitStack() as pst:
            psb = lambda name, shape, dt: pst.enter_context(nc.sbuf_tensor(name, shape, dt))
            pps = lambda name, shape, dt: pst.enter_context(nc.psum_tensor(name, shape, dt))
            arena = [psb(f"arena{i}", [128, 16, 512], BF16) for i in range(3)]
            xc_sb = psb("xc_sbC", [128, 4, D], F32)
            yz_sb = psb("yz_sb", [128, 4, D], F32)
            gT_sb = psb("gT_sb", [128, 32, 512], BF16)
            h2_bf = psb("h2_bf", [128, D], BF16)
            h2T_sb = psb("h2T_sb", [128, 16, 512], BF16)
            sg_sb = [psb(f"sg_sb{i}", [128, 512], BF16) for i in range(2)]
            t_scr = psb("t_scr", [128, D], F32)
            out_sb = psb("out_sb", [128, D], F32)
            sums3 = psb("sums3", [128, 1], F32)
            ss3 = psb("ss3", [128, 1], F32)
            nmu3 = psb("nmu3", [128, 1], F32)
            rst3 = psb("rst3", [128, 1], F32)
            s43 = (sums3[:, 0:1], ss3, nmu3[:, 0:1], rst3[:, 0:1])
            pG = [pps(f"pG{i}", [128, 512], F32) for i in range(2)]
            pU = [pps(f"pU{i}", [128, 512], F32) for i in range(2)]
            pD = [pps(f"pD{i}", [128, 512], F32) for i in range(2)]
            pH = pps("pH", [128, 2048], BF16)

            E.wait("sync", ("CVT", E.cnt("CVT")))
            arena_i = 0

            def arena_load(src_ap, pin=False, ring3=False, q="sync"):
                nonlocal arena_i
                if pin:
                    sl = 2
                else:
                    sl = arena_i % (3 if ring3 else 2)
                    arena_i += 1
                E.wait(q, st.get(f"car{sl}"))
                n = E.op(q, lambda e, sl=sl, src_ap=src_ap: e.dma_start(
                    out=arena[sl][:, :, :], in_=src_ap), inc=f"CA{sl}", k=16)
                return sl, (f"CA{sl}", n)

            for t in range(min(NT, C_MAX_T)):
                t0 = t * 512
                ti = t % 2
                arena_i = 0
                sl_a, af_a = arena_load(r128(s_at_h)[:, :, t0:t0 + 512], pin=True)
                E.wait("sync", st.get("cx_rel"))
                for qs in range(4):
                    xn = E.op("sync", lambda e, t0=t0, qs=qs: e.dma_start(
                        out=xc_sb[:, qs, :], in_=x_h[t0 + qs * 128:t0 + qs * 128 + 128, :]),
                        inc=f"CX{ti}", k=16)
                cx_rdy = (f"CX{ti}", xn)
                for fo in range(4):
                    sl, af = arena_load(r128(wo_b_h)[:, :, fo * 512:fo * 512 + 512])
                    E.wait("tensor", af)
                    E.wait("tensor", af_a)
                    for qs in range(4):
                        pi = (fo * 4 + qs) % 2
                        E.wait("tensor", st.get(f"cpg{pi}"))
                        for kc in range(16):
                            E.op("tensor", lambda e, kc=kc, qs=qs, sl=sl, pi=pi, sl_a=sl_a: e.matmul(
                                pG[pi][:, :], arena[sl_a][:, kc, qs * 128:qs * 128 + 128],
                                arena[sl][:, kc, :], start=(kc == 0), stop=(kc == 15)),
                                inc=("TE" if kc == 15 else None))
                        mmn = E.cnt("TE")
                        if qs == 3:
                            st[f"car{sl}"] = ("TE", mmn)
                            if fo == 3:
                                st[f"car{sl_a}"] = ("TE", mmn)
                        E.wait("vector", ("TE", mmn))
                        E.wait("vector", st.get("cyz_w"))
                        ev = E.op("vector", lambda e, qs=qs, fo=fo, pi=pi: e.tensor_copy(
                            yz_sb[:, qs, fo * 512:fo * 512 + 512], pG[pi][:, :]), inc="VE")
                        st[f"cpg{pi}"] = ("VE", ev)
                yrdy = ("VE", E.cnt("VE"))
                if dbg:
                    E.wait("sync", yrdy)
                    E.op("sync", lambda e, t0=t0: e.dma_start(
                        out=d_wo_h.rearrange("(a p) d -> p a d", p=128)[:, t0 // 128:t0 // 128 + 4, :],
                        in_=yz_sb[:, :, :]), inc="DBG", k=16)
                for qs in range(4):
                    E.wait("vector", st.get("ct_rel"))
                    apf = ln_chain(yz_sb[:, qs, :], t_scr[:, :], D, epsln, s43, "lnC",
                                   src_f=(yrdy if qs == 0 else None))
                    E.wait("gpsimd", apf)
                    E.wait("gpsimd", cx_rdy)
                    gadd = E.op("gpsimd", lambda e, qs=qs: e.tensor_tensor(
                        xc_sb[:, qs, :], xc_sb[:, qs, :], t_scr[:, :], ALU.add), inc="GE")
                    st["ct_rel"] = ("GE", gadd)
                    E.wait("vector", ("GE", gadd))
                    E.wait("scalar", ("GE", gadd))
                    E.wait("vector", st.get("ch2"))
                    apf2 = ln_chain(xc_sb[:, qs, :], h2_bf[:, :],
                                    D, epsln, s43, "lnC", src_f=None)
                    E.wait("tensor", apf2)
                    E.wait("tensor", st.get("cpH"))
                    for cc in range(16):
                        E.op("tensor", lambda e, cc=cc: e.transpose(
                            pH[:, cc * 128:(cc + 1) * 128], h2_bf[:, cc * 128:(cc + 1) * 128],
                            id_sb[:, :]), inc=("TE" if cc == 15 else None))
                    tn = E.cnt("TE")
                    st["ch2"] = ("TE", tn)
                    E.wait("vector", ("TE", tn))
                    E.wait("vector", st.get("ch2T"))
                    cph = E.op("vector", lambda e, qs=qs: e.tensor_copy(
                        h2T_sb[:, :, qs * 128:qs * 128 + 128],
                        pH[:, :]), inc="VE")
                    st["cpH"] = ("VE", cph)
                st["cyz_w2"] = ("VE", E.cnt("VE"))
                if dbg:
                    E.wait("sync", st.get("ct_rel"))
                    E.op("sync", lambda e, t0=t0: e.dma_start(
                        out=d_x2_h.rearrange("(a p) d -> p a d", p=128)[:, t0 // 128:t0 // 128 + 4, :],
                        in_=xc_sb[:, :, :]), inc="DBG", k=16)
                h2T_rdy = ("VE", E.cnt("VE"))
                E.wait("tensor", h2T_rdy)
                for half in range(2):
                    hb = half * 32
                    for hcg in range(8):
                        slg, afg = arena_load(
                            r128(wg_b_h)[:, :, (hb + hcg * 4) * 128:(hb + hcg * 4) * 128 + 512],
                            ring3=True)
                        slu, afu = arena_load(
                            r128(wu_b_h)[:, :, (hb + hcg * 4) * 128:(hb + hcg * 4) * 128 + 512],
                            ring3=True)
                        for hl in range(4):
                            hc = hcg * 4 + hl
                            pi = hc % 2
                            E.wait("tensor", afg)
                            E.wait("tensor", st.get(f"cpg{pi}"))
                            for kc in range(16):
                                E.op("tensor", lambda e, kc=kc, hl=hl, slg=slg, pi=pi: e.matmul(
                                    pG[pi][:, :], arena[slg][:, kc, hl * 128:hl * 128 + 128],
                                    h2T_sb[:, kc, :], start=(kc == 0), stop=(kc == 15)),
                                    inc=("TE" if kc == 15 else None))
                            gmm = E.cnt("TE")
                            E.wait("tensor", afu)
                            E.wait("tensor", st.get(f"cpu{pi}"))
                            for kc in range(16):
                                E.op("tensor", lambda e, kc=kc, hl=hl, slu=slu, pi=pi: e.matmul(
                                    pU[pi][:, :], arena[slu][:, kc, hl * 128:hl * 128 + 128],
                                    h2T_sb[:, kc, :], start=(kc == 0), stop=(kc == 15)),
                                    inc=("TE" if kc == 15 else None))
                            umm = E.cnt("TE")
                            if hl == 3:
                                st[f"car{slg}"] = ("TE", gmm)
                                st[f"car{slu}"] = ("TE", umm)
                            E.wait("scalar", ("TE", gmm))
                            E.wait("scalar", st.get(f"csg{pi}"))
                            sg_n = E.op("scalar", lambda e, pi=pi: e.activation(
                                sg_sb[pi][:, :], pG[pi][:, :], AF.Silu), inc="AE")
                            st[f"cpg{pi}"] = ("AE", sg_n)
                            E.wait("vector", ("AE", sg_n))
                            E.wait("vector", ("TE", umm))
                            E.wait("vector", st.get("cgT_rel"))
                            gv = E.op("vector", lambda e, hc=hc, pi=pi: e.tensor_tensor(
                                gT_sb[:, hc % 32, :], sg_sb[pi][:, :], pU[pi][:, :], ALU.mult),
                                inc="VE")
                            st[f"csg{pi}"] = ("VE", gv)
                            st[f"cpu{pi}"] = ("VE", gv)
                    gT_rdy = ("VE", E.cnt("VE"))
                    E.wait("tensor", gT_rdy)
                    for fo in range(4):
                        wdsl = []
                        for j in range(2):
                            slw, afw = arena_load(
                                r128(wd_b_h)[:, (hb + j * 16):(hb + j * 16) + 16,
                                             fo * 512:fo * 512 + 512],
                                ring3=True, q="scalar")
                            wdsl.append((slw, afw))
                        for qs in range(4):
                            pi = qs % 2
                            E.wait("tensor", st.get(f"cpd{pi}"))
                            for j in range(2):
                                slw, afw = wdsl[j]
                                E.wait("tensor", afw)
                                for hl in range(16):
                                    hc = j * 16 + hl
                                    E.op("tensor", lambda e, hc=hc, hl=hl, qs=qs, slw=slw, pi=pi, j=j: e.matmul(
                                        pD[pi][:, :], gT_sb[:, hc, qs * 128:qs * 128 + 128],
                                        arena[slw][:, hl, :],
                                        start=(j == 0 and hl == 0), stop=(j == 1 and hl == 15)),
                                        inc=("TE" if (j == 1 and hl == 15) else None))
                            dmm = E.cnt("TE")
                            if qs == 3:
                                for (slw, afw) in wdsl:
                                    st[f"car{slw}"] = ("TE", dmm)
                                if fo == 3:
                                    st["cgT_rel"] = ("TE", dmm)
                            E.wait("vector", ("TE", dmm))
                            E.wait("vector", st.get("cyz_w2"))
                            if half == 0:
                                ev = E.op("vector", lambda e, qs=qs, fo=fo, pi=pi: e.tensor_copy(
                                    yz_sb[:, qs, fo * 512:fo * 512 + 512], pD[pi][:, :]),
                                    inc="VE")
                            else:
                                ev = E.op("vector", lambda e, qs=qs, fo=fo, pi=pi: e.tensor_tensor(
                                    yz_sb[:, qs, fo * 512:fo * 512 + 512],
                                    yz_sb[:, qs, fo * 512:fo * 512 + 512], pD[pi][:, :],
                                    ALU.add), inc="VE")
                            st[f"cpd{pi}"] = ("VE", ev)
                zrdy = ("VE", E.cnt("VE"))
                if dbg:
                    E.wait("sync", zrdy)
                    E.op("sync", lambda e, t0=t0: e.dma_start(
                        out=d_z_h.rearrange("(a p) d -> p a d", p=128)[:, t0 // 128:t0 // 128 + 4, :],
                        in_=yz_sb[:, :, :]), inc="DBG", k=16)
                for qs in range(4):
                    E.wait("vector", st.get("ct_rel"))
                    apf = ln_chain(yz_sb[:, qs, :], t_scr[:, :], D, epsln, s43, "lnC",
                                   src_f=(zrdy if qs == 0 else None))
                    E.wait("gpsimd", apf)
                    E.wait("gpsimd", st.get("cout"))
                    E.sbar("gpsimd", "GE")
                    gadd = E.op("gpsimd", lambda e, qs=qs: e.tensor_tensor(
                        out_sb[:, :], xc_sb[:, qs, :], t_scr[:, :], ALU.add), inc="GE")
                    st["ct_rel"] = ("GE", gadd)
                    if qs == 3:
                        st["cx_rel"] = ("GE", gadd)
                    E.wait("sync", ("GE", gadd))
                    yo = E.op("sync", lambda e, t0=t0, qs=qs: e.dma_start(
                        out=y_h[t0 + qs * 128:t0 + qs * 128 + 128, :], in_=out_sb[:, :]),
                        inc="YO", k=16)
                    st["cout"] = ("YO", yo)
                st["cyz_w"] = ("VE", E.cnt("VE"))
            E.wait("sync", ("YO", E.cnt("YO")))
            if dbg:
                E.wait("sync", ("DBG", E.cnt("DBG")))

        with nc.Block() as block:
            @block.sync
            def _(e):
                for f in E.ops["sync"]:
                    f(e)

            @block.tensor
            def _(e):
                for f in E.ops["tensor"]:
                    f(e)

            @block.vector
            def _(e):
                for f in E.ops["vector"]:
                    f(e)

            @block.scalar
            def _(e):
                for f in E.ops["scalar"]:
                    f(e)

            @block.gpsimd
            def _(e):
                for f in E.ops["gpsimd"]:
                    f(e)

    return nc


def _prep_inputs(inputs, S):
    T = lambda a: np.ascontiguousarray(np.asarray(a).T.astype(np.float32))
    wqr = T(inputs["L0_attn_wq_rope"])
    tq = np.empty(NH * DR, dtype=np.int64)
    for d in range(NH * DR):
        H, r = d // DR, d % DR
        tq[d] = (512 if r % 2 else 0) + H * 32 + r // 2
    wqr_p = np.empty_like(wqr)
    wqr_p[:, tq] = wqr
    wkr = T(inputs["L0_attn_wk_rope"])
    tk = np.empty(DR, dtype=np.int64)
    for r in range(DR):
        tk[r] = (32 if r % 2 else 0) + r // 2
    wkr_p = np.empty_like(wkr)
    wkr_p[:, tk] = wkr
    ident = np.eye(128, dtype=np.float32)
    ii = np.arange(128)
    trimask = np.where(ii[None, :] <= ii[:, None], 0.0, NEGBIG).astype(np.float32)
    inv = 10000.0 ** (-(np.arange(32, dtype=np.float64) * 2.0 / DR))
    fr = np.outer(inv, np.arange(S, dtype=np.float64))
    rcos = np.tile(np.cos(fr), (4, 1)).astype(np.float32)
    rsin = np.tile(np.sin(fr), (4, 1)).astype(np.float32)
    return dict(
        wqd=T(inputs["L0_attn_wq_down"]), wqu=T(inputs["L0_attn_wq_up"]), wqr=wqr_p,
        qnw=np.ascontiguousarray(np.asarray(inputs["L0_attn_q_norm_w"], dtype=np.float32).reshape(QR // 128, 128).T),
        wkd=T(inputs["L0_attn_wkv_down"]),
        kvw=np.ascontiguousarray(np.asarray(inputs["L0_attn_kv_norm_w"], dtype=np.float32).reshape(KVR // 128, 128).T),
        wku=T(inputs["L0_attn_wkv_up"]), wkr=wkr_p, wo=T(inputs["L0_attn_wo"]),
        wg=T(inputs["L0_ffn_w_gate"]), wu=T(inputs["L0_ffn_w_up"]),
        wd=T(inputs["L0_ffn_w_down"]), ident=ident, trimask=trimask,
        rcos=rcos, rsin=rsin)


def kernel(**inputs):
    S = 4096
    x = np.asarray(inputs["L0_attn_x"])
    B = x.shape[0]
    common = _prep_inputs(inputs, S)
    in_maps = [dict(common, x=np.ascontiguousarray(x[c], dtype=np.float32)) for c in range(B)]
    nc = build_program(S)
    res = run_bass_kernel_spmd(nc, in_maps, core_ids=list(range(B)))
    out = np.stack([np.asarray(res.results[c]["y"]) for c in range(B)], axis=0)
    return out.astype(np.float32)
